# version 40 (final)
# baseline (speedup 1.0000x reference)
import numpy as np
import concourse.bass as bass
import concourse.mybir as mybir
from concourse.bass_utils import run_bass_kernel_spmd
from concourse.ap import AP
from contextlib import ExitStack

F32 = mybir.dt.float32
BF16 = mybir.dt.bfloat16
AF = mybir.ActivationFunctionType
ALU = mybir.AluOpType
AX = mybir.AxisListType

DM = 1024
S = 4096
HD = 64
FF = 4096
TOK = 2048
NQT = 4
QT = 512
NKC = 32
L = 6144
WPC = 2432
EPS_LN = 1e-5
EPS_RMS = 1e-8
SHIFT = 20.0

PAIRS = [(4 * (j // 2) + (j % 2), 4 * (j // 2) + (j % 2) + 2) for j in range(8)]
PERM_ROWS = np.concatenate(
    [np.arange(h * HD, (h + 1) * HD) for j in range(8) for h in PAIRS[j]])

ENGS = ("sync", "tensor", "scalar", "vector", "gpsimd")


class Prog:
    def __init__(self):
        self.ops = {e: [] for e in ENGS}
        self.cnt = {}
        self.step = {}
        self.marks = {e: {} for e in ENGS}

    def sem(self, name, step=1):
        self.cnt[name] = 0
        self.step[name] = step

    def op(self, eng, fn, deps=(), inc=None):
        deps = list(deps)
        if inc is not None and self.step[inc] == 16:
            if self.cnt[inc] > 0:
                deps.append((inc, self.cnt[inc]))
        waits = []
        for d in deps:
            if d is None:
                continue
            s, v = d
            if v <= 0 or self.marks[eng].get(s, 0) >= v:
                continue
            self.marks[eng][s] = v
            waits.append((s, v))
        tok = None
        if inc is not None:
            self.cnt[inc] += self.step[inc]
            tok = (inc, self.cnt[inc])
        self.ops[eng].append((waits, fn, inc))
        return tok

    def barrier(self):
        toks = [(s, self.cnt[s]) for s in self.cnt]
        for e in ENGS:
            self.op(e, lambda eng: None, deps=toks)

    def replay(self, name, eng, sems):
        for waits, fn, inc in self.ops[name]:
            for (s, v) in waits:
                eng.wait_ge(sems[s], v)
            ins = fn(eng)
            if inc is not None:
                if ins is None:
                    eng.sem_inc(sems[inc], self.step[inc])
                else:
                    ins.then_inc(sems[inc], self.step[inc])


class Res:

    def __init__(self):
        self.w = None
        self.r = []

    def wd(self):
        return ([self.w] if self.w else []) + list(self.r)

    def rd(self):
        return [self.w] if self.w else []

    def wrote(self, tok):
        self.w = tok
        self.r = []

    def read(self, tok):
        if tok is None:
            return
        best = {}
        for s, v in self.r + [tok]:
            best[s] = max(best.get(s, 0), v)
        self.r = list(best.items())


def build_nc():
    nc = bass.Bass("TRN2", target_bir_lowering=False)

    xT = nc.dram_tensor("xT", [DM, S], F32, kind="ExternalInput").ap()
    xqT = nc.dram_tensor("xqT", [DM, TOK], F32, kind="ExternalInput").ap()
    wqT = nc.dram_tensor("wqT", [8, 8, 128, 128], F32, kind="ExternalInput").ap()
    wkT = nc.dram_tensor("wkT", [DM, 512], F32, kind="ExternalInput").ap()
    wvT = nc.dram_tensor("wvT", [DM, 512], F32, kind="ExternalInput").ap()
    woT = nc.dram_tensor("woT", [8, 8, 128, 128], F32, kind="ExternalInput").ap()
    bo = nc.dram_tensor("bo", [1, DM], F32, kind="ExternalInput").ap()
    wup = nc.dram_tensor("wup", [32, 8, 128, 128], F32, kind="ExternalInput").ap()
    wdn = nc.dram_tensor("wdn", [8, 32, 128, 128], F32, kind="ExternalInput").ap()
    dist = nc.dram_tensor("dist", [16, L], F32, kind="ExternalInput").ap()
    slop = nc.dram_tensor("slop", [16, 1], F32, kind="ExternalInput").ap()
    out = nc.dram_tensor("out", [DM, TOK], F32, kind="ExternalOutput").ap()

    erow_h = nc.dram_tensor("erow_h", [16, L], BF16, kind="Internal").ap()
    erep = nc.dram_tensor("erep", [16, 128, L], BF16, kind="Internal").ap()
    q_hbm = nc.dram_tensor("q_hbm", [NQT, 8, 128, QT], BF16, kind="Internal").ap()
    wo2 = nc.dram_tensor("wo2", [8, 8, 128, 128], BF16, kind="Internal").ap()
    wtu2 = nc.dram_tensor("wtu2", [32, 8, 128, 128], BF16, kind="Internal").ap()
    wtd2 = nc.dram_tensor("wtd2", [8, 32, 128, 128], BF16, kind="Internal").ap()
    scal_h = nc.dram_tensor("scal_h", [1, 8], F32, kind="Internal").ap()

    P = Prog()
    DMASEMS = ("xq0", "xq1", "xf0", "xf1", "xf2", "xf3", "rep", "scal", "wqs0", "wqs1",
               "wk", "wv", "qs0", "qs1", "wts0", "wts1", "w2s", "tl0", "tl1",
               "ew0", "ew1", "ql0", "ql1", "xo", "wo0", "wo1", "wu0", "wu1", "wu2", "wu3",
               "wd0", "wd1", "ot0", "ot1", "t2s", "rep2")
    for s in ("pe", "act", "dve", "pl"):
        P.sem(s)
    for s in DMASEMS:
        P.sem(s, 16)

    ctx = ExitStack()
    sbt = lambda n, sh, dt: ctx.enter_context(nc.sbuf_tensor(n, sh, dt))
    pst = lambda n, sh: ctx.enter_context(nc.psum_tensor(n, sh, F32))

    k2_sb = sbt("k2", [128, 4 * S], BF16)
    v_sb = sbt("v", [128, NKC * 512], BF16)
    vec = sbt("vec", [128, 4 * QT], F32)
    ones64 = sbt("ones64", [128, 64], BF16)
    ones1f = sbt("ones1f", [1, 128], F32)
    onesb = sbt("onesb", [128, 1], BF16)
    ones1c = sbt("ones1c", [128, 1], F32)
    bo_sb = sbt("bosb", [128, 8], F32)
    bias_sb = sbt("biasc", [128, 1], F32)
    tauu_sb = sbt("tauu", [128, 2], F32)
    taud_sb = sbt("taud", [128, 2], F32)
    slop_sb = sbt("slopsb", [16, 1], F32)

    psS = [pst("psS0", [128, 1024]), pst("psS1", [128, 1024])]
    psO = pst("psO", [128, QT])
    psR = pst("psR", [128, QT])
    psW = [pst("psW0", [128, QT]), pst("psW1", [128, QT])]

    R = {}
    for n in ("psS0", "psS1", "psO", "psR", "psW0", "psW1", "eo0", "eo1",
              "pt0", "pt1", "pt2", "ew0", "ew1", "q0", "q1", "osb", "xo0", "xo1",
              "x1", "xn", "h2", "rhb", "r2", "ot0", "ot1", "wo0", "wo1",
              "wtu0", "wtu1", "wtd0", "wtd1", "rr", "bias", "vs0", "vs1",
              "vs2", "vs3", "wm1", "wm2", "wt2t", "wts0", "wts1", "acc",
              "tauu", "taud", "nrm0", "nrm1", "nrm2", "h2p", "accr",
              "tu2", "tu3", "td2", "td3", "tw2", "tw3", "ts2", "ts3",
              "ts4", "ts5"):
        R[n] = Res()
    for fc in range(32):
        R[f"h{fc}"] = Res()
    hbm = {}

    vrow = lambda k: vec[0:1, k * QT:(k + 1) * QT]

    slop_t = P.op("sync", lambda e: e.dma_start(out=slop_sb[:, :], in_=slop[:, :]),
                  inc="scal")
    bo_t = P.op("sync", lambda e: e.dma_start(
        out=bo_sb[:, :], in_=AP(bo.tensor, 0, [[1, 128], [128, 8]]),
        allow_slow_non_contiguous=True), inc="scal")
    R["bias"].wrote(bo_t)
    c1 = P.op("gpsimd", lambda e: e.memset(ones64[:, :], 1.0), inc="pl")
    c2 = P.op("gpsimd", lambda e: e.memset(ones1f[:, :], 1.0), inc="pl")
    c3 = P.op("gpsimd", lambda e: e.memset(onesb[:, :], 1.0), inc="pl")
    c4 = P.op("gpsimd", lambda e: e.memset(ones1c[:, :], 1.0), inc="pl")
    ones_t = [c1, c2, c3, c4]

    ctxQ = ExitStack()
    dst = ctxQ.enter_context(nc.sbuf_tensor("dst", [16, L], F32))
    ero = ctxQ.enter_context(nc.sbuf_tensor("ero", [16, L], BF16))
    xsq = ctxQ.enter_context(nc.sbuf_tensor("xsq", [128, 2, 512], F32))
    xsf = ctxQ.enter_context(nc.sbuf_tensor("xsf", [128, 4, 512], F32))
    xbq = ctxQ.enter_context(nc.sbuf_tensor("xbq", [128, 2, 8, 512], BF16))
    xbf = ctxQ.enter_context(nc.sbuf_tensor("xbf", [128, 2, 8, 512], BF16))
    wqs = ctxQ.enter_context(nc.sbuf_tensor("wqs", [128, 2, 1024], F32))
    wqb = ctxQ.enter_context(nc.sbuf_tensor("wqb", [128, 8, 8, 128], BF16))
    qev = ctxQ.enter_context(nc.sbuf_tensor("qev", [128, 2, QT], BF16))
    qsq = ctxQ.enter_context(nc.sbuf_tensor("qsq", [128, QT], BF16))
    wks = ctxQ.enter_context(nc.sbuf_tensor("wks", [128, 4, 512], F32))
    wkb = ctxQ.enter_context(nc.sbuf_tensor("wkb", [128, 8, 512], BF16))
    wvb = ctxQ.enter_context(nc.sbuf_tensor("wvb", [128, 8, 512], BF16))
    ksq = ctxQ.enter_context(nc.sbuf_tensor("ksq", [128, QT], BF16))

    dst_t = P.op("sync", lambda e: e.dma_start(out=dst[:, :], in_=dist[:, :]),
                 inc="rep")

    wqb_res = Res()
    wq_ld = []
    for j in range(2):
        wq_ld.append(P.op("scalar", lambda e, j=j: e.dma_start(
            out=wqs[:, j, :].rearrange("p (a m) -> p a m", a=8),
            in_=wqT[j, :, :, :].transpose([1, 0, 2])), inc=f"wqs{j}"))

    ero_t = P.op("scalar", lambda e: e.activation(
        out=ero[:, :], in_=dst[:, :], func=AF.Exp, bias=0.0,
        scale=slop_sb[:, 0:1]), deps=[dst_t, slop_t], inc="act")
    erow_t = P.op("scalar", lambda e: e.dma_start(out=erow_h[:, :], in_=ero[:, :]),
                  deps=[ero_t], inc="rep")
    for h in [hh for j in range(8) for hh in PAIRS[j]]:
        hbm[f"erep{h}"] = P.op("gpsimd", lambda e, h=h: e.dma_start(
            out=erep[h, :, :], in_=AP(erow_h.tensor, h * L, [[0, 128], [1, L]])),
            deps=[erow_t], inc="rep2")

    xsq_res = [Res(), Res()]
    xsf_res = [Res(), Res(), Res(), Res()]
    xbq_res = [Res(), Res()]
    xbf_res = [Res(), Res()]
    xq_cv = {}
    xf_cv = {}

    def load_xq_slab(qt):
        xq_cv[qt] = []
        for c in range(8):
            sb = c % 2
            ld = P.op("sync", lambda e, c=c, qt=qt, sb=sb: e.dma_start(
                out=xsq[:, sb, :],
                in_=xqT[c * 128:(c + 1) * 128, qt * 512:(qt + 1) * 512]),
                deps=xsq_res[sb].wd(), inc=f"xq{sb}")
            xsq_res[sb].wrote(ld)
            cv = P.op("vector", lambda e, c=c, qt=qt, sb=sb: e.tensor_copy(
                xbq[:, qt % 2, c, :], xsq[:, sb, :]),
                deps=[ld] + xbq_res[qt % 2].wd(), inc="dve")
            xsq_res[sb].read(cv)
            xbq_res[qt % 2].wrote(cv)
            xq_cv[qt].append(cv)

    def load_xf_slab(tb):
        xf_cv[tb] = []
        for c in range(8):
            sb = c % 4
            ld = P.op("sync", lambda e, c=c, tb=tb, sb=sb: e.dma_start(
                out=xsf[:, sb, :],
                in_=xT[c * 128:(c + 1) * 128, tb * 512:(tb + 1) * 512]),
                deps=xsf_res[sb].wd(), inc=f"xf{sb}")
            xsf_res[sb].wrote(ld)
            cv = P.op("vector", lambda e, c=c, tb=tb, sb=sb: e.tensor_copy(
                xbf[:, tb % 2, c, :], xsf[:, sb, :]),
                deps=[ld] + xbf_res[tb % 2].wd(), inc="dve")
            xsf_res[sb].read(cv)
            xbf_res[tb % 2].wrote(cv)
            xf_cv[tb].append(cv)

    load_xq_slab(0)
    load_xf_slab(0)

    wk_cv, wv_cv = [], []
    for c in range(8):
        l1 = P.op("scalar", lambda e, c=c: e.dma_start(
            out=wks[:, c % 2, :], in_=wkT[c * 128:(c + 1) * 128, :]),
            deps=[wk_cv[c - 2]] if c >= 2 else [], inc="wk")
        k1 = P.op("vector", lambda e, c=c: e.tensor_copy(wkb[:, c, :], wks[:, c % 2, :]),
                  deps=[l1], inc="dve")
        wk_cv.append(k1)
        l2 = P.op("scalar", lambda e, c=c: e.dma_start(
            out=wks[:, 2 + c % 2, :], in_=wvT[c * 128:(c + 1) * 128, :]),
            deps=[wv_cv[c - 2]] if c >= 2 else [], inc="wv")
        k2t_ = P.op("vector", lambda e, c=c: e.tensor_copy(wvb[:, c, :], wks[:, 2 + c % 2, :]),
                    deps=[l2], inc="dve")
        wv_cv.append(k2t_)

    wq_cv = []
    for j in range(8):
        if j >= 2:
            wq_ld.append(P.op("scalar", lambda e, j=j: e.dma_start(
                out=wqs[:, j % 2, :].rearrange("p (a m) -> p a m", a=8),
                in_=wqT[j, :, :, :].transpose([1, 0, 2])),
                deps=[wq_cv[j - 2]], inc=f"wqs{j % 2}"))
        cv = P.op("vector", lambda e, j=j: e.tensor_copy(
            wqb[:, j, :, :].rearrange("p a m -> p (a m)"), wqs[:, j % 2, :]),
            deps=[wq_ld[j]], inc="dve")
        wq_cv.append(cv)

    load_xq_slab(1)
    load_xf_slab(1)

    qev_res = [Res(), Res()]
    qsq_res = Res()
    ksq_res = Res()
    psW_res = [R["psW0"], R["psW1"]]
    k2_ev = {}
    v_ev = []
    gq = [0]
    nrm_i = [0]
    qn_i = [0]
    kn_i = [0]

    def emit_q(qt):
        for j in range(8):
            pw = gq[0] % 2
            gq[0] += 1
            deps = [wq_cv[j]] + xq_cv[qt] + psW_res[pw].wd()

            def mk_q(e, j=j, qt=qt, pw=pw):
                last = None
                for k in range(8):
                    last = e.matmul(psW[pw][:, :], wqb[:, j, k, :],
                                    xbq[:, qt % 2, k, :],
                                    start=(k == 0), stop=(k == 7),
                                    skip_group_check=True)
                return last
            mm = P.op("tensor", mk_q, deps=deps, inc="pe")
            psW_res[pw].wrote(mm)
            xbq_res[qt % 2].read(mm)
            ev = P.op("vector", lambda e, pw=pw: e.tensor_scalar(
                qev[:, pw, :], psW[pw][:, :], 0.125, None, ALU.mult),
                deps=[mm] + qev_res[pw].wd(), inc="dve")
            psW_res[pw].read(ev)
            qev_res[pw].wrote(ev)
            st = P.op("scalar", lambda e, j=j, qt=qt, pw=pw: e.dma_start(
                out=q_hbm[qt, j, :, :], in_=qev[:, pw, :]), deps=[ev],
                inc=f"qs{pw}")
            hbm[f"q{qt}_{j}"] = st
            qev_res[pw].read(st)
            sq = P.op("scalar", lambda e, pw=pw: e.activation(
                out=qsq[:, :], in_=qev[:, pw, :], func=AF.Square),
                deps=[ev] + qsq_res.wd(), inc="act")
            qev_res[pw].read(sq)
            qsq_res.wrote(sq)
            row = 32 * (nrm_i[0] % 3)
            rown = f"nrm{nrm_i[0] % 3}"
            nrm_i[0] += 1
            qn = qn_i[0]
            qn_i[0] += 1
            nm = P.op("tensor", lambda e, row=row: e.matmul(
                psR[row:row + 1, :], onesb[:, :], qsq[:, :], start=True,
                stop=True, skip_group_check=True),
                deps=[sq, c3] + R[rown].wd(), inc="pe")
            qsq_res.read(nm)
            R[rown].wrote(nm)
            rd = P.op("vector", lambda e, row=row, qn=qn: e.reduce_max(
                vec[0:1, 16 + qn:17 + qn], psR[row:row + 1, :], axis=AX.X),
                deps=[nm] + (R["vs0"].wd() if qn == 0 else []), inc="dve")
            R[rown].read(rd)
            R["vs0"].w = rd

    def emit_knorm():
        kn = kn_i[0]
        if kn >= 32:
            return
        tt, tb = kn % 4, kn // 4
        if (tt, tb) not in k2_ev:
            return
        kn_i[0] += 1
        sq = P.op("scalar", lambda e, tt=tt, tb=tb: e.activation(
            out=ksq[:, :], in_=k2_sb[:, tt * S + tb * 512:tt * S + tb * 512 + 512],
            func=AF.Square), deps=[k2_ev[(tt, tb)]] + ksq_res.wd(), inc="act")
        ksq_res.wrote(sq)
        row = 32 * (nrm_i[0] % 3)
        rown = f"nrm{nrm_i[0] % 3}"
        nrm_i[0] += 1
        nm = P.op("tensor", lambda e, row=row: e.matmul(
            psR[row:row + 1, :], onesb[:, :], ksq[:, :], start=True,
            stop=True, skip_group_check=True),
            deps=[sq, c3] + R[rown].wd(), inc="pe")
        ksq_res.read(nm)
        R[rown].wrote(nm)
        rd = P.op("vector", lambda e, row=row, kn=kn: e.reduce_max(
            vec[0:1, 48 + kn:49 + kn], psR[row:row + 1, :], axis=AX.X),
            deps=[nm], inc="dve")
        R[rown].read(rd)
        R["vs0"].w = rd

    def emit_kv(tb):
        for tt in range(4):
            pw = gq[0] % 2
            gq[0] += 1
            deps = wk_cv + xf_cv[tb] + psW_res[pw].wd()

            def mk_k(e, tt=tt, tb=tb, pw=pw):
                last = None
                for k in range(8):
                    last = e.matmul(psW[pw][:, :],
                                    wkb[:, k, tt * 128:tt * 128 + 128],
                                    xbf[:, tb % 2, k, :],
                                    start=(k == 0), stop=(k == 7),
                                    skip_group_check=True)
                return last
            mm = P.op("tensor", mk_k, deps=deps, inc="pe")
            psW_res[pw].wrote(mm)
            xbf_res[tb % 2].read(mm)
            ev = P.op("vector", lambda e, tt=tt, tb=tb, pw=pw: e.tensor_copy(
                k2_sb[:, tt * S + tb * 512:tt * S + (tb + 1) * 512],
                psW[pw][:, :]), deps=[mm], inc="dve")
            psW_res[pw].read(ev)
            k2_ev[(tt, tb)] = ev
        for kcl in range(4):
            kc = tb * 4 + kcl
            pw = gq[0] % 2
            gq[0] += 1
            deps = wv_cv + xf_cv[tb] + psW_res[pw].wd()

            def mk_v(e, kcl=kcl, tb=tb, pw=pw):
                last = None
                for k in range(8):
                    last = e.matmul(psW[pw][:, :],
                                    xbf[:, tb % 2, k, kcl * 128:(kcl + 1) * 128],
                                    wvb[:, k, :], start=(k == 0), stop=(k == 7),
                                    skip_group_check=True)
                return last
            mm = P.op("tensor", mk_v, deps=deps, inc="pe")
            psW_res[pw].wrote(mm)
            xbf_res[tb % 2].read(mm)
            ev = P.op("vector", lambda e, kc=kc, pw=pw: e.tensor_copy(
                v_sb[:, kc * 512:(kc + 1) * 512], psW[pw][:, :]), deps=[mm],
                inc="dve")
            psW_res[pw].read(ev)
            v_ev.append(ev)
            emit_knorm()

    for stp in range(8):
        emit_kv(stp)
        if stp < 4:
            emit_q(stp)
            if stp + 2 < 4:
                load_xq_slab(stp + 2)
        if stp + 2 < 8:
            load_xf_slab(stp + 2)
    while kn_i[0] < 32:
        emit_knorm()
    k2_t = k2_ev[(3, 7)]
    v_t = v_ev[-1]

    ctxQ.close()
    P.barrier()

    nrm = R["vs0"].w
    r1_ = P.op("vector", lambda e: e.reduce_max(
        vec[0:1, 0:1], vec[0:1, 16:48], axis=AX.X), deps=[nrm], inc="dve")
    r2_ = P.op("vector", lambda e: e.reduce_max(
        vec[0:1, 5:6], vec[0:1, 48:80], axis=AX.X), deps=[nrm], inc="dve")
    R["vs0"].w = r2_
    m1_ = P.op("vector", lambda e: e.tensor_tensor(
        vec[0:1, 1:2], vec[0:1, 0:1], vec[0:1, 5:6], ALU.mult),
        deps=[r1_, r2_], inc="dve")
    l1_ = P.op("scalar", lambda e: e.activation(
        out=vec[0:1, 2:3], in_=vec[0:1, 1:2], func=AF.Ln), deps=[m1_], inc="act")
    e1_ = P.op("scalar", lambda e: e.activation(
        out=vec[0:1, 3:4], in_=vec[0:1, 2:3], func=AF.Exp, scale=0.5),
        deps=[l1_], inc="act")
    n1_ = P.op("vector", lambda e: e.tensor_scalar(
        vec[0:1, 4:5], vec[0:1, 3:4], -1.0, SHIFT, ALU.mult, ALU.add),
        deps=[e1_], inc="dve")
    s1_ = P.op("sync", lambda e: e.dma_start(
        out=AP(scal_h.tensor, 4, [[1, 1], [1, 1]]), in_=vec[0:1, 4:5]),
        deps=[n1_], inc="scal")
    bias_t = P.op("sync", lambda e: e.dma_start(
        out=bias_sb[:, :], in_=AP(scal_h.tensor, 4, [[0, 128], [1, 1]])),
        deps=[s1_], inc="scal")
    R["bias"].wrote(bias_t)
    for _tk in (m1_, l1_, e1_, n1_, s1_):
        R["vs0"].read(_tk)

    ewin = sbt("ewin", [128, 2, 2, WPC], BF16)
    eo_sb = sbt("eo", [128, 2, 1024], BF16)
    pt_sb = sbt("pt", [128, 3, 1024], BF16)
    rr_sb = sbt("rr", [128, QT], F32)
    o_sb = sbt("osb", [128, 8, QT], BF16)
    q_sb = sbt("qsb", [128, 2, QT], BF16)
    xo_sb = sbt("xo", [128, QT], F32)
    x1_sb = sbt("x1", [128, 8, QT], F32)
    xn_sb = sbt("xn", [128, 8, QT], BF16)
    h_sb = sbt("h", [128, 32, QT], BF16)
    h2_sb = sbt("h2", [128, QT], BF16)
    rhb_sb = sbt("rhb", [128, QT], F32)
    r2t_sb = sbt("r2t", [128, QT], F32)
    ot_sb = sbt("ot", [128, 1, QT], F32)
    wo_sb = sbt("wosb", [128, 2, 8, 128], BF16)
    wtu_sb = sbt("wtusb", [128, 2, 8, 128], BF16)
    wtd_sb = sbt("wtdsb", [128, 2, 4, 128], BF16)

    wts = sbt("wts", [128, 2, 1024], F32)
    wmix = sbt("wmix", [128, 2048], BF16)
    acca = sbt("acca", [128, 64], F32)
    accr = sbt("accr", [128, 32], F32)
    wt2t = sbt("wt2t", [128, 1024], BF16)

    wts_res = [R["wts0"], R["wts1"]]
    for dc in range(8):
        b = dc % 2
        l1 = P.op("scalar", lambda e, dc=dc, b=b: e.dma_start(
            out=wts[:, b, :].rearrange("p (a m) -> p a m", a=8), in_=woT[dc, :, :, :].transpose([1, 0, 2])),
            deps=wts_res[b].wd(), inc=f"wts{b}")
        wts_res[b].wrote(l1)
        cv = P.op("vector", lambda e, b=b: e.tensor_copy(wt2t[:, :], wts[:, b, :]),
                  deps=[l1] + R["wt2t"].wd(), inc="dve")
        wts_res[b].read(cv)
        R["wt2t"].wrote(cv)
        st = P.op("scalar", lambda e, dc=dc: e.dma_start(
            out=wo2[dc, :, :, :].transpose([1, 0, 2]),
            in_=wt2t[:, :].rearrange("p (a m) -> p a m", a=8)),
            deps=[cv], inc="w2s")
        R["wt2t"].read(st)
        hbm[f"wo2_{dc}"] = st

    def tern_units(which):
        if which == "up":
            for fc in range(32):
                yield (lambda fc=fc: wup[fc, :, :, :].transpose([1, 0, 2]),
                       lambda fc=fc: wtu2[fc, :, :, :].transpose([1, 0, 2]),
                       f"wtu2_{fc}")
        else:
            for dc in range(8):
                for u in range(4):
                    yield (lambda dc=dc, u=u: wdn[dc, u * 8:(u + 1) * 8, :, :].transpose([1, 0, 2]),
                           lambda dc=dc, u=u: wtd2[dc, u * 8:(u + 1) * 8, :, :].transpose([1, 0, 2]),
                           f"wtd2_{dc}" if u == 3 else None)

    ai_ctr = [0]
    acc2_res = Res()
    wm1f_res = R["wm1"]
    R["wm2"] = R["wm1"]
    m0_sb = sbt("m0sb", [128, 2], F32)
    m0_res = Res()
    sum_toks = {}
    tau_toks = {}
    tern_thunks = []

    tern_led = {}

    def _mk_ldA(inf, uid):
        def th():
            b = uid % 2
            ld = P.op("gpsimd", lambda e: e.dma_start(
                out=wts[:, b, :].rearrange("p (a m) -> p a m", a=8), in_=inf()),
                deps=wts_res[b].wd(), inc=f"tl{b}")
            wts_res[b].wrote(ld)
            tern_led[uid] = ld
        return th

    def _mk_unitA(inf, uid):
        def th():
            b = uid % 2
            ld = tern_led[uid]
            ab = P.op("scalar", lambda e, uid=uid: e.activation(
                out=wmix[:, :].bitcast(F32), in_=wts[:, b, :], func=AF.Abs,
                accum_out=acca[:, uid:uid + 1]),
                deps=[ld] + R["wm1"].wd() + acc2_res.wd(), inc="act")
            wts_res[b].read(ab)
            R["wm1"].wrote(ab)
            acc2_res.w = ab
        return th

    def _mk_fin(which):
        row = 0 if which == "up" else 32
        col = 0 if which == "up" else 1
        c0 = 0 if which == "up" else 32
        def th():
            deps = [acc2_res.w, c4] + psW_res[0].wd()
            mm = P.op("tensor", lambda e, row=row, c0=c0: e.matmul(
                psW[0][row:row + 1, 0:32], ones1c[:, :], acca[:, c0:c0 + 32],
                start=True, stop=True, skip_group_check=True),
                deps=deps, inc="pe")
            acc2_res.read(mm)
            psW_res[0].w = mm
            s_rough = P.op("vector", lambda e, row=row, col=col: e.reduce_sum(
                vec[0:1, 8 + col:9 + col], psW[0][row:row + 1, 0:32], axis=AX.X),
                deps=[mm] + R["vs0"].wd(), inc="dve")
            R["vs0"].w = s_rough
            psW_res[0].read(s_rough)
            m0 = P.op("vector", lambda e, col=col: e.tensor_scalar(
                vec[0:1, 10 + col:11 + col], vec[0:1, 8 + col:9 + col],
                1.0 / 4096.0, None, ALU.mult), deps=[s_rough], inc="dve")
            sm0 = P.op("sync", lambda e, col=col: e.dma_start(
                out=AP(scal_h.tensor, 6 + col, [[1, 1], [1, 1]]),
                in_=vec[0:1, 10 + col:11 + col]), deps=[m0], inc="scal")
            m0b = P.op("sync", lambda e, col=col: e.dma_start(
                out=m0_sb[:, col:col + 1],
                in_=AP(scal_h.tensor, 6 + col, [[0, 128], [1, 1]])),
                deps=[sm0] + m0_res.wd(), inc="scal")
            m0_res.wrote(m0b)
            rs = P.op("vector", lambda e, col=col, c0=c0: e.tensor_scalar(
                accr[:, :], acca[:, c0:c0 + 32], m0_sb[:, col:col + 1], None,
                ALU.subtract), deps=[acc2_res.w, m0b] + R["accr"].wd(),
                inc="dve")
            R["accr"].wrote(rs)
            acc2_res.read(rs)
            m0_res.read(rs)
            rrow = 64 if which == "up" else 0
            mmr = P.op("tensor", lambda e, rrow=rrow: e.matmul(
                psW[0][rrow:rrow + 1, 0:32], ones1c[:, :], accr[:, :],
                start=True, stop=True, skip_group_check=True),
                deps=[rs, c4, psW_res[0].w], inc="pe")
            R["accr"].read(mmr)
            psW_res[0].w = mmr
            rsum = P.op("vector", lambda e, col=col, rrow=rrow: e.reduce_sum(
                vec[0:1, 12 + col:13 + col], psW[0][rrow:rrow + 1, 0:32],
                axis=AX.X), deps=[mmr], inc="dve")
            psW_res[0].read(rsum)
            fin = P.op("vector", lambda e, col=col: e.scalar_tensor_tensor(
                vec[0:1, 8 + col:9 + col], vec[0:1, 10 + col:11 + col], 4096.0,
                vec[0:1, 12 + col:13 + col], ALU.mult, ALU.add),
                deps=[rsum, m0, R["vs0"].w], inc="dve")
            R["vs0"].w = fin
            sum_toks[which] = fin
        return th

    def _tau_thunk():
        t1 = P.op("vector", lambda e: e.tensor_scalar(
            vec[0:1, 10:11], vec[0:1, 8:9], -0.5 / (4 * 1024 * 1024), None,
            ALU.mult), deps=[sum_toks["up"], sum_toks["dn"]], inc="dve")
        t2 = P.op("vector", lambda e: e.tensor_scalar(
            vec[0:1, 11:12], vec[0:1, 8:9], 0.5 / (4 * 1024 * 1024), None,
            ALU.mult), deps=[sum_toks["up"]], inc="dve")
        t3 = P.op("vector", lambda e: e.tensor_scalar(
            vec[0:1, 12:13], vec[0:1, 9:10], -0.5 / (4 * 1024 * 1024), None,
            ALU.mult), deps=[sum_toks["dn"]], inc="dve")
        t4 = P.op("vector", lambda e: e.tensor_scalar(
            vec[0:1, 13:14], vec[0:1, 9:10], 0.5 / (4 * 1024 * 1024), None,
            ALU.mult), deps=[sum_toks["dn"]], inc="dve")
        t5 = P.op("sync", lambda e: e.dma_start(
            out=AP(scal_h.tensor, 0, [[1, 1], [1, 4]]), in_=vec[0:1, 10:14]),
            deps=[t1, t2, t3, t4], inc="scal")
        tu_t = P.op("sync", lambda e: e.dma_start(
            out=tauu_sb[:, :], in_=AP(scal_h.tensor, 0, [[0, 128], [1, 2]])),
            deps=[t5], inc="scal")
        td_t = P.op("sync", lambda e: e.dma_start(
            out=taud_sb[:, :], in_=AP(scal_h.tensor, 2, [[0, 128], [1, 2]])),
            deps=[t5] + R["taud"].wd(), inc="scal")
        R["tauu"].wrote(tu_t)
        R["taud"].wrote(td_t)
        tau_toks["up"] = tu_t
        tau_toks["dn"] = td_t

    def _mk_unitB(which, inf, outf, nname, uid):
        tau = tauu_sb if which == "up" else taud_sb
        def th():
            b = uid % 2
            tau_t = tau_toks[which]
            ld = tern_led[uid]
            p1 = P.op("vector", lambda e: e.tensor_scalar(
                wmix[:, 0:1024], wts[:, b, :], tau[:, 1:2], 2.0, ALU.is_ge,
                ALU.mult), deps=[ld, tau_t] + R["wm1"].wd(), inc="dve")
            R["wm1"].wrote(p1)
            p2 = P.op("vector", lambda e: e.tensor_scalar(
                wmix[:, 1024:2048], wts[:, b, :], tau[:, 0:1], -2.0, ALU.is_le,
                ALU.mult), deps=[ld] + R["wm2"].wd(), inc="dve")
            R["wm2"].wrote(p2)
            wts_res[b].read(p2)
            p3 = P.op("gpsimd", lambda e: e.tensor_tensor(
                wt2t[:, :], wmix[:, 0:1024], wmix[:, 1024:2048], ALU.add),
                deps=[p1, p2] + R["wt2t"].wd(), inc="pl")
            R["wm1"].read(p3)
            R["wm2"].read(p3)
            R["wt2t"].wrote(p3)
            st = P.op("gpsimd", lambda e: e.dma_start(
                out=outf(), in_=wt2t[:, :].rearrange("p (a m) -> p a m", a=8)),
                deps=[p3], inc="t2s")
            R["wt2t"].read(st)
            if nname:
                hbm[nname] = st
        return th

    lds = []
    comps = []
    uid = 0
    for which in ("up", "dn"):
        first = True
        for (inf, _outf, _n) in tern_units(which):
            lds.append(_mk_ldA(inf, uid))
            th = _mk_unitA(inf, uid)
            th._first = first
            first = False
            comps.append(th)
            uid += 1
    for which in ("up", "dn"):
        for (inf, outf, nname) in tern_units(which):
            lds.append(_mk_ldA(inf, uid))
            comps.append(_mk_unitB(which, inf, outf, nname, uid))
            uid += 1
    tern_thunks.append(lds[0])
    tern_thunks.append(lds[1])
    for u in range(128):
        tern_thunks.append(comps[u])
        if u + 2 < 128:
            tern_thunks.append(lds[u + 2])
        if u == 31:
            tern_thunks.append(_mk_fin("up"))
        if u == 63:
            tern_thunks.append(_mk_fin("dn"))
            tern_thunks.append(_tau_thunk)

    def load_piece(qt, pr, ph):
        slot = 1 - ph
        base = 127 + qt * 512 + (2048 if ph == 1 else 0)
        hA, hB = PAIRS[pr]
        rn = f"ew{slot}"
        dep = R[rn].wd() + [hbm[f"erep{hA}"], hbm[f"erep{hB}"]]
        tA = P.op("sync", lambda e, hA=hA, slot=slot, base=base: e.dma_start(
            out=ewin[:, slot, 0, :],
            in_=AP(erep.tensor, hA * 128 * L + base, [[L - 1, 128], [1, WPC]])),
            deps=dep, inc=f"ew{slot}")
        tB = P.op("sync", lambda e, hB=hB, slot=slot, base=base: e.dma_start(
            out=ewin[:, slot, 1, :],
            in_=AP(erep.tensor, hB * 128 * L + base, [[L - 1, 128], [1, WPC]])),
            deps=dep, inc=f"ew{slot}")
        R[rn].wrote(tB)
        R[rn].read(tA)
        return (tA, tB)

    def load_qtile(qt, pr):
        g = qt * 8 + pr
        b = g % 2
        rn = f"q{b}"
        tok = P.op("scalar", lambda e, qt=qt, pr=pr, b=b: e.dma_start(
            out=q_sb[:, b, :], in_=q_hbm[qt, pr, :, :]),
            deps=R[rn].wd() + [hbm[f"q{qt}_{pr}"]], inc=f"ql{b}")
        R[rn].wrote(tok)
        return tok

    win_tok = {}
    q_tok = {}
    win_tok[(0, 0, 1)] = load_piece(0, 0, 1)
    q_tok[0] = load_qtile(0, 0)
    q_tok[1] = load_qtile(0, 1)

    onorm = {q: [] for q in range(NQT)}
    out_toks = []

    from collections import deque
    work_q = deque()
    epi_q = []
    wu_led = {}

    def pump(n=1):
        for _ in range(n):
            if work_q:
                work_q.popleft()()

    def attention(qt, pr):
        g = qt * 8 + pr
        qb = g % 2
        t4 = pr // 2
        hA, hB = PAIRS[pr]
        kvA, kvB = hA // 2, hB // 2
        def emit_scores(kc):
            par = kc % 2
            sn = f"psS{par}"
            if kc == 0:
                win_tok[(qt, pr, 0)] = load_piece(qt, pr, 0)
            if kc == 16 and g + 1 < NQT * 8:
                win_tok[((g + 1) // 8, (g + 1) % 8, 1)] = \
                    load_piece((g + 1) // 8, (g + 1) % 8, 1)
            deps = R[sn].wd() + [k2_t, q_tok[g]]
            def mk_sc(e, t4=t4, kc=kc, qb=qb, par=par):
                e.matmul(psS[par][:, 0:512],
                         k2_sb[0:64, t4 * S + kc * 128:t4 * S + kc * 128 + 128],
                         q_sb[0:64, qb, :], start=True, stop=True,
                         tile_position=(0, 0), skip_group_check=True)
                return e.matmul(psS[par][:, 512:1024],
                                k2_sb[64:128, t4 * S + kc * 128:t4 * S + kc * 128 + 128],
                                q_sb[64:128, qb, :], start=True, stop=True,
                                tile_position=(64, 0), skip_group_check=True)
            sc = P.op("tensor", mk_sc, deps=deps, inc="pe")
            R[sn].wrote(sc)
            R[f"q{qb}"].read(sc)

        def emit_expmult(kc):
            par = kc % 2
            sn = f"psS{par}"
            en = f"eo{par}"
            ex = P.op("scalar", lambda e, par=par: e.activation(
                out=eo_sb[:, par, :], in_=psS[par][:, :], func=AF.Exp,
                bias=bias_sb[:, 0:1], scale=1.0),
                deps=R[sn].rd() + R[en].wd() + R["bias"].rd(), inc="act")
            R[sn].read(ex)
            R[en].wrote(ex)
            pn = f"pt{kc % 3}"
            ph = 1 if kc < 16 else 0
            slot = 1 - ph
            w0 = (1920 - 128 * kc) if ph == 1 else (3968 - 128 * kc)
            wt_ = win_tok[(qt, pr, ph)]
            deps_em = R[en].rd() + R[pn].wd() + [wt_[0], wt_[1]]
            p3 = kc % 3
            em0 = P.op("vector", lambda e, par=par, p3=p3, slot=slot, w0=w0:
                e.tensor_tensor(
                    pt_sb[:, p3, 0:512], eo_sb[:, par, 0:512],
                    ewin[:, slot, 0, w0:w0 + 512], ALU.mult),
                deps=deps_em, inc="dve")
            em = P.op("vector", lambda e, par=par, p3=p3, slot=slot, w0=w0:
                e.tensor_tensor(
                    pt_sb[:, p3, 512:1024], eo_sb[:, par, 512:1024],
                    ewin[:, slot, 1, w0:w0 + 512], ALU.mult),
                deps=deps_em, inc="dve")
            R[en].read(em0)
            R[en].read(em)
            R[pn].wrote(em)
            R[pn].read(em0)
            R[f"ew{slot}"].read(em)
            R[f"ew{slot}"].read(em0)

        def emit_av(kc):
            P.op("tensor", lambda e: e.ldweights(ones64[:, :]), deps=[c1])
            par = kc % 3
            pn = f"pt{par}"
            deps = R[pn].rd() + [v_t, c1]
            if kc == 0:
                deps += R["psO"].wd() + R["psR"].wd()
            def mk_av(e, kc=kc, par=par, kvA=kvA, kvB=kvB):
                st_, sp_ = (kc == 0), (kc == NKC - 1)
                e.matmul(psO[0:64, :],
                         v_sb[:, kc * 512 + 64 * kvA:kc * 512 + 64 * kvA + 64],
                         pt_sb[:, par, 0:512], start=st_, stop=sp_,
                         tile_position=(0, 0), skip_group_check=True)
                e.matmul(psO[64:128, :],
                         v_sb[:, kc * 512 + 64 * kvB:kc * 512 + 64 * kvB + 64],
                         pt_sb[:, par, 512:1024], start=st_, stop=sp_,
                         tile_position=(0, 64), skip_group_check=True)
                e.matmul(psR[0:64, :], ones64[:, :], pt_sb[:, par, 0:512],
                         start=st_, stop=sp_, tile_position=(0, 0),
                         skip_group_check=True)
                return e.matmul(psR[64:128, :], ones64[:, :], pt_sb[:, par, 512:1024],
                                start=st_, stop=sp_, tile_position=(0, 64),
                                skip_group_check=True)
            av = P.op("tensor", mk_av, deps=deps, inc="pe")
            R[pn].read(av)
            if kc == 0:
                R["psO"].wrote(av)
                R["psR"].wrote(av)
            else:
                R["psO"].w = av
                R["psR"].w = av

        for kc in range(NKC + 3):
            if kc < NKC:
                emit_scores(kc)
            if kc == 2 and epi_q:
                epi_q.pop(0)()
            if kc >= 3:
                emit_av(kc - 3)
            if kc < NKC:
                emit_expmult(kc)
            pump(1 + (len(work_q) > 160) + (len(work_q) > 320))
        if g + 2 < NQT * 8:
            q_tok[g + 2] = load_qtile((g + 2) // 8, (g + 2) % 8)

        def epilogue(qt=qt, pr=pr):
            rc0 = P.op("scalar", lambda e: e.activation(
                out=rr_sb[:, :], in_=psR[:, :], func=AF.Ln),
                deps=R["psR"].rd() + R["rr"].wd(), inc="act")
            R["psR"].read(rc0)
            R["rr"].wrote(rc0)
            rc = P.op("scalar", lambda e: e.activation(
                out=rr_sb[:, :], in_=rr_sb[:, :], func=AF.Exp, scale=-1.0),
                deps=[rc0], inc="act")
            R["rr"].wrote(rc)
            on = P.op("vector", lambda e, pr=pr: e.tensor_tensor(
                o_sb[:, pr, :], psO[:, :], rr_sb[:, :], ALU.mult),
                deps=R["psO"].rd() + [rc] + R["osb"].wd(), inc="dve")
            R["psO"].read(on)
            R["rr"].read(on)
            onorm[qt].append(on)
            R["osb"].w = on
            if pr == 7:
                R["osb"].wrote(on)
        epi_q.append(epilogue)

    def ln_scalars(sum_ap, sum2_ap, dep, n):
        m = P.op("vector", lambda e: e.tensor_scalar(
            vrow(0), sum_ap, 1.0 / n, None, ALU.mult),
            deps=[dep] + R["vs0"].wd(), inc="dve")
        R["vs0"].wrote(m)
        m2 = P.op("gpsimd", lambda e: e.tensor_tensor(
            vrow(3), vrow(0), vrow(0), ALU.mult),
            deps=[m] + R["vs3"].wd(), inc="pl")
        R["vs0"].read(m2)
        R["vs3"].wrote(m2)
        v = P.op("vector", lambda e: e.scalar_tensor_tensor(
            vrow(1), sum2_ap, 1.0 / n, vrow(3), ALU.mult, ALU.subtract),
            deps=[dep, m2] + R["vs1"].wd(), inc="dve")
        R["vs3"].read(v)
        R["vs1"].wrote(v)
        u = P.op("vector", lambda e: e.tensor_scalar(
            vrow(3), vrow(1), EPS_LN, None, ALU.add),
            deps=[v] + R["vs3"].wd(), inc="dve")
        R["vs1"].read(u)
        R["vs3"].wrote(u)
        lnu = P.op("scalar", lambda e: e.activation(
            out=vrow(3), in_=vrow(3), func=AF.Ln), deps=[u], inc="act")
        R["vs3"].wrote(lnu)
        a = P.op("scalar", lambda e: e.activation(
            out=vrow(2), in_=vrow(3), func=AF.Exp, scale=-0.5),
            deps=[lnu] + R["vs2"].wd(), inc="act")
        R["vs3"].read(a)
        R["vs2"].wrote(a)
        b = P.op("vector", lambda e: e.scalar_tensor_tensor(
            vrow(0), vrow(0), -1.0, vrow(2), ALU.mult, ALU.mult),
            deps=[a, m] + R["vs0"].wd(), inc="dve")
        R["vs2"].read(b)
        R["vs0"].wrote(b)
        return {"a": a, "b": b, "var": v}

    def rms_scale(var_dep, a_tok, out_slot_guarded):
        g1 = P.op("vector", lambda e: e.tensor_scalar_max(
            vrow(3), vrow(1), 1e-30), deps=[var_dep] + R["vs3"].wd(), inc="dve")
        R["vs1"].read(g1)
        R["vs3"].wrote(g1)
        l1 = P.op("scalar", lambda e: e.activation(
            out=vrow(3), in_=vrow(3), func=AF.Ln), deps=[g1], inc="act")
        R["vs3"].wrote(l1)
        e1 = P.op("scalar", lambda e: e.activation(
            out=vrow(3), in_=vrow(3), func=AF.Exp, scale=0.5), deps=[l1], inc="act")
        R["vs3"].wrote(e1)
        m1 = P.op("gpsimd", lambda e: e.tensor_tensor(
            vrow(3), vrow(3), vrow(2), ALU.mult), deps=[e1, a_tok], inc="pl")
        R["vs2"].read(m1)
        R["vs3"].wrote(m1)
        a1 = P.op("vector", lambda e: e.tensor_scalar(
            vrow(3), vrow(3), EPS_RMS, 2.0, ALU.add, ALU.mult),
            deps=[m1], inc="dve")
        R["vs3"].wrote(a1)
        rc0 = P.op("scalar", lambda e: e.activation(
            out=vrow(3), in_=vrow(3), func=AF.Ln), deps=[a1], inc="act")
        R["vs3"].wrote(rc0)
        sc = P.op("scalar", lambda e: e.activation(
            out=vrow(3), in_=vrow(3), func=AF.Exp, scale=-1.0),
            deps=[rc0], inc="act")
        R["vs3"].wrote(sc)
        return sc

    r2deps = [None] * 8

    def down_chunk(qt, dcs):
        NBD = 4 if qt == 3 else 2
        dbank = [psW[0][:, :], psW[1][:, :],
                 psS[1][:, 0:512], psS[1][:, 512:1024]]
        dbres = [psW_res[0], psW_res[1], R["td2"], R["td3"]]
        NWD = 6 if qt == 3 else 2
        wtsb = wts[:, 0, :].bitcast(BF16)
        dslot = [wtd_sb[:, 0, :, :].rearrange("p a m -> p (a m)"),
                 wtd_sb[:, 1, :, :].rearrange("p a m -> p (a m)"),
                 wtsb[:, 0:512], wtsb[:, 512:1024],
                 wtsb[:, 1024:1536], wtsb[:, 1536:2048]]
        dsres = [R["wtd0"], R["wtd1"], R["ts2"], R["ts3"], R["ts4"], R["ts5"]]
        for dc in dcs:
            pw = dc % NBD
            pwn = f"psW{pw}"
            state = {"first": True, "mm": None}
            def th_load(dc=dc, blk=0, state=state):
                g = dc * 8 + blk
                wb = g % NWD
                seed = (R["wts0"].wd() if (wb >= 2 and g == wb) else [])
                wt_ = P.op("scalar", lambda e, dc=dc, blk=blk, wb=wb: e.dma_start(
                    out=dslot[wb].rearrange("p (a m) -> p a m", a=4),
                    in_=wtd2[dc, blk * 4:blk * 4 + 4, :, :].transpose([1, 0, 2])),
                    deps=[hbm[f"wtd2_{dc}"]] + dsres[wb].wd() + seed,
                    inc=f"wd{wb % 2}")
                dsres[wb].wrote(wt_)
                state[f"ld{blk}"] = wt_
            def th_mm(dc=dc, blk=0, state=state, pw=pw, pwn=pwn):
                g = dc * 8 + blk
                wb = g % NWD
                first = state["first"]
                deps = [state[f"ld{blk}"]] + \
                    [R[f"h{blk * 4 + i}"].rd()[0] for i in range(4)
                     if R[f"h{blk * 4 + i}"].rd()]
                if first:
                    deps += dbres[pw].wd()
                    if pw >= 2 and dc == pw:
                        deps += R["psS1"].wd()
                def mk_dn(e, wb=wb, blk=blk, pw=pw, first=first):
                    last = None
                    for i in range(4):
                        last = e.matmul(dbank[pw],
                                        dslot[wb][:, i * 128:(i + 1) * 128],
                                        h_sb[:, blk * 4 + i, :],
                                        start=(first and i == 0),
                                        stop=(blk == 7 and i == 3),
                                        skip_group_check=True)
                    return last
                mm_tok = P.op("tensor", mk_dn, deps=deps, inc="pe")
                dsres[wb].read(mm_tok)
                if first:
                    dbres[pw].wrote(mm_tok)
                else:
                    dbres[pw].w = mm_tok
                state["first"] = False
                state["mm"] = mm_tok
            def th_epi(dc=dc, state=state, pw=pw, pwn=pwn):
                mm_tok = state["mm"]
                for i in range(32):
                    R[f"h{i}"].read(mm_tok)
                deps = [mm_tok] + R["rhb"].rd() + R["r2"].wd()
                tt1 = P.op("vector", lambda e, pw=pw: e.tensor_tensor(
                    r2t_sb[:, :], dbank[pw], rhb_sb[:, :], ALU.mult),
                    deps=deps, inc="dve")
                dbres[pw].read(tt1)
                R["rhb"].read(tt1)
                R["r2"].wrote(tt1)
                tt2 = P.op("vector", lambda e, dc=dc: e.tensor_tensor(
                    x1_sb[:, dc, :], r2t_sb[:, :], x1_sb[:, dc, :], ALU.add),
                    deps=[tt1] + R["x1"].wd(), inc="dve")
                R["r2"].read(tt2)
                R["x1"].w = tt2
                r2deps[dc] = tt2
            import functools
            lead = 4 if NWD > 2 else 2
            thunks = [functools.partial(th_load, dc, blk, state)
                      for blk in range(8)]
            seq = list(thunks[:lead])
            for blk in range(8):
                seq.append(functools.partial(th_mm, dc, blk, state, pw, pwn))
                if blk + lead < 8:
                    seq.append(thunks[blk + lead])
            seq.append(functools.partial(th_epi, dc, state, pw, pwn))
            work_q.extend(seq)

    def stage2(qt, p):
        ob = qt % 2
        if p == 0:
            wo_led = {}
            def th_wo_load(dc):
                wb = dc % 2
                wn = f"wo{wb}"
                wt_ = P.op("scalar", lambda e, dc=dc, wb=wb: e.dma_start(
                    out=wo_sb[:, wb, :, :].rearrange("p a m -> p (a m)"),
                    in_=wo2[dc, :, :, :].transpose([1, 0, 2])),
                    deps=[hbm[f"wo2_{dc}"]] + R[wn].wd(), inc=f"wo{wb}")
                R[wn].wrote(wt_)
                wo_led[dc] = wt_
            def th_wo_mm(dc, half):
                wb = dc % 2
                wn = f"wo{wb}"
                pw = dc % 2
                if half == 0:
                    xt_ = P.op("sync", lambda e, dc=dc, qt=qt: e.dma_start(
                        out=xo_sb[:, :],
                        in_=xqT[dc * 128:dc * 128 + 128,
                                qt * 512:qt * 512 + 512]),
                        deps=R["xo0"].wd(), inc="xo")
                    R["xo0"].wrote(xt_)
                    wo_led[(dc, "xo")] = xt_
                xt_ = wo_led[(dc, "xo")]
                deps = [wo_led[dc], R["osb"].w]
                if half == 0:
                    deps += psW_res[pw].wd()
                def mk_wo(e, wb=wb, pw=pw, half=half):
                    last = None
                    for j in range(half * 4, half * 4 + 4):
                        last = e.matmul(psW[pw][:, :], wo_sb[:, wb, j, :],
                                        o_sb[:, j, :],
                                        start=(half == 0 and j == 0),
                                        stop=(half == 1 and j == 7),
                                        skip_group_check=True)
                    return last
                mm = P.op("tensor", mk_wo, deps=deps, inc="pe")
                if half == 0:
                    psW_res[pw].wrote(mm)
                    R[wn].read(mm)
                    return
                psW_res[pw].w = mm
                R[wn].read(mm)
                if dc == 7:
                    R["osb"].read(mm)
                    R["osb"].wrote(mm)
                r1 = P.op("vector", lambda e, dc=dc, pw=pw: e.scalar_tensor_tensor(
                    x1_sb[:, dc, :], psW[pw][:, :], bo_sb[:, dc:dc + 1], xo_sb[:, :],
                    ALU.add, ALU.add),
                    deps=[mm, xt_, bo_t] + (R["x1"].wd() if dc == 0 else []),
                    inc="dve")
                psW_res[pw].read(r1)
                R["xo0"].read(r1)
                R["x1"].w = r1
            def th_stat(dc):
                x1w = R["x1"].w
                sq = P.op("vector", lambda e, dc=dc: e.tensor_tensor(
                    h2_sb[:, :], x1_sb[:, dc, :], x1_sb[:, dc, :], ALU.mult),
                    deps=[x1w] + R["h2"].wd(), inc="dve")
                R["h2"].wrote(sq)
                deps = [x1w, sq, c3, c4]
                if dc == 0:
                    deps += psW_res[0].wd() + psW_res[1].wd()
                def mk_st(e, dc=dc):
                    e.matmul(psW[0][0:1, :], ones1c[:, :], x1_sb[:, dc, :],
                             start=(dc == 0), stop=(dc == 7), skip_group_check=True)
                    return e.matmul(psW[1][0:1, :], onesb[:, :], h2_sb[:, :],
                                    start=(dc == 0), stop=(dc == 7),
                                    skip_group_check=True)
                mm = P.op("tensor", mk_st, deps=deps, inc="pe")
                R["h2"].read(mm)
                if dc == 0:
                    psW_res[0].wrote(mm)
                    psW_res[1].wrote(mm)
                else:
                    psW_res[0].w = mm
                    psW_res[1].w = mm
                R["x1"].read(mm)
            ln_state = {}
            def th_ln_a():
                toks = ln_scalars(psW[0][0:1, :], psW[1][0:1, :], psW_res[0].w, DM)
                psW_res[0].read(toks["var"])
                psW_res[1].read(toks["var"])
                ln_state.update(toks)
            def th_ln_b():
                s_tok = rms_scale(ln_state["var"], ln_state["a"], None)
                ln_state["s"] = s_tok
            def th_bcast():
                bca = P.op("tensor", lambda e: e.matmul(
                    psW[0][:, :], ones1f[:, :], vrow(2), start=True, stop=True,
                    skip_group_check=True),
                    deps=[ln_state["a"], c2] + psW_res[0].wd(), inc="pe")
                psW_res[0].wrote(bca)
                R["vs2"].read(bca)
                bcb = P.op("tensor", lambda e: e.matmul(
                    psW[1][:, :], ones1f[:, :], vrow(0), start=True, stop=True,
                    skip_group_check=True),
                    deps=[ln_state["b"]] + psW_res[1].wd(), inc="pe")
                psW_res[1].wrote(bcb)
                R["vs0"].read(bcb)
                ln_state["bca"] = bca
                ln_state["bcb"] = bcb
            def th_apply(dc):
                bca, bcb = ln_state["bca"], ln_state["bcb"]
                a1 = P.op("vector", lambda e, dc=dc: e.tensor_tensor(
                    r2t_sb[:, :], x1_sb[:, dc, :], psW[0][:, :], ALU.mult),
                    deps=[bca] + R["x1"].wd() + R["r2"].wd(), inc="dve")
                psW_res[0].read(a1)
                R["r2"].wrote(a1)
                a2 = P.op("vector", lambda e, dc=dc: e.tensor_tensor(
                    x1_sb[:, dc, :], r2t_sb[:, :], psW[1][:, :], ALU.add),
                    deps=[bcb, a1], inc="dve")
                psW_res[1].read(a2)
                R["r2"].read(a2)
                R["x1"].w = a2
            def th_bcast_s():
                bcs = P.op("tensor", lambda e: e.matmul(
                    psW[0][:, :], ones1f[:, :], vrow(3), start=True, stop=True,
                    skip_group_check=True),
                    deps=[ln_state["s"], R["x1"].w] + psW_res[0].wd(), inc="pe")
                psW_res[0].wrote(bcs)
                R["vs3"].read(bcs)
                ln_state["bcs"] = bcs
            def th_xn(dc):
                xn = P.op("vector", lambda e, dc=dc: e.tensor_tensor(
                    xn_sb[:, dc, :], x1_sb[:, dc, :], psW[0][:, :], ALU.mult),
                    deps=[ln_state["bcs"], R["x1"].w] +
                    (R["xn"].wd() if dc == 0 else []), inc="dve")
                psW_res[0].read(xn)
                R["x1"].read(xn)
                R["xn"].w = xn
            import functools as _ft
            seq = [_ft.partial(th_wo_load, 0), _ft.partial(th_wo_load, 1)]
            for dc in range(8):
                seq.append(_ft.partial(th_wo_mm, dc, 0))
                seq.append(_ft.partial(th_wo_mm, dc, 1))
                if dc + 2 < 8:
                    seq.append(_ft.partial(th_wo_load, dc + 2))
            for dc in range(8):
                seq.append(_ft.partial(th_stat, dc))
            seq += [th_ln_a, th_ln_b, th_bcast]
            for dc in range(8):
                seq.append(_ft.partial(th_apply, dc))
            seq.append(th_bcast_s)
            for dc in range(8):
                seq.append(_ft.partial(th_xn, dc))
            work_q.extend(seq)
        elif p in (1, 2, 3, 4):
            NW = 4 if qt == 3 else 2
            wslot = [wtu_sb[:, 0, :, :].rearrange("p a m -> p (a m)"),
                     wtu_sb[:, 1, :, :].rearrange("p a m -> p (a m)"),
                     wmix[:, 0:1024], wmix[:, 1024:2048]]
            wsres = [R["wtu0"], R["wtu1"], R["tw2"], R["tw3"]]
            def th_up_load(fc):
                wb = fc % NW
                seed = R["wm1"].wd() if (wb >= 2 and fc == wb) else []
                wt_ = P.op("scalar", lambda e, fc=fc, wb=wb: e.dma_start(
                    out=wslot[wb].rearrange("p (a m) -> p a m", a=8),
                    in_=wtu2[fc, :, :, :].transpose([1, 0, 2])),
                    deps=[hbm[f"wtu2_{fc}"]] + wsres[wb].wd() + seed,
                    inc=f"wu{wb}")
                wsres[wb].wrote(wt_)
                wu_led[fc] = wt_
            NB = 4 if qt == 3 else 2
            ubank = [psW[0][:, :], psW[1][:, :],
                     psS[0][:, 0:512], psS[0][:, 512:1024]]
            ubres = [psW_res[0], psW_res[1], R["tu2"], R["tu3"]]
            def th_up_mm(fc, half):
                wb = fc % NW
                pw = fc % NB
                deps = [wu_led[fc], R["xn"].w]
                if half == 0:
                    deps += ubres[pw].wd()
                    if pw >= 2 and fc == pw:
                        deps += R["psS0"].wd()
                def mk_up(e, wb=wb, pw=pw, half=half):
                    last = None
                    for j in range(half * 4, half * 4 + 4):
                        last = e.matmul(ubank[pw],
                                        wslot[wb][:, j * 128:(j + 1) * 128],
                                        xn_sb[:, j, :],
                                        start=(half == 0 and j == 0),
                                        stop=(half == 1 and j == 7),
                                        skip_group_check=True)
                    return last
                mm = P.op("tensor", mk_up, deps=deps, inc="pe")
                if half == 0:
                    ubres[pw].wrote(mm)
                else:
                    ubres[pw].w = mm
                wsres[wb].read(mm)
                R["xn"].read(mm)
            def th_gelu_pair(fc0):
                for fc in (fc0, fc0 + 1):
                    pw = fc % NB
                    g_ = P.op("scalar", lambda e, fc=fc, pw=pw: e.activation(
                        out=h_sb[:, fc, :], in_=ubank[pw], func=AF.Gelu),
                        deps=[ubres[pw].w] + R[f"h{fc}"].wd(), inc="act")
                    ubres[pw].read(g_)
                    R[f"h{fc}"].wrote(g_)
            import functools as _ft
            f0 = (p - 1) * 8
            seq = []
            if p == 1:
                seq += [_ft.partial(th_up_load, i) for i in range(NW)]
            for fc in range(f0, f0 + 8):
                seq.append(_ft.partial(th_up_mm, fc, 0))
                seq.append(_ft.partial(th_up_mm, fc, 1))
                if fc % 2 == 1:
                    seq.append(_ft.partial(th_gelu_pair, fc - 1))
                if fc + NW < 32:
                    seq.append(_ft.partial(th_up_load, fc + NW))
            work_q.extend(seq)
        elif p == 5:
            hsum_state = {"mm": None}
            def th_h2(fc):
                par = fc % 2
                hb = r2t_sb[:, :].bitcast(BF16)[:, 0:512] if par else h2_sb[:, :]
                hn = "r2" if par else "h2"
                sq = P.op("gpsimd", lambda e, fc=fc, hb=hb: e.tensor_tensor(
                    hb, h_sb[:, fc, :], h_sb[:, fc, :], ALU.mult),
                    deps=R[f"h{fc}"].rd() + R[hn].wd(), inc="pl")
                R[hn].wrote(sq)
                deps = [sq, c3]
                if fc == 0:
                    deps += psW_res[0].wd()
                mm = P.op("tensor", lambda e, fc=fc, hb=hb: e.matmul(
                    psW[0][0:1, :], onesb[:, :], hb, start=(fc == 0),
                    stop=(fc == 31), skip_group_check=True), deps=deps, inc="pe")
                R[hn].read(mm)
                if fc == 0:
                    psW_res[0].wrote(mm)
                else:
                    psW_res[0].w = mm
                hsum_state["mm"] = mm
            def th_rmsh():
                hmm = hsum_state["mm"]
                u0 = P.op("vector", lambda e: e.tensor_scalar(
                    vrow(3), psW[0][0:1, :], 1.0 / FF, None, ALU.mult),
                    deps=[hmm] + R["vs3"].wd(), inc="dve")
                psW_res[0].read(u0)
                R["vs3"].wrote(u0)
                u = P.op("vector", lambda e: e.tensor_scalar_max(
                    vrow(3), vrow(3), 1e-30), deps=[u0], inc="dve")
                R["vs3"].wrote(u)
                l1 = P.op("scalar", lambda e: e.activation(
                    out=vrow(3), in_=vrow(3), func=AF.Ln), deps=[u], inc="act")
                R["vs3"].wrote(l1)
                e1 = P.op("scalar", lambda e: e.activation(
                    out=vrow(3), in_=vrow(3), func=AF.Exp, scale=0.5),
                    deps=[l1], inc="act")
                R["vs3"].wrote(e1)
                a1 = P.op("vector", lambda e: e.tensor_scalar(
                    vrow(3), vrow(3), EPS_RMS, 2.0, ALU.add, ALU.mult),
                    deps=[e1], inc="dve")
                R["vs3"].wrote(a1)
                rc0 = P.op("scalar", lambda e: e.activation(
                    out=vrow(3), in_=vrow(3), func=AF.Ln), deps=[a1], inc="act")
                R["vs3"].wrote(rc0)
                sc = P.op("scalar", lambda e: e.activation(
                    out=vrow(3), in_=vrow(3), func=AF.Exp, scale=-1.0),
                    deps=[rc0], inc="act")
                R["vs3"].wrote(sc)
                bc = P.op("tensor", lambda e: e.matmul(
                    psW[1][:, :], ones1f[:, :], vrow(3), start=True, stop=True,
                    skip_group_check=True), deps=[sc, c2] + psW_res[1].wd(),
                    inc="pe")
                psW_res[1].wrote(bc)
                R["vs3"].read(bc)
                cp = P.op("vector", lambda e: e.tensor_copy(rhb_sb[:, :],
                                                            psW[1][:, :]),
                          deps=[bc] + R["rhb"].wd(), inc="dve")
                psW_res[1].read(cp)
                R["rhb"].wrote(cp)
            import functools as _ft
            seq = [_ft.partial(th_h2, fc) for fc in range(32)]
            seq.append(th_rmsh)
            work_q.extend(seq)
            down_chunk(qt, [0, 1])
        elif p == 6:
            down_chunk(qt, [2, 3, 4])
        elif p == 7:
            ln2 = {}
            if qt == 3:
                sA, sB = psO[0:1, :], psR[0:1, :]
                resA, resB = R["psO"], R["psR"]
            else:
                sA, sB = psW[0][0:1, :], psW[1][0:1, :]
                resA, resB = psW_res[0], psW_res[1]
            def th_stat2(dc):
                sq = P.op("vector", lambda e, dc=dc: e.tensor_tensor(
                    h2_sb[:, :], x1_sb[:, dc, :], x1_sb[:, dc, :], ALU.mult),
                    deps=[r2deps[dc]] + R["h2"].wd(), inc="dve")
                R["h2"].wrote(sq)
                deps = [r2deps[dc], sq, c3, c4]
                if dc == 0:
                    deps += resA.wd() + resB.wd()
                def mk_st(e, dc=dc):
                    e.matmul(sA, ones1c[:, :], x1_sb[:, dc, :],
                             start=(dc == 0), stop=(dc == 7), skip_group_check=True)
                    return e.matmul(sB, onesb[:, :], h2_sb[:, :],
                                    start=(dc == 0), stop=(dc == 7),
                                    skip_group_check=True)
                mm = P.op("tensor", mk_st, deps=deps, inc="pe")
                R["h2"].read(mm)
                if dc == 0:
                    resA.wrote(mm)
                    resB.wrote(mm)
                else:
                    resA.w = mm
                    resB.w = mm
                R["x1"].read(mm)
            def th_ln2():
                toks = ln_scalars(sA, sB, resA.w, DM)
                resA.read(toks["var"])
                resB.read(toks["var"])
                bca = P.op("tensor", lambda e: e.matmul(
                    psW[0][:, :], ones1f[:, :], vrow(2), start=True, stop=True,
                    skip_group_check=True),
                    deps=[toks["a"], c2] + psW_res[0].wd(), inc="pe")
                psW_res[0].wrote(bca)
                R["vs2"].read(bca)
                bcb = P.op("tensor", lambda e: e.matmul(
                    psW[1][:, :], ones1f[:, :], vrow(0), start=True, stop=True,
                    skip_group_check=True),
                    deps=[toks["b"]] + psW_res[1].wd(), inc="pe")
                psW_res[1].wrote(bcb)
                R["vs0"].read(bcb)
                ln2["bca"] = bca
                ln2["bcb"] = bcb
            def th_out(dc):
                bca, bcb = ln2["bca"], ln2["bcb"]
                obuf = dc % 2 if qt == 3 else 0
                ob = ot_sb[:, 0, :] if obuf == 0 else rhb_sb[:, :]
                onn = f"ot{obuf}" if obuf == 0 else "rhb"
                a1 = P.op("vector", lambda e, dc=dc, ob=ob: e.tensor_tensor(
                    ob, x1_sb[:, dc, :], psW[0][:, :], ALU.mult),
                    deps=[bca, r2deps[dc]] + R[onn].wd(), inc="dve")
                psW_res[0].read(a1)
                R["x1"].read(a1)
                a2 = P.op("vector", lambda e, dc=dc, ob=ob: e.tensor_tensor(
                    ob, ob, psW[1][:, :], ALU.add),
                    deps=[bcb, a1], inc="dve")
                psW_res[1].read(a2)
                R[onn].wrote(a2)
                od = P.op("sync", lambda e, dc=dc, qt=qt, ob=ob: e.dma_start(
                    out=out[dc * 128:dc * 128 + 128, qt * 512:qt * 512 + 512],
                    in_=ob), deps=[a2], inc=f"ot{obuf}")
                R[onn].read(od)
                out_toks.append(od)
            import functools as _ft
            if qt == 3:
                down_chunk(qt, [5])
                work_q.extend([_ft.partial(th_stat2, 0),
                               _ft.partial(th_stat2, 1)])
                down_chunk(qt, [6])
                work_q.extend([_ft.partial(th_stat2, 2),
                               _ft.partial(th_stat2, 3)])
                down_chunk(qt, [7])
                seq = [_ft.partial(th_stat2, dc) for dc in range(4, 8)]
            else:
                down_chunk(qt, [5, 6, 7])
                seq = [_ft.partial(th_stat2, dc) for dc in range(8)]
            seq.append(th_ln2)
            seq += [_ft.partial(th_out, dc) for dc in range(8)]
            work_q.extend(seq)

    work_q.extend(tern_thunks)
    tern_thunks = []
    for qt in range(NQT + 1):
        for pr in range(8):
            if qt >= 1 and pr == 0:
                while epi_q:
                    epi_q.pop(0)()
                for p_ in range(8):
                    stage2(qt - 1, p_)
            if qt < NQT:
                attention(qt, pr)
        if qt == NQT:
            while work_q:
                work_q.popleft()()

    P.op("sync", lambda e: None, deps=[out_toks[-1]])

    sems = {}
    semctx = ExitStack()
    for s in ("pe", "act", "dve", "pl") + DMASEMS:
        sems[s] = semctx.enter_context(nc.semaphore(s))
    with nc.Block() as block:
        @block.sync
        def _(eng):
            P.replay("sync", eng, sems)

        @block.tensor
        def _(eng):
            P.replay("tensor", eng, sems)

        @block.scalar
        def _(eng):
            P.replay("scalar", eng, sems)

        @block.vector
        def _(eng):
            P.replay("vector", eng, sems)

        @block.gpsimd
        def _(eng):
            P.replay("gpsimd", eng, sems)
    semctx.close()
    ctx.close()
    return nc


_NC_CACHE = {}


def kernel(L0_attn_x, L0_attn_Wq, L0_attn_Wk, L0_attn_Wv, L0_attn_Wo,
           L0_attn_bo, L0_ffn_w_up, L0_ffn_w_down):
    x = np.ascontiguousarray(np.asarray(L0_attn_x, dtype=np.float32))
    Wq = np.asarray(L0_attn_Wq, dtype=np.float32)
    Wk = np.asarray(L0_attn_Wk, dtype=np.float32)
    Wv = np.asarray(L0_attn_Wv, dtype=np.float32)
    Wo = np.asarray(L0_attn_Wo, dtype=np.float32)
    bov = np.asarray(L0_attn_bo, dtype=np.float32)
    wu = np.asarray(L0_ffn_w_up, dtype=np.float32)
    wd = np.asarray(L0_ffn_w_down, dtype=np.float32)

    perm = PERM_ROWS
    wqT_t = np.ascontiguousarray(
        Wq[perm].T.reshape(8, 128, 8, 128).transpose(2, 0, 1, 3))
    woT_t = np.ascontiguousarray(
        Wo.T[perm].reshape(8, 128, 8, 128).transpose(2, 0, 1, 3))
    wkT = np.ascontiguousarray(Wk.T)
    wvT = np.ascontiguousarray(Wv.T)
    wupT_t = np.ascontiguousarray(
        wu.T.reshape(8, 128, 32, 128).transpose(2, 0, 1, 3))
    wdnT_t = np.ascontiguousarray(
        wd.T.reshape(32, 128, 8, 128).transpose(2, 0, 1, 3))
    bo2 = bov.reshape(1, DM)
    slopes = (-(2.0 ** (-(np.arange(16, dtype=np.float64) + 1))) / 8.0)\
        .astype(np.float32).reshape(16, 1)

    j = np.arange(L, dtype=np.float64)
    dist_h = [np.ascontiguousarray(
        np.broadcast_to(np.abs(j - 4095.0 + half * 2048.0), (16, L))
        .astype(np.float32)) for half in (0, 1)]

    if "nc" not in _NC_CACHE:
        _NC_CACHE["nc"] = build_nc()
    nc = _NC_CACHE["nc"]

    in_maps = []
    for c in range(8):
        b, half = c // 2, c % 2
        xT_c = np.ascontiguousarray(x[b].T)
        xqT_c = np.ascontiguousarray(x[b].T[:, half * TOK:(half + 1) * TOK])
        in_maps.append({
            "xT": xT_c, "xqT": xqT_c, "wqT": wqT_t, "wkT": wkT, "wvT": wvT,
            "woT": woT_t, "bo": bo2, "wup": wupT_t, "wdn": wdnT_t,
            "dist": dist_h[half], "slop": slopes,
        })
    res = run_bass_kernel_spmd(nc, in_maps, core_ids=list(range(8)))
    y = np.empty((4, S, DM), dtype=np.float32)
    for c in range(8):
        b, half = c // 2, c % 2
        y[b, half * TOK:(half + 1) * TOK, :] = res.results[c]["out"].T
    return y


# version 41
# speedup vs baseline: 1.0211x; 1.0211x over previous
import numpy as np
import concourse.bass as bass
import concourse.mybir as mybir
from concourse.bass_utils import run_bass_kernel_spmd
from concourse.ap import AP
from contextlib import ExitStack

F32 = mybir.dt.float32
BF16 = mybir.dt.bfloat16
AF = mybir.ActivationFunctionType
ALU = mybir.AluOpType
AX = mybir.AxisListType

DM = 1024
S = 4096
HD = 64
FF = 4096
TOK = 2048
NQT = 4
QT = 512
NKC = 32
L = 6144
WPC = 2432
EPS_LN = 1e-5
EPS_RMS = 1e-8
SHIFT = 20.0

PAIRS = [(4 * (j // 2) + (j % 2), 4 * (j // 2) + (j % 2) + 2) for j in range(8)]
PERM_ROWS = np.concatenate(
    [np.arange(h * HD, (h + 1) * HD) for j in range(8) for h in PAIRS[j]])

ENGS = ("sync", "tensor", "scalar", "vector", "gpsimd")


class Prog:
    def __init__(self):
        self.ops = {e: [] for e in ENGS}
        self.cnt = {}
        self.step = {}
        self.marks = {e: {} for e in ENGS}

    def sem(self, name, step=1):
        self.cnt[name] = 0
        self.step[name] = step

    def op(self, eng, fn, deps=(), inc=None):
        deps = list(deps)
        if inc is not None and self.step[inc] == 16:
            if self.cnt[inc] > 0:
                deps.append((inc, self.cnt[inc]))
        waits = []
        for d in deps:
            if d is None:
                continue
            s, v = d
            if v <= 0 or self.marks[eng].get(s, 0) >= v:
                continue
            self.marks[eng][s] = v
            waits.append((s, v))
        tok = None
        if inc is not None:
            self.cnt[inc] += self.step[inc]
            tok = (inc, self.cnt[inc])
        self.ops[eng].append((waits, fn, inc))
        return tok

    def barrier(self):
        toks = [(s, self.cnt[s]) for s in self.cnt]
        for e in ENGS:
            self.op(e, lambda eng: None, deps=toks)

    def replay(self, name, eng, sems):
        for waits, fn, inc in self.ops[name]:
            for (s, v) in waits:
                eng.wait_ge(sems[s], v)
            ins = fn(eng)
            if inc is not None:
                if ins is None:
                    eng.sem_inc(sems[inc], self.step[inc])
                else:
                    ins.then_inc(sems[inc], self.step[inc])


class Res:

    def __init__(self):
        self.w = None
        self.r = []

    def wd(self):
        return ([self.w] if self.w else []) + list(self.r)

    def rd(self):
        return [self.w] if self.w else []

    def wrote(self, tok):
        self.w = tok
        self.r = []

    def read(self, tok):
        if tok is None:
            return
        best = {}
        for s, v in self.r + [tok]:
            best[s] = max(best.get(s, 0), v)
        self.r = list(best.items())


def build_nc():
    nc = bass.Bass("TRN2", target_bir_lowering=False)

    xT = nc.dram_tensor("xT", [DM, S], F32, kind="ExternalInput").ap()
    xqT = nc.dram_tensor("xqT", [DM, TOK], F32, kind="ExternalInput").ap()
    wqT = nc.dram_tensor("wqT", [8, 8, 128, 128], F32, kind="ExternalInput").ap()
    wkT = nc.dram_tensor("wkT", [DM, 512], F32, kind="ExternalInput").ap()
    wvT = nc.dram_tensor("wvT", [DM, 512], F32, kind="ExternalInput").ap()
    woT = nc.dram_tensor("woT", [8, 8, 128, 128], F32, kind="ExternalInput").ap()
    bo = nc.dram_tensor("bo", [1, DM], F32, kind="ExternalInput").ap()
    wup = nc.dram_tensor("wup", [32, 8, 128, 128], F32, kind="ExternalInput").ap()
    wdn = nc.dram_tensor("wdn", [8, 32, 128, 128], F32, kind="ExternalInput").ap()
    dist = nc.dram_tensor("dist", [16, L], F32, kind="ExternalInput").ap()
    slop = nc.dram_tensor("slop", [16, 1], F32, kind="ExternalInput").ap()
    out = nc.dram_tensor("out", [DM, TOK], F32, kind="ExternalOutput").ap()

    erow_h = nc.dram_tensor("erow_h", [16, L], BF16, kind="Internal").ap()
    erep = nc.dram_tensor("erep", [16, 128, L], BF16, kind="Internal").ap()
    q_hbm = nc.dram_tensor("q_hbm", [NQT, 8, 128, QT], BF16, kind="Internal").ap()
    wo2 = nc.dram_tensor("wo2", [8, 8, 128, 128], BF16, kind="Internal").ap()
    wtu2 = nc.dram_tensor("wtu2", [32, 8, 128, 128], BF16, kind="Internal").ap()
    wtd2 = nc.dram_tensor("wtd2", [8, 32, 128, 128], BF16, kind="Internal").ap()
    scal_h = nc.dram_tensor("scal_h", [1, 8], F32, kind="Internal").ap()

    P = Prog()
    DMASEMS = ("xq0", "xq1", "xf0", "xf1", "xf2", "xf3", "rep", "scal", "wqs0", "wqs1",
               "wk", "wv", "qs0", "qs1", "wts0", "wts1", "w2s", "tl0", "tl1",
               "ew0", "ew1", "ql0", "ql1", "xo", "wo0", "wo1", "wu0", "wu1", "wu2", "wu3",
               "wd0", "wd1", "ot0", "ot1", "t2s", "rep2")
    for s in ("pe", "act", "dve", "pl"):
        P.sem(s)
    for s in DMASEMS:
        P.sem(s, 16)

    ctx = ExitStack()
    sbt = lambda n, sh, dt: ctx.enter_context(nc.sbuf_tensor(n, sh, dt))
    pst = lambda n, sh: ctx.enter_context(nc.psum_tensor(n, sh, F32))

    k2_sb = sbt("k2", [128, 4 * S], BF16)
    v_sb = sbt("v", [128, NKC * 512], BF16)
    vec = sbt("vec", [128, 4 * QT], F32)
    ones64 = sbt("ones64", [128, 64], BF16)
    ones1f = sbt("ones1f", [1, 128], F32)
    onesb = sbt("onesb", [128, 1], BF16)
    ones1c = sbt("ones1c", [128, 1], F32)
    bo_sb = sbt("bosb", [128, 8], F32)
    bias_sb = sbt("biasc", [128, 1], F32)
    tauu_sb = sbt("tauu", [128, 2], F32)
    taud_sb = sbt("taud", [128, 2], F32)
    slop_sb = sbt("slopsb", [16, 1], F32)

    psS = [pst("psS0", [128, 1024]), pst("psS1", [128, 1024])]
    psO = pst("psO", [128, QT])
    psR = pst("psR", [128, QT])
    psW = [pst("psW0", [128, QT]), pst("psW1", [128, QT])]

    R = {}
    for n in ("psS0", "psS1", "psO", "psR", "psW0", "psW1", "eo0", "eo1",
              "pt0", "pt1", "pt2", "ew0", "ew1", "q0", "q1", "osb", "xo0", "xo1",
              "x1", "xn", "h2", "rhb", "r2", "ot0", "ot1", "wo0", "wo1",
              "wtu0", "wtu1", "wtd0", "wtd1", "rr", "bias", "vs0", "vs1",
              "vs2", "vs3", "wm1", "wm2", "wt2t", "wts0", "wts1", "acc",
              "tauu", "taud", "nrm0", "nrm1", "nrm2", "h2p", "accr",
              "tu2", "tu3", "td2", "td3", "tw2", "tw3", "ts2", "ts3",
              "ts4", "ts5"):
        R[n] = Res()
    for fc in range(32):
        R[f"h{fc}"] = Res()
    hbm = {}

    vrow = lambda k: vec[0:1, k * QT:(k + 1) * QT]

    slop_t = P.op("sync", lambda e: e.dma_start(out=slop_sb[:, :], in_=slop[:, :]),
                  inc="scal")
    bo_t = P.op("sync", lambda e: e.dma_start(
        out=bo_sb[:, :], in_=AP(bo.tensor, 0, [[1, 128], [128, 8]]),
        allow_slow_non_contiguous=True), inc="scal")
    R["bias"].wrote(bo_t)
    c1 = P.op("gpsimd", lambda e: e.memset(ones64[:, :], 1.0), inc="pl")
    c2 = P.op("gpsimd", lambda e: e.memset(ones1f[:, :], 1.0), inc="pl")
    c3 = P.op("gpsimd", lambda e: e.memset(onesb[:, :], 1.0), inc="pl")
    c4 = P.op("gpsimd", lambda e: e.memset(ones1c[:, :], 1.0), inc="pl")
    ones_t = [c1, c2, c3, c4]

    ctxQ = ExitStack()
    dst = ctxQ.enter_context(nc.sbuf_tensor("dst", [16, L], F32))
    ero = ctxQ.enter_context(nc.sbuf_tensor("ero", [16, L], BF16))
    xsq = ctxQ.enter_context(nc.sbuf_tensor("xsq", [128, 2, 512], F32))
    xsf = ctxQ.enter_context(nc.sbuf_tensor("xsf", [128, 4, 512], F32))
    xbq = ctxQ.enter_context(nc.sbuf_tensor("xbq", [128, 2, 8, 512], BF16))
    xbf = ctxQ.enter_context(nc.sbuf_tensor("xbf", [128, 2, 8, 512], BF16))
    wqs = ctxQ.enter_context(nc.sbuf_tensor("wqs", [128, 2, 1024], F32))
    wqb = ctxQ.enter_context(nc.sbuf_tensor("wqb", [128, 8, 8, 128], BF16))
    qev = ctxQ.enter_context(nc.sbuf_tensor("qev", [128, 2, QT], BF16))
    qsq = ctxQ.enter_context(nc.sbuf_tensor("qsq", [128, QT], BF16))
    wks = ctxQ.enter_context(nc.sbuf_tensor("wks", [128, 4, 512], F32))
    wkb = ctxQ.enter_context(nc.sbuf_tensor("wkb", [128, 8, 512], BF16))
    wvb = ctxQ.enter_context(nc.sbuf_tensor("wvb", [128, 8, 512], BF16))
    ksq = ctxQ.enter_context(nc.sbuf_tensor("ksq", [128, QT], BF16))

    dst_t = P.op("sync", lambda e: e.dma_start(out=dst[:, :], in_=dist[:, :]),
                 inc="rep")

    wqb_res = Res()
    wq_ld = []
    for j in range(2):
        wq_ld.append(P.op("scalar", lambda e, j=j: e.dma_start(
            out=wqs[:, j, :].rearrange("p (a m) -> p a m", a=8),
            in_=wqT[j, :, :, :].transpose([1, 0, 2])), inc=f"wqs{j}"))

    ero_t = P.op("scalar", lambda e: e.activation(
        out=ero[:, :], in_=dst[:, :], func=AF.Exp, bias=0.0,
        scale=slop_sb[:, 0:1]), deps=[dst_t, slop_t], inc="act")
    erow_t = P.op("scalar", lambda e: e.dma_start(out=erow_h[:, :], in_=ero[:, :]),
                  deps=[ero_t], inc="rep")
    for h in [hh for j in range(8) for hh in PAIRS[j]]:
        hbm[f"erep{h}"] = P.op("gpsimd", lambda e, h=h: e.dma_start(
            out=erep[h, :, :], in_=AP(erow_h.tensor, h * L, [[0, 128], [1, L]])),
            deps=[erow_t], inc="rep2")

    xsq_res = [Res(), Res()]
    xsf_res = [Res(), Res(), Res(), Res()]
    xbq_res = [Res(), Res()]
    xbf_res = [Res(), Res()]
    xq_cv = {}
    xf_cv = {}

    def load_xq_slab(qt):
        xq_cv[qt] = []
        for c in range(8):
            sb = c % 2
            ld = P.op("sync", lambda e, c=c, qt=qt, sb=sb: e.dma_start(
                out=xsq[:, sb, :],
                in_=xqT[c * 128:(c + 1) * 128, qt * 512:(qt + 1) * 512]),
                deps=xsq_res[sb].wd(), inc=f"xq{sb}")
            xsq_res[sb].wrote(ld)
            cv = P.op("vector", lambda e, c=c, qt=qt, sb=sb: e.tensor_copy(
                xbq[:, qt % 2, c, :], xsq[:, sb, :]),
                deps=[ld] + xbq_res[qt % 2].wd(), inc="dve")
            xsq_res[sb].read(cv)
            xbq_res[qt % 2].wrote(cv)
            xq_cv[qt].append(cv)

    def load_xf_slab(tb):
        xf_cv[tb] = []
        for c in range(8):
            sb = c % 4
            ld = P.op("sync", lambda e, c=c, tb=tb, sb=sb: e.dma_start(
                out=xsf[:, sb, :],
                in_=xT[c * 128:(c + 1) * 128, tb * 512:(tb + 1) * 512]),
                deps=xsf_res[sb].wd(), inc=f"xf{sb}")
            xsf_res[sb].wrote(ld)
            cv = P.op("vector", lambda e, c=c, tb=tb, sb=sb: e.tensor_copy(
                xbf[:, tb % 2, c, :], xsf[:, sb, :]),
                deps=[ld] + xbf_res[tb % 2].wd(), inc="dve")
            xsf_res[sb].read(cv)
            xbf_res[tb % 2].wrote(cv)
            xf_cv[tb].append(cv)

    load_xq_slab(0)
    load_xf_slab(0)

    wk_cv, wv_cv = [], []
    for c in range(8):
        l1 = P.op("scalar", lambda e, c=c: e.dma_start(
            out=wks[:, c % 2, :], in_=wkT[c * 128:(c + 1) * 128, :]),
            deps=[wk_cv[c - 2]] if c >= 2 else [], inc="wk")
        k1 = P.op("vector", lambda e, c=c: e.tensor_copy(wkb[:, c, :], wks[:, c % 2, :]),
                  deps=[l1], inc="dve")
        wk_cv.append(k1)
        l2 = P.op("scalar", lambda e, c=c: e.dma_start(
            out=wks[:, 2 + c % 2, :], in_=wvT[c * 128:(c + 1) * 128, :]),
            deps=[wv_cv[c - 2]] if c >= 2 else [], inc="wv")
        k2t_ = P.op("vector", lambda e, c=c: e.tensor_copy(wvb[:, c, :], wks[:, 2 + c % 2, :]),
                    deps=[l2], inc="dve")
        wv_cv.append(k2t_)

    wq_cv = []
    for j in range(8):
        if j >= 2:
            wq_ld.append(P.op("scalar", lambda e, j=j: e.dma_start(
                out=wqs[:, j % 2, :].rearrange("p (a m) -> p a m", a=8),
                in_=wqT[j, :, :, :].transpose([1, 0, 2])),
                deps=[wq_cv[j - 2]], inc=f"wqs{j % 2}"))
        cv = P.op("vector", lambda e, j=j: e.tensor_copy(
            wqb[:, j, :, :].rearrange("p a m -> p (a m)"), wqs[:, j % 2, :]),
            deps=[wq_ld[j]], inc="dve")
        wq_cv.append(cv)

    load_xq_slab(1)
    load_xf_slab(1)

    qev_res = [Res(), Res()]
    qsq_res = Res()
    ksq_res = Res()
    psW_res = [R["psW0"], R["psW1"]]
    k2_ev = {}
    v_ev = []
    gq = [0]
    nrm_i = [0]
    qn_i = [0]
    kn_i = [0]

    def emit_q(qt):
        for j in range(8):
            pw = gq[0] % 2
            gq[0] += 1
            deps = [wq_cv[j]] + xq_cv[qt] + psW_res[pw].wd()

            def mk_q(e, j=j, qt=qt, pw=pw):
                last = None
                for k in range(8):
                    last = e.matmul(psW[pw][:, :], wqb[:, j, k, :],
                                    xbq[:, qt % 2, k, :],
                                    start=(k == 0), stop=(k == 7),
                                    skip_group_check=True)
                return last
            mm = P.op("tensor", mk_q, deps=deps, inc="pe")
            psW_res[pw].wrote(mm)
            xbq_res[qt % 2].read(mm)
            ev = P.op("vector", lambda e, pw=pw: e.tensor_scalar(
                qev[:, pw, :], psW[pw][:, :], 0.125, None, ALU.mult),
                deps=[mm] + qev_res[pw].wd(), inc="dve")
            psW_res[pw].read(ev)
            qev_res[pw].wrote(ev)
            st = P.op("scalar", lambda e, j=j, qt=qt, pw=pw: e.dma_start(
                out=q_hbm[qt, j, :, :], in_=qev[:, pw, :]), deps=[ev],
                inc=f"qs{pw}")
            hbm[f"q{qt}_{j}"] = st
            qev_res[pw].read(st)
            sq = P.op("scalar", lambda e, pw=pw: e.activation(
                out=qsq[:, :], in_=qev[:, pw, :], func=AF.Square),
                deps=[ev] + qsq_res.wd(), inc="act")
            qev_res[pw].read(sq)
            qsq_res.wrote(sq)
            row = 32 * (nrm_i[0] % 3)
            rown = f"nrm{nrm_i[0] % 3}"
            nrm_i[0] += 1
            qn = qn_i[0]
            qn_i[0] += 1
            nm = P.op("tensor", lambda e, row=row: e.matmul(
                psR[row:row + 1, :], onesb[:, :], qsq[:, :], start=True,
                stop=True, skip_group_check=True),
                deps=[sq, c3] + R[rown].wd(), inc="pe")
            qsq_res.read(nm)
            R[rown].wrote(nm)
            rd = P.op("vector", lambda e, row=row, qn=qn: e.reduce_max(
                vec[0:1, 16 + qn:17 + qn], psR[row:row + 1, :], axis=AX.X),
                deps=[nm] + (R["vs0"].wd() if qn == 0 else []), inc="dve")
            R[rown].read(rd)
            R["vs0"].w = rd

    def emit_knorm():
        kn = kn_i[0]
        if kn >= 32:
            return
        tt, tb = kn % 4, kn // 4
        if (tt, tb) not in k2_ev:
            return
        kn_i[0] += 1
        sq = P.op("scalar", lambda e, tt=tt, tb=tb: e.activation(
            out=ksq[:, :], in_=k2_sb[:, tt * S + tb * 512:tt * S + tb * 512 + 512],
            func=AF.Square), deps=[k2_ev[(tt, tb)]] + ksq_res.wd(), inc="act")
        ksq_res.wrote(sq)
        row = 32 * (nrm_i[0] % 3)
        rown = f"nrm{nrm_i[0] % 3}"
        nrm_i[0] += 1
        nm = P.op("tensor", lambda e, row=row: e.matmul(
            psR[row:row + 1, :], onesb[:, :], ksq[:, :], start=True,
            stop=True, skip_group_check=True),
            deps=[sq, c3] + R[rown].wd(), inc="pe")
        ksq_res.read(nm)
        R[rown].wrote(nm)
        rd = P.op("vector", lambda e, row=row, kn=kn: e.reduce_max(
            vec[0:1, 48 + kn:49 + kn], psR[row:row + 1, :], axis=AX.X),
            deps=[nm], inc="dve")
        R[rown].read(rd)
        R["vs0"].w = rd

    def emit_kv(tb):
        for tt in range(4):
            pw = gq[0] % 2
            gq[0] += 1
            deps = wk_cv + xf_cv[tb] + psW_res[pw].wd()

            def mk_k(e, tt=tt, tb=tb, pw=pw):
                last = None
                for k in range(8):
                    last = e.matmul(psW[pw][:, :],
                                    wkb[:, k, tt * 128:tt * 128 + 128],
                                    xbf[:, tb % 2, k, :],
                                    start=(k == 0), stop=(k == 7),
                                    skip_group_check=True)
                return last
            mm = P.op("tensor", mk_k, deps=deps, inc="pe")
            psW_res[pw].wrote(mm)
            xbf_res[tb % 2].read(mm)
            ev = P.op("vector", lambda e, tt=tt, tb=tb, pw=pw: e.tensor_copy(
                k2_sb[:, tt * S + tb * 512:tt * S + (tb + 1) * 512],
                psW[pw][:, :]), deps=[mm], inc="dve")
            psW_res[pw].read(ev)
            k2_ev[(tt, tb)] = ev
        for kcl in range(4):
            kc = tb * 4 + kcl
            pw = gq[0] % 2
            gq[0] += 1
            deps = wv_cv + xf_cv[tb] + psW_res[pw].wd()

            def mk_v(e, kcl=kcl, tb=tb, pw=pw):
                last = None
                for k in range(8):
                    last = e.matmul(psW[pw][:, :],
                                    xbf[:, tb % 2, k, kcl * 128:(kcl + 1) * 128],
                                    wvb[:, k, :], start=(k == 0), stop=(k == 7),
                                    skip_group_check=True)
                return last
            mm = P.op("tensor", mk_v, deps=deps, inc="pe")
            psW_res[pw].wrote(mm)
            xbf_res[tb % 2].read(mm)
            ev = P.op("vector", lambda e, kc=kc, pw=pw: e.tensor_copy(
                v_sb[:, kc * 512:(kc + 1) * 512], psW[pw][:, :]), deps=[mm],
                inc="dve")
            psW_res[pw].read(ev)
            v_ev.append(ev)
            emit_knorm()

    for stp in range(8):
        emit_kv(stp)
        if stp < 4:
            emit_q(stp)
            if stp + 2 < 4:
                load_xq_slab(stp + 2)
        if stp + 2 < 8:
            load_xf_slab(stp + 2)
    while kn_i[0] < 32:
        emit_knorm()
    k2_t = k2_ev[(3, 7)]
    v_t = v_ev[-1]

    ctxQ.close()
    P.barrier()

    nrm = R["vs0"].w
    r1_ = P.op("vector", lambda e: e.reduce_max(
        vec[0:1, 0:1], vec[0:1, 16:48], axis=AX.X), deps=[nrm], inc="dve")
    r2_ = P.op("vector", lambda e: e.reduce_max(
        vec[0:1, 5:6], vec[0:1, 48:80], axis=AX.X), deps=[nrm], inc="dve")
    R["vs0"].w = r2_
    m1_ = P.op("vector", lambda e: e.tensor_tensor(
        vec[0:1, 1:2], vec[0:1, 0:1], vec[0:1, 5:6], ALU.mult),
        deps=[r1_, r2_], inc="dve")
    l1_ = P.op("scalar", lambda e: e.activation(
        out=vec[0:1, 2:3], in_=vec[0:1, 1:2], func=AF.Ln), deps=[m1_], inc="act")
    e1_ = P.op("scalar", lambda e: e.activation(
        out=vec[0:1, 3:4], in_=vec[0:1, 2:3], func=AF.Exp, scale=0.5),
        deps=[l1_], inc="act")
    n1_ = P.op("vector", lambda e: e.tensor_scalar(
        vec[0:1, 4:5], vec[0:1, 3:4], -1.0, SHIFT, ALU.mult, ALU.add),
        deps=[e1_], inc="dve")
    s1_ = P.op("sync", lambda e: e.dma_start(
        out=AP(scal_h.tensor, 4, [[1, 1], [1, 1]]), in_=vec[0:1, 4:5]),
        deps=[n1_], inc="scal")
    bias_t = P.op("sync", lambda e: e.dma_start(
        out=bias_sb[:, :], in_=AP(scal_h.tensor, 4, [[0, 128], [1, 1]])),
        deps=[s1_], inc="scal")
    R["bias"].wrote(bias_t)
    for _tk in (m1_, l1_, e1_, n1_, s1_):
        R["vs0"].read(_tk)

    ewin = sbt("ewin", [128, 2, 2, WPC], BF16)
    eo_sb = sbt("eo", [128, 2, 1024], BF16)
    pt_sb = sbt("pt", [128, 3, 1024], BF16)
    rr_sb = sbt("rr", [128, QT], F32)
    o_sb = sbt("osb", [128, 8, QT], BF16)
    q_sb = sbt("qsb", [128, 2, QT], BF16)
    xo_sb = sbt("xo", [128, QT], F32)
    x1_sb = sbt("x1", [128, 8, QT], F32)
    xn_sb = sbt("xn", [128, 8, QT], BF16)
    h_sb = sbt("h", [128, 32, QT], BF16)
    h2_sb = sbt("h2", [128, QT], BF16)
    rhb_sb = sbt("rhb", [128, QT], F32)
    r2t_sb = sbt("r2t", [128, QT], F32)
    ot_sb = sbt("ot", [128, 1, QT], F32)
    wo_sb = sbt("wosb", [128, 2, 8, 128], BF16)
    wtu_sb = sbt("wtusb", [128, 2, 8, 128], BF16)
    wtd_sb = sbt("wtdsb", [128, 2, 4, 128], BF16)

    wts = sbt("wts", [128, 2, 1024], F32)
    wmix = sbt("wmix", [128, 2048], BF16)
    acca = sbt("acca", [128, 64], F32)
    accr = sbt("accr", [128, 32], F32)
    wt2t = sbt("wt2t", [128, 1024], BF16)

    wts_res = [R["wts0"], R["wts1"]]
    for dc in range(8):
        b = dc % 2
        l1 = P.op("scalar", lambda e, dc=dc, b=b: e.dma_start(
            out=wts[:, b, :].rearrange("p (a m) -> p a m", a=8), in_=woT[dc, :, :, :].transpose([1, 0, 2])),
            deps=wts_res[b].wd(), inc=f"wts{b}")
        wts_res[b].wrote(l1)
        cv = P.op("vector", lambda e, b=b: e.tensor_copy(wt2t[:, :], wts[:, b, :]),
                  deps=[l1] + R["wt2t"].wd(), inc="dve")
        wts_res[b].read(cv)
        R["wt2t"].wrote(cv)
        st = P.op("scalar", lambda e, dc=dc: e.dma_start(
            out=wo2[dc, :, :, :].transpose([1, 0, 2]),
            in_=wt2t[:, :].rearrange("p (a m) -> p a m", a=8)),
            deps=[cv], inc="w2s")
        R["wt2t"].read(st)
        hbm[f"wo2_{dc}"] = st

    def tern_units(which):
        if which == "up":
            for fc in range(32):
                yield (lambda fc=fc: wup[fc, :, :, :].transpose([1, 0, 2]),
                       lambda fc=fc: wtu2[fc, :, :, :].transpose([1, 0, 2]),
                       f"wtu2_{fc}")
        else:
            for dc in range(8):
                for u in range(4):
                    yield (lambda dc=dc, u=u: wdn[dc, u * 8:(u + 1) * 8, :, :].transpose([1, 0, 2]),
                           lambda dc=dc, u=u: wtd2[dc, u * 8:(u + 1) * 8, :, :].transpose([1, 0, 2]),
                           f"wtd2_{dc}" if u == 3 else None)

    ai_ctr = [0]
    acc2_res = Res()
    wm1f_res = R["wm1"]
    R["wm2"] = R["wm1"]
    m0_sb = sbt("m0sb", [128, 2], F32)
    m0_res = Res()
    sum_toks = {}
    tau_toks = {}
    tern_thunks = []

    tern_led = {}

    def _mk_ldA(inf, uid):
        def th():
            b = uid % 2
            ld = P.op("gpsimd", lambda e: e.dma_start(
                out=wts[:, b, :].rearrange("p (a m) -> p a m", a=8), in_=inf()),
                deps=wts_res[b].wd(), inc=f"tl{b}")
            wts_res[b].wrote(ld)
            tern_led[uid] = ld
        return th

    def _mk_unitA(inf, uid):
        def th():
            b = uid % 2
            ld = tern_led[uid]
            ab = P.op("scalar", lambda e, uid=uid: e.activation(
                out=wmix[:, :].bitcast(F32), in_=wts[:, b, :], func=AF.Abs,
                accum_out=acca[:, uid:uid + 1]),
                deps=[ld] + R["wm1"].wd() + acc2_res.wd(), inc="act")
            wts_res[b].read(ab)
            R["wm1"].wrote(ab)
            acc2_res.w = ab
        return th

    def _mk_fin(which):
        row = 0 if which == "up" else 32
        col = 0 if which == "up" else 1
        c0 = 0 if which == "up" else 32
        def th():
            deps = [acc2_res.w, c4] + psW_res[0].wd()
            mm = P.op("tensor", lambda e, row=row, c0=c0: e.matmul(
                psW[0][row:row + 1, 0:32], ones1c[:, :], acca[:, c0:c0 + 32],
                start=True, stop=True, skip_group_check=True),
                deps=deps, inc="pe")
            acc2_res.read(mm)
            psW_res[0].w = mm
            s_rough = P.op("vector", lambda e, row=row, col=col: e.reduce_sum(
                vec[0:1, 8 + col:9 + col], psW[0][row:row + 1, 0:32], axis=AX.X),
                deps=[mm] + R["vs0"].wd(), inc="dve")
            R["vs0"].w = s_rough
            psW_res[0].read(s_rough)
            m0 = P.op("vector", lambda e, col=col: e.tensor_scalar(
                vec[0:1, 10 + col:11 + col], vec[0:1, 8 + col:9 + col],
                1.0 / 4096.0, None, ALU.mult), deps=[s_rough], inc="dve")
            sm0 = P.op("sync", lambda e, col=col: e.dma_start(
                out=AP(scal_h.tensor, 6 + col, [[1, 1], [1, 1]]),
                in_=vec[0:1, 10 + col:11 + col]), deps=[m0], inc="scal")
            m0b = P.op("sync", lambda e, col=col: e.dma_start(
                out=m0_sb[:, col:col + 1],
                in_=AP(scal_h.tensor, 6 + col, [[0, 128], [1, 1]])),
                deps=[sm0] + m0_res.wd(), inc="scal")
            m0_res.wrote(m0b)
            rs = P.op("vector", lambda e, col=col, c0=c0: e.tensor_scalar(
                accr[:, :], acca[:, c0:c0 + 32], m0_sb[:, col:col + 1], None,
                ALU.subtract), deps=[acc2_res.w, m0b] + R["accr"].wd(),
                inc="dve")
            R["accr"].wrote(rs)
            acc2_res.read(rs)
            m0_res.read(rs)
            rrow = 64 if which == "up" else 0
            mmr = P.op("tensor", lambda e, rrow=rrow: e.matmul(
                psW[0][rrow:rrow + 1, 0:32], ones1c[:, :], accr[:, :],
                start=True, stop=True, skip_group_check=True),
                deps=[rs, c4, psW_res[0].w], inc="pe")
            R["accr"].read(mmr)
            psW_res[0].w = mmr
            rsum = P.op("vector", lambda e, col=col, rrow=rrow: e.reduce_sum(
                vec[0:1, 12 + col:13 + col], psW[0][rrow:rrow + 1, 0:32],
                axis=AX.X), deps=[mmr], inc="dve")
            psW_res[0].read(rsum)
            fin = P.op("vector", lambda e, col=col: e.scalar_tensor_tensor(
                vec[0:1, 8 + col:9 + col], vec[0:1, 10 + col:11 + col], 4096.0,
                vec[0:1, 12 + col:13 + col], ALU.mult, ALU.add),
                deps=[rsum, m0, R["vs0"].w], inc="dve")
            R["vs0"].w = fin
            sum_toks[which] = fin
        return th

    def _tau_thunk():
        t1 = P.op("vector", lambda e: e.tensor_scalar(
            vec[0:1, 10:11], vec[0:1, 8:9], -0.5 / (4 * 1024 * 1024), None,
            ALU.mult), deps=[sum_toks["up"], sum_toks["dn"]], inc="dve")
        t2 = P.op("vector", lambda e: e.tensor_scalar(
            vec[0:1, 11:12], vec[0:1, 8:9], 0.5 / (4 * 1024 * 1024), None,
            ALU.mult), deps=[sum_toks["up"]], inc="dve")
        t3 = P.op("vector", lambda e: e.tensor_scalar(
            vec[0:1, 12:13], vec[0:1, 9:10], -0.5 / (4 * 1024 * 1024), None,
            ALU.mult), deps=[sum_toks["dn"]], inc="dve")
        t4 = P.op("vector", lambda e: e.tensor_scalar(
            vec[0:1, 13:14], vec[0:1, 9:10], 0.5 / (4 * 1024 * 1024), None,
            ALU.mult), deps=[sum_toks["dn"]], inc="dve")
        t5 = P.op("sync", lambda e: e.dma_start(
            out=AP(scal_h.tensor, 0, [[1, 1], [1, 4]]), in_=vec[0:1, 10:14]),
            deps=[t1, t2, t3, t4], inc="scal")
        tu_t = P.op("sync", lambda e: e.dma_start(
            out=tauu_sb[:, :], in_=AP(scal_h.tensor, 0, [[0, 128], [1, 2]])),
            deps=[t5], inc="scal")
        td_t = P.op("sync", lambda e: e.dma_start(
            out=taud_sb[:, :], in_=AP(scal_h.tensor, 2, [[0, 128], [1, 2]])),
            deps=[t5] + R["taud"].wd(), inc="scal")
        R["tauu"].wrote(tu_t)
        R["taud"].wrote(td_t)
        tau_toks["up"] = tu_t
        tau_toks["dn"] = td_t

    def _mk_unitB(which, inf, outf, nname, uid):
        tau = tauu_sb if which == "up" else taud_sb
        def th():
            b = uid % 2
            tau_t = tau_toks[which]
            ld = tern_led[uid]
            p1 = P.op("vector", lambda e: e.tensor_scalar(
                wmix[:, 0:1024], wts[:, b, :], tau[:, 1:2], 2.0, ALU.is_ge,
                ALU.mult), deps=[ld, tau_t] + R["wm1"].wd(), inc="dve")
            R["wm1"].wrote(p1)
            p2 = P.op("vector", lambda e: e.tensor_scalar(
                wmix[:, 1024:2048], wts[:, b, :], tau[:, 0:1], -2.0, ALU.is_le,
                ALU.mult), deps=[ld] + R["wm2"].wd(), inc="dve")
            R["wm2"].wrote(p2)
            wts_res[b].read(p2)
            p3 = P.op("gpsimd", lambda e: e.tensor_tensor(
                wt2t[:, :], wmix[:, 0:1024], wmix[:, 1024:2048], ALU.add),
                deps=[p1, p2] + R["wt2t"].wd(), inc="pl")
            R["wm1"].read(p3)
            R["wm2"].read(p3)
            R["wt2t"].wrote(p3)
            st = P.op("gpsimd", lambda e: e.dma_start(
                out=outf(), in_=wt2t[:, :].rearrange("p (a m) -> p a m", a=8)),
                deps=[p3], inc="t2s")
            R["wt2t"].read(st)
            if nname:
                hbm[nname] = st
        return th

    lds = []
    comps = []
    uid = 0
    for which in ("up", "dn"):
        first = True
        for (inf, _outf, _n) in tern_units(which):
            lds.append(_mk_ldA(inf, uid))
            th = _mk_unitA(inf, uid)
            th._first = first
            first = False
            comps.append(th)
            uid += 1
    for which in ("up", "dn"):
        for (inf, outf, nname) in tern_units(which):
            lds.append(_mk_ldA(inf, uid))
            comps.append(_mk_unitB(which, inf, outf, nname, uid))
            uid += 1
    tern_thunks.append(lds[0])
    tern_thunks.append(lds[1])
    for u in range(128):
        tern_thunks.append(comps[u])
        if u + 2 < 128:
            tern_thunks.append(lds[u + 2])
        if u == 31:
            tern_thunks.append(_mk_fin("up"))
        if u == 63:
            tern_thunks.append(_mk_fin("dn"))
            tern_thunks.append(_tau_thunk)

    def load_piece(qt, pr, ph):
        slot = 1 - ph
        base = 127 + qt * 512 + (2048 if ph == 1 else 0)
        hA, hB = PAIRS[pr]
        rn = f"ew{slot}"
        dep = R[rn].wd() + [hbm[f"erep{hA}"], hbm[f"erep{hB}"]]
        tA = P.op("sync", lambda e, hA=hA, slot=slot, base=base: e.dma_start(
            out=ewin[:, slot, 0, :],
            in_=AP(erep.tensor, hA * 128 * L + base, [[L - 1, 128], [1, WPC]])),
            deps=dep, inc=f"ew{slot}")
        tB = P.op("sync", lambda e, hB=hB, slot=slot, base=base: e.dma_start(
            out=ewin[:, slot, 1, :],
            in_=AP(erep.tensor, hB * 128 * L + base, [[L - 1, 128], [1, WPC]])),
            deps=dep, inc=f"ew{slot}")
        R[rn].wrote(tB)
        R[rn].read(tA)
        return (tA, tB)

    def load_qtile(qt, pr):
        g = qt * 8 + pr
        b = g % 2
        rn = f"q{b}"
        tok = P.op("scalar", lambda e, qt=qt, pr=pr, b=b: e.dma_start(
            out=q_sb[:, b, :], in_=q_hbm[qt, pr, :, :]),
            deps=R[rn].wd() + [hbm[f"q{qt}_{pr}"]], inc=f"ql{b}")
        R[rn].wrote(tok)
        return tok

    win_tok = {}
    q_tok = {}
    win_tok[(0, 0, 1)] = load_piece(0, 0, 1)
    q_tok[0] = load_qtile(0, 0)
    q_tok[1] = load_qtile(0, 1)

    onorm = {q: [] for q in range(NQT)}
    out_toks = []

    from collections import deque
    work_q = deque()
    epi_q = []
    wu_led = {}

    def pump(n=1):
        for _ in range(n):
            if work_q:
                work_q.popleft()()

    def attention(qt, pr):
        g = qt * 8 + pr
        qb = g % 2
        t4 = pr // 2
        hA, hB = PAIRS[pr]
        kvA, kvB = hA // 2, hB // 2
        def emit_scores(kc):
            par = kc % 2
            sn = f"psS{par}"
            if kc == 0:
                win_tok[(qt, pr, 0)] = load_piece(qt, pr, 0)
            if kc == 16 and g + 1 < NQT * 8:
                win_tok[((g + 1) // 8, (g + 1) % 8, 1)] = \
                    load_piece((g + 1) // 8, (g + 1) % 8, 1)
            deps = R[sn].wd() + [k2_t, q_tok[g]]
            def mk_sc(e, t4=t4, kc=kc, qb=qb, par=par):
                e.matmul(psS[par][:, 0:512],
                         k2_sb[0:64, t4 * S + kc * 128:t4 * S + kc * 128 + 128],
                         q_sb[0:64, qb, :], start=True, stop=True,
                         tile_position=(0, 0), skip_group_check=True)
                return e.matmul(psS[par][:, 512:1024],
                                k2_sb[64:128, t4 * S + kc * 128:t4 * S + kc * 128 + 128],
                                q_sb[64:128, qb, :], start=True, stop=True,
                                tile_position=(64, 0), skip_group_check=True)
            sc = P.op("tensor", mk_sc, deps=deps, inc="pe")
            R[sn].wrote(sc)
            R[f"q{qb}"].read(sc)

        def emit_expmult(kc):
            par = kc % 2
            sn = f"psS{par}"
            en = f"eo{par}"
            ex = P.op("scalar", lambda e, par=par: e.activation(
                out=eo_sb[:, par, :], in_=psS[par][:, :], func=AF.Exp,
                bias=bias_sb[:, 0:1], scale=1.0),
                deps=R[sn].rd() + R[en].wd() + R["bias"].rd(), inc="act")
            R[sn].read(ex)
            R[en].wrote(ex)
            pn = f"pt{kc % 3}"
            ph = 1 if kc < 16 else 0
            slot = 1 - ph
            w0 = (1920 - 128 * kc) if ph == 1 else (3968 - 128 * kc)
            wt_ = win_tok[(qt, pr, ph)]
            deps_em = R[en].rd() + R[pn].wd() + [wt_[0], wt_[1]]
            p3 = kc % 3
            em0 = P.op("vector", lambda e, par=par, p3=p3, slot=slot, w0=w0:
                e.tensor_tensor(
                    pt_sb[:, p3, 0:512], eo_sb[:, par, 0:512],
                    ewin[:, slot, 0, w0:w0 + 512], ALU.mult),
                deps=deps_em, inc="dve")
            em = P.op("vector", lambda e, par=par, p3=p3, slot=slot, w0=w0:
                e.tensor_tensor(
                    pt_sb[:, p3, 512:1024], eo_sb[:, par, 512:1024],
                    ewin[:, slot, 1, w0:w0 + 512], ALU.mult),
                deps=deps_em, inc="dve")
            R[en].read(em0)
            R[en].read(em)
            R[pn].wrote(em)
            R[pn].read(em0)
            R[f"ew{slot}"].read(em)
            R[f"ew{slot}"].read(em0)

        def emit_av(kc):
            P.op("tensor", lambda e: e.ldweights(ones64[:, :]), deps=[c1])
            par = kc % 3
            pn = f"pt{par}"
            deps = R[pn].rd() + [v_t, c1]
            if kc == 0:
                deps += R["psO"].wd() + R["psR"].wd()
            def mk_av(e, kc=kc, par=par, kvA=kvA, kvB=kvB):
                st_, sp_ = (kc == 0), (kc == NKC - 1)
                e.matmul(psO[0:64, :],
                         v_sb[:, kc * 512 + 64 * kvA:kc * 512 + 64 * kvA + 64],
                         pt_sb[:, par, 0:512], start=st_, stop=sp_,
                         tile_position=(0, 0), skip_group_check=True)
                e.matmul(psO[64:128, :],
                         v_sb[:, kc * 512 + 64 * kvB:kc * 512 + 64 * kvB + 64],
                         pt_sb[:, par, 512:1024], start=st_, stop=sp_,
                         tile_position=(0, 64), skip_group_check=True)
                e.matmul(psR[0:64, :], ones64[:, :], pt_sb[:, par, 0:512],
                         start=st_, stop=sp_, tile_position=(0, 0),
                         skip_group_check=True)
                return e.matmul(psR[64:128, :], ones64[:, :], pt_sb[:, par, 512:1024],
                                start=st_, stop=sp_, tile_position=(0, 64),
                                skip_group_check=True)
            av = P.op("tensor", mk_av, deps=deps, inc="pe")
            R[pn].read(av)
            if kc == 0:
                R["psO"].wrote(av)
                R["psR"].wrote(av)
            else:
                R["psO"].w = av
                R["psR"].w = av

        for kc in range(NKC + 3):
            if kc < NKC:
                emit_scores(kc)
            if kc == 2 and epi_q:
                epi_q.pop(0)()
            if kc >= 3:
                emit_av(kc - 3)
            if kc < NKC:
                emit_expmult(kc)
            pump(1 + (len(work_q) > 120) + (len(work_q) > 280))
        if g + 2 < NQT * 8:
            q_tok[g + 2] = load_qtile((g + 2) // 8, (g + 2) % 8)

        def epilogue(qt=qt, pr=pr):
            rc0 = P.op("scalar", lambda e: e.activation(
                out=rr_sb[:, :], in_=psR[:, :], func=AF.Ln),
                deps=R["psR"].rd() + R["rr"].wd(), inc="act")
            R["psR"].read(rc0)
            R["rr"].wrote(rc0)
            rc = P.op("scalar", lambda e: e.activation(
                out=rr_sb[:, :], in_=rr_sb[:, :], func=AF.Exp, scale=-1.0),
                deps=[rc0], inc="act")
            R["rr"].wrote(rc)
            on = P.op("vector", lambda e, pr=pr: e.tensor_tensor(
                o_sb[:, pr, :], psO[:, :], rr_sb[:, :], ALU.mult),
                deps=R["psO"].rd() + [rc] + R["osb"].wd(), inc="dve")
            R["psO"].read(on)
            R["rr"].read(on)
            onorm[qt].append(on)
            R["osb"].w = on
            if pr == 7:
                R["osb"].wrote(on)
        epi_q.append(epilogue)

    def ln_scalars(sum_ap, sum2_ap, dep, n):
        m = P.op("vector", lambda e: e.tensor_scalar(
            vrow(0), sum_ap, 1.0 / n, None, ALU.mult),
            deps=[dep] + R["vs0"].wd(), inc="dve")
        R["vs0"].wrote(m)
        m2 = P.op("gpsimd", lambda e: e.tensor_tensor(
            vrow(3), vrow(0), vrow(0), ALU.mult),
            deps=[m] + R["vs3"].wd(), inc="pl")
        R["vs0"].read(m2)
        R["vs3"].wrote(m2)
        v = P.op("vector", lambda e: e.scalar_tensor_tensor(
            vrow(1), sum2_ap, 1.0 / n, vrow(3), ALU.mult, ALU.subtract),
            deps=[dep, m2] + R["vs1"].wd(), inc="dve")
        R["vs3"].read(v)
        R["vs1"].wrote(v)
        u = P.op("vector", lambda e: e.tensor_scalar(
            vrow(3), vrow(1), EPS_LN, None, ALU.add),
            deps=[v] + R["vs3"].wd(), inc="dve")
        R["vs1"].read(u)
        R["vs3"].wrote(u)
        lnu = P.op("scalar", lambda e: e.activation(
            out=vrow(3), in_=vrow(3), func=AF.Ln), deps=[u], inc="act")
        R["vs3"].wrote(lnu)
        a = P.op("scalar", lambda e: e.activation(
            out=vrow(2), in_=vrow(3), func=AF.Exp, scale=-0.5),
            deps=[lnu] + R["vs2"].wd(), inc="act")
        R["vs3"].read(a)
        R["vs2"].wrote(a)
        b = P.op("vector", lambda e: e.scalar_tensor_tensor(
            vrow(0), vrow(0), -1.0, vrow(2), ALU.mult, ALU.mult),
            deps=[a, m] + R["vs0"].wd(), inc="dve")
        R["vs2"].read(b)
        R["vs0"].wrote(b)
        return {"a": a, "b": b, "var": v}

    def rms_scale(var_dep, a_tok, out_slot_guarded):
        g1 = P.op("vector", lambda e: e.tensor_scalar_max(
            vrow(3), vrow(1), 1e-30), deps=[var_dep] + R["vs3"].wd(), inc="dve")
        R["vs1"].read(g1)
        R["vs3"].wrote(g1)
        l1 = P.op("scalar", lambda e: e.activation(
            out=vrow(3), in_=vrow(3), func=AF.Ln), deps=[g1], inc="act")
        R["vs3"].wrote(l1)
        e1 = P.op("scalar", lambda e: e.activation(
            out=vrow(3), in_=vrow(3), func=AF.Exp, scale=0.5), deps=[l1], inc="act")
        R["vs3"].wrote(e1)
        m1 = P.op("gpsimd", lambda e: e.tensor_tensor(
            vrow(3), vrow(3), vrow(2), ALU.mult), deps=[e1, a_tok], inc="pl")
        R["vs2"].read(m1)
        R["vs3"].wrote(m1)
        a1 = P.op("vector", lambda e: e.tensor_scalar(
            vrow(3), vrow(3), EPS_RMS, 2.0, ALU.add, ALU.mult),
            deps=[m1], inc="dve")
        R["vs3"].wrote(a1)
        rc0 = P.op("scalar", lambda e: e.activation(
            out=vrow(3), in_=vrow(3), func=AF.Ln), deps=[a1], inc="act")
        R["vs3"].wrote(rc0)
        sc = P.op("scalar", lambda e: e.activation(
            out=vrow(3), in_=vrow(3), func=AF.Exp, scale=-1.0),
            deps=[rc0], inc="act")
        R["vs3"].wrote(sc)
        return sc

    r2deps = [None] * 8

    def down_chunk(qt, dcs):
        NBD = 4 if qt == 3 else 2
        dbank = [psW[0][:, :], psW[1][:, :],
                 psS[1][:, 0:512], psS[1][:, 512:1024]]
        dbres = [psW_res[0], psW_res[1], R["td2"], R["td3"]]
        NWD = 6 if qt == 3 else 2
        wtsb = wts[:, 0, :].bitcast(BF16)
        dslot = [wtd_sb[:, 0, :, :].rearrange("p a m -> p (a m)"),
                 wtd_sb[:, 1, :, :].rearrange("p a m -> p (a m)"),
                 wtsb[:, 0:512], wtsb[:, 512:1024],
                 wtsb[:, 1024:1536], wtsb[:, 1536:2048]]
        dsres = [R["wtd0"], R["wtd1"], R["ts2"], R["ts3"], R["ts4"], R["ts5"]]
        for dc in dcs:
            pw = dc % NBD
            pwn = f"psW{pw}"
            state = {"first": True, "mm": None}
            def th_load(dc=dc, blk=0, state=state):
                g = dc * 8 + blk
                wb = g % NWD
                seed = (R["wts0"].wd() if (wb >= 2 and g == wb) else [])
                wt_ = P.op("scalar", lambda e, dc=dc, blk=blk, wb=wb: e.dma_start(
                    out=dslot[wb].rearrange("p (a m) -> p a m", a=4),
                    in_=wtd2[dc, blk * 4:blk * 4 + 4, :, :].transpose([1, 0, 2])),
                    deps=[hbm[f"wtd2_{dc}"]] + dsres[wb].wd() + seed,
                    inc=f"wd{wb % 2}")
                dsres[wb].wrote(wt_)
                state[f"ld{blk}"] = wt_
            def th_mm(dc=dc, blk=0, state=state, pw=pw, pwn=pwn):
                g = dc * 8 + blk
                wb = g % NWD
                first = state["first"]
                deps = [state[f"ld{blk}"]] + \
                    [R[f"h{blk * 4 + i}"].rd()[0] for i in range(4)
                     if R[f"h{blk * 4 + i}"].rd()]
                if first:
                    deps += dbres[pw].wd()
                    if pw >= 2 and dc == pw:
                        deps += R["psS1"].wd()
                def mk_dn(e, wb=wb, blk=blk, pw=pw, first=first):
                    last = None
                    for i in range(4):
                        last = e.matmul(dbank[pw],
                                        dslot[wb][:, i * 128:(i + 1) * 128],
                                        h_sb[:, blk * 4 + i, :],
                                        start=(first and i == 0),
                                        stop=(blk == 7 and i == 3),
                                        skip_group_check=True)
                    return last
                mm_tok = P.op("tensor", mk_dn, deps=deps, inc="pe")
                dsres[wb].read(mm_tok)
                if first:
                    dbres[pw].wrote(mm_tok)
                else:
                    dbres[pw].w = mm_tok
                state["first"] = False
                state["mm"] = mm_tok
            def th_epi(dc=dc, state=state, pw=pw, pwn=pwn):
                mm_tok = state["mm"]
                for i in range(32):
                    R[f"h{i}"].read(mm_tok)
                deps = [mm_tok] + R["rhb"].rd() + R["r2"].wd()
                tt1 = P.op("vector", lambda e, pw=pw: e.tensor_tensor(
                    r2t_sb[:, :], dbank[pw], rhb_sb[:, :], ALU.mult),
                    deps=deps, inc="dve")
                dbres[pw].read(tt1)
                R["rhb"].read(tt1)
                R["r2"].wrote(tt1)
                tt2 = P.op("vector", lambda e, dc=dc: e.tensor_tensor(
                    x1_sb[:, dc, :], r2t_sb[:, :], x1_sb[:, dc, :], ALU.add),
                    deps=[tt1] + R["x1"].wd(), inc="dve")
                R["r2"].read(tt2)
                R["x1"].w = tt2
                r2deps[dc] = tt2
            import functools
            lead = 4 if NWD > 2 else 2
            thunks = [functools.partial(th_load, dc, blk, state)
                      for blk in range(8)]
            seq = list(thunks[:lead])
            for blk in range(8):
                seq.append(functools.partial(th_mm, dc, blk, state, pw, pwn))
                if blk + lead < 8:
                    seq.append(thunks[blk + lead])
            seq.append(functools.partial(th_epi, dc, state, pw, pwn))
            work_q.extend(seq)

    def stage2(qt, p):
        ob = qt % 2
        if p == 0:
            wo_led = {}
            def th_wo_load(dc):
                wb = dc % 2
                wn = f"wo{wb}"
                wt_ = P.op("scalar", lambda e, dc=dc, wb=wb: e.dma_start(
                    out=wo_sb[:, wb, :, :].rearrange("p a m -> p (a m)"),
                    in_=wo2[dc, :, :, :].transpose([1, 0, 2])),
                    deps=[hbm[f"wo2_{dc}"]] + R[wn].wd(), inc=f"wo{wb}")
                R[wn].wrote(wt_)
                wo_led[dc] = wt_
            def th_wo_mm(dc, half):
                wb = dc % 2
                wn = f"wo{wb}"
                pw = dc % 2
                if half == 0:
                    xt_ = P.op("sync", lambda e, dc=dc, qt=qt: e.dma_start(
                        out=xo_sb[:, :],
                        in_=xqT[dc * 128:dc * 128 + 128,
                                qt * 512:qt * 512 + 512]),
                        deps=R["xo0"].wd(), inc="xo")
                    R["xo0"].wrote(xt_)
                    wo_led[(dc, "xo")] = xt_
                xt_ = wo_led[(dc, "xo")]
                deps = [wo_led[dc], R["osb"].w]
                if half == 0:
                    deps += psW_res[pw].wd()
                def mk_wo(e, wb=wb, pw=pw, half=half):
                    last = None
                    for j in range(half * 4, half * 4 + 4):
                        last = e.matmul(psW[pw][:, :], wo_sb[:, wb, j, :],
                                        o_sb[:, j, :],
                                        start=(half == 0 and j == 0),
                                        stop=(half == 1 and j == 7),
                                        skip_group_check=True)
                    return last
                mm = P.op("tensor", mk_wo, deps=deps, inc="pe")
                if half == 0:
                    psW_res[pw].wrote(mm)
                    R[wn].read(mm)
                    return
                psW_res[pw].w = mm
                R[wn].read(mm)
                if dc == 7:
                    R["osb"].read(mm)
                    R["osb"].wrote(mm)
                r1 = P.op("vector", lambda e, dc=dc, pw=pw: e.scalar_tensor_tensor(
                    x1_sb[:, dc, :], psW[pw][:, :], bo_sb[:, dc:dc + 1], xo_sb[:, :],
                    ALU.add, ALU.add),
                    deps=[mm, xt_, bo_t] + (R["x1"].wd() if dc == 0 else []),
                    inc="dve")
                psW_res[pw].read(r1)
                R["xo0"].read(r1)
                R["x1"].w = r1
            def th_stat(dc):
                x1w = R["x1"].w
                sq = P.op("vector", lambda e, dc=dc: e.tensor_tensor(
                    h2_sb[:, :], x1_sb[:, dc, :], x1_sb[:, dc, :], ALU.mult),
                    deps=[x1w] + R["h2"].wd(), inc="dve")
                R["h2"].wrote(sq)
                deps = [x1w, sq, c3, c4]
                if dc == 0:
                    deps += psW_res[0].wd() + psW_res[1].wd()
                def mk_st(e, dc=dc):
                    e.matmul(psW[0][0:1, :], ones1c[:, :], x1_sb[:, dc, :],
                             start=(dc == 0), stop=(dc == 7), skip_group_check=True)
                    return e.matmul(psW[1][0:1, :], onesb[:, :], h2_sb[:, :],
                                    start=(dc == 0), stop=(dc == 7),
                                    skip_group_check=True)
                mm = P.op("tensor", mk_st, deps=deps, inc="pe")
                R["h2"].read(mm)
                if dc == 0:
                    psW_res[0].wrote(mm)
                    psW_res[1].wrote(mm)
                else:
                    psW_res[0].w = mm
                    psW_res[1].w = mm
                R["x1"].read(mm)
            ln_state = {}
            def th_ln_a():
                toks = ln_scalars(psW[0][0:1, :], psW[1][0:1, :], psW_res[0].w, DM)
                psW_res[0].read(toks["var"])
                psW_res[1].read(toks["var"])
                ln_state.update(toks)
            def th_ln_b():
                s_tok = rms_scale(ln_state["var"], ln_state["a"], None)
                ln_state["s"] = s_tok
            def th_bcast():
                bca = P.op("tensor", lambda e: e.matmul(
                    psW[0][:, :], ones1f[:, :], vrow(2), start=True, stop=True,
                    skip_group_check=True),
                    deps=[ln_state["a"], c2] + psW_res[0].wd(), inc="pe")
                psW_res[0].wrote(bca)
                R["vs2"].read(bca)
                bcb = P.op("tensor", lambda e: e.matmul(
                    psW[1][:, :], ones1f[:, :], vrow(0), start=True, stop=True,
                    skip_group_check=True),
                    deps=[ln_state["b"]] + psW_res[1].wd(), inc="pe")
                psW_res[1].wrote(bcb)
                R["vs0"].read(bcb)
                ln_state["bca"] = bca
                ln_state["bcb"] = bcb
            def th_apply(dc):
                bca, bcb = ln_state["bca"], ln_state["bcb"]
                a1 = P.op("vector", lambda e, dc=dc: e.tensor_tensor(
                    r2t_sb[:, :], x1_sb[:, dc, :], psW[0][:, :], ALU.mult),
                    deps=[bca] + R["x1"].wd() + R["r2"].wd(), inc="dve")
                psW_res[0].read(a1)
                R["r2"].wrote(a1)
                a2 = P.op("vector", lambda e, dc=dc: e.tensor_tensor(
                    x1_sb[:, dc, :], r2t_sb[:, :], psW[1][:, :], ALU.add),
                    deps=[bcb, a1], inc="dve")
                psW_res[1].read(a2)
                R["r2"].read(a2)
                R["x1"].w = a2
            def th_bcast_s():
                bcs = P.op("tensor", lambda e: e.matmul(
                    psW[0][:, :], ones1f[:, :], vrow(3), start=True, stop=True,
                    skip_group_check=True),
                    deps=[ln_state["s"], R["x1"].w] + psW_res[0].wd(), inc="pe")
                psW_res[0].wrote(bcs)
                R["vs3"].read(bcs)
                ln_state["bcs"] = bcs
            def th_xn(dc):
                xn = P.op("vector", lambda e, dc=dc: e.tensor_tensor(
                    xn_sb[:, dc, :], x1_sb[:, dc, :], psW[0][:, :], ALU.mult),
                    deps=[ln_state["bcs"], R["x1"].w] +
                    (R["xn"].wd() if dc == 0 else []), inc="dve")
                psW_res[0].read(xn)
                R["x1"].read(xn)
                R["xn"].w = xn
            import functools as _ft
            seq = [_ft.partial(th_wo_load, 0), _ft.partial(th_wo_load, 1)]
            for dc in range(8):
                seq.append(_ft.partial(th_wo_mm, dc, 0))
                seq.append(_ft.partial(th_wo_mm, dc, 1))
                if dc + 2 < 8:
                    seq.append(_ft.partial(th_wo_load, dc + 2))
            for dc in range(8):
                seq.append(_ft.partial(th_stat, dc))
            seq += [th_ln_a, th_ln_b, th_bcast]
            for dc in range(8):
                seq.append(_ft.partial(th_apply, dc))
            seq.append(th_bcast_s)
            for dc in range(8):
                seq.append(_ft.partial(th_xn, dc))
            work_q.extend(seq)
        elif p in (1, 2, 3, 4):
            NW = 4 if qt == 3 else 2
            wslot = [wtu_sb[:, 0, :, :].rearrange("p a m -> p (a m)"),
                     wtu_sb[:, 1, :, :].rearrange("p a m -> p (a m)"),
                     wmix[:, 0:1024], wmix[:, 1024:2048]]
            wsres = [R["wtu0"], R["wtu1"], R["tw2"], R["tw3"]]
            def th_up_load(fc):
                wb = fc % NW
                seed = R["wm1"].wd() if (wb >= 2 and fc == wb) else []
                wt_ = P.op("scalar", lambda e, fc=fc, wb=wb: e.dma_start(
                    out=wslot[wb].rearrange("p (a m) -> p a m", a=8),
                    in_=wtu2[fc, :, :, :].transpose([1, 0, 2])),
                    deps=[hbm[f"wtu2_{fc}"]] + wsres[wb].wd() + seed,
                    inc=f"wu{wb}")
                wsres[wb].wrote(wt_)
                wu_led[fc] = wt_
            NB = 4 if qt == 3 else 2
            ubank = [psW[0][:, :], psW[1][:, :],
                     psS[0][:, 0:512], psS[0][:, 512:1024]]
            ubres = [psW_res[0], psW_res[1], R["tu2"], R["tu3"]]
            def th_up_mm(fc, half):
                wb = fc % NW
                pw = fc % NB
                deps = [wu_led[fc], R["xn"].w]
                if half == 0:
                    deps += ubres[pw].wd()
                    if pw >= 2 and fc == pw:
                        deps += R["psS0"].wd()
                def mk_up(e, wb=wb, pw=pw, half=half):
                    last = None
                    for j in range(half * 4, half * 4 + 4):
                        last = e.matmul(ubank[pw],
                                        wslot[wb][:, j * 128:(j + 1) * 128],
                                        xn_sb[:, j, :],
                                        start=(half == 0 and j == 0),
                                        stop=(half == 1 and j == 7),
                                        skip_group_check=True)
                    return last
                mm = P.op("tensor", mk_up, deps=deps, inc="pe")
                if half == 0:
                    ubres[pw].wrote(mm)
                else:
                    ubres[pw].w = mm
                wsres[wb].read(mm)
                R["xn"].read(mm)
            def th_gelu_pair(fc0):
                for fc in (fc0, fc0 + 1):
                    pw = fc % NB
                    g_ = P.op("scalar", lambda e, fc=fc, pw=pw: e.activation(
                        out=h_sb[:, fc, :], in_=ubank[pw], func=AF.Gelu),
                        deps=[ubres[pw].w] + R[f"h{fc}"].wd(), inc="act")
                    ubres[pw].read(g_)
                    R[f"h{fc}"].wrote(g_)
            import functools as _ft
            f0 = (p - 1) * 8
            seq = []
            if p == 1:
                seq += [_ft.partial(th_up_load, i) for i in range(NW)]
            for fc in range(f0, f0 + 8):
                seq.append(_ft.partial(th_up_mm, fc, 0))
                seq.append(_ft.partial(th_up_mm, fc, 1))
                if fc % 2 == 1:
                    seq.append(_ft.partial(th_gelu_pair, fc - 1))
                if fc + NW < 32:
                    seq.append(_ft.partial(th_up_load, fc + NW))
            work_q.extend(seq)
        elif p == 5:
            hsum_state = {"mm": None}
            def th_h2(fc):
                par = fc % 2
                hb = r2t_sb[:, :].bitcast(BF16)[:, 0:512] if par else h2_sb[:, :]
                hn = "r2" if par else "h2"
                sq = P.op("gpsimd", lambda e, fc=fc, hb=hb: e.tensor_tensor(
                    hb, h_sb[:, fc, :], h_sb[:, fc, :], ALU.mult),
                    deps=R[f"h{fc}"].rd() + R[hn].wd(), inc="pl")
                R[hn].wrote(sq)
                deps = [sq, c3]
                if fc == 0:
                    deps += psW_res[0].wd()
                mm = P.op("tensor", lambda e, fc=fc, hb=hb: e.matmul(
                    psW[0][0:1, :], onesb[:, :], hb, start=(fc == 0),
                    stop=(fc == 31), skip_group_check=True), deps=deps, inc="pe")
                R[hn].read(mm)
                if fc == 0:
                    psW_res[0].wrote(mm)
                else:
                    psW_res[0].w = mm
                hsum_state["mm"] = mm
            def th_rmsh():
                hmm = hsum_state["mm"]
                u0 = P.op("vector", lambda e: e.tensor_scalar(
                    vrow(3), psW[0][0:1, :], 1.0 / FF, None, ALU.mult),
                    deps=[hmm] + R["vs3"].wd(), inc="dve")
                psW_res[0].read(u0)
                R["vs3"].wrote(u0)
                u = P.op("vector", lambda e: e.tensor_scalar_max(
                    vrow(3), vrow(3), 1e-30), deps=[u0], inc="dve")
                R["vs3"].wrote(u)
                l1 = P.op("scalar", lambda e: e.activation(
                    out=vrow(3), in_=vrow(3), func=AF.Ln), deps=[u], inc="act")
                R["vs3"].wrote(l1)
                e1 = P.op("scalar", lambda e: e.activation(
                    out=vrow(3), in_=vrow(3), func=AF.Exp, scale=0.5),
                    deps=[l1], inc="act")
                R["vs3"].wrote(e1)
                a1 = P.op("vector", lambda e: e.tensor_scalar(
                    vrow(3), vrow(3), EPS_RMS, 2.0, ALU.add, ALU.mult),
                    deps=[e1], inc="dve")
                R["vs3"].wrote(a1)
                rc0 = P.op("scalar", lambda e: e.activation(
                    out=vrow(3), in_=vrow(3), func=AF.Ln), deps=[a1], inc="act")
                R["vs3"].wrote(rc0)
                sc = P.op("scalar", lambda e: e.activation(
                    out=vrow(3), in_=vrow(3), func=AF.Exp, scale=-1.0),
                    deps=[rc0], inc="act")
                R["vs3"].wrote(sc)
                bc = P.op("tensor", lambda e: e.matmul(
                    psW[1][:, :], ones1f[:, :], vrow(3), start=True, stop=True,
                    skip_group_check=True), deps=[sc, c2] + psW_res[1].wd(),
                    inc="pe")
                psW_res[1].wrote(bc)
                R["vs3"].read(bc)
                cp = P.op("vector", lambda e: e.tensor_copy(rhb_sb[:, :],
                                                            psW[1][:, :]),
                          deps=[bc] + R["rhb"].wd(), inc="dve")
                psW_res[1].read(cp)
                R["rhb"].wrote(cp)
            import functools as _ft
            seq = [_ft.partial(th_h2, fc) for fc in range(32)]
            seq.append(th_rmsh)
            work_q.extend(seq)
            down_chunk(qt, [0, 1])
        elif p == 6:
            down_chunk(qt, [2, 3, 4])
        elif p == 7:
            ln2 = {}
            if qt == 3:
                sA, sB = psO[0:1, :], psR[0:1, :]
                resA, resB = R["psO"], R["psR"]
            else:
                sA, sB = psW[0][0:1, :], psW[1][0:1, :]
                resA, resB = psW_res[0], psW_res[1]
            def th_stat2(dc):
                sq = P.op("vector", lambda e, dc=dc: e.tensor_tensor(
                    h2_sb[:, :], x1_sb[:, dc, :], x1_sb[:, dc, :], ALU.mult),
                    deps=[r2deps[dc]] + R["h2"].wd(), inc="dve")
                R["h2"].wrote(sq)
                deps = [r2deps[dc], sq, c3, c4]
                if dc == 0:
                    deps += resA.wd() + resB.wd()
                def mk_st(e, dc=dc):
                    e.matmul(sA, ones1c[:, :], x1_sb[:, dc, :],
                             start=(dc == 0), stop=(dc == 7), skip_group_check=True)
                    return e.matmul(sB, onesb[:, :], h2_sb[:, :],
                                    start=(dc == 0), stop=(dc == 7),
                                    skip_group_check=True)
                mm = P.op("tensor", mk_st, deps=deps, inc="pe")
                R["h2"].read(mm)
                if dc == 0:
                    resA.wrote(mm)
                    resB.wrote(mm)
                else:
                    resA.w = mm
                    resB.w = mm
                R["x1"].read(mm)
            def th_ln2():
                toks = ln_scalars(sA, sB, resA.w, DM)
                resA.read(toks["var"])
                resB.read(toks["var"])
                bca = P.op("tensor", lambda e: e.matmul(
                    psW[0][:, :], ones1f[:, :], vrow(2), start=True, stop=True,
                    skip_group_check=True),
                    deps=[toks["a"], c2] + psW_res[0].wd(), inc="pe")
                psW_res[0].wrote(bca)
                R["vs2"].read(bca)
                bcb = P.op("tensor", lambda e: e.matmul(
                    psW[1][:, :], ones1f[:, :], vrow(0), start=True, stop=True,
                    skip_group_check=True),
                    deps=[toks["b"]] + psW_res[1].wd(), inc="pe")
                psW_res[1].wrote(bcb)
                R["vs0"].read(bcb)
                ln2["bca"] = bca
                ln2["bcb"] = bcb
            def th_out(dc):
                bca, bcb = ln2["bca"], ln2["bcb"]
                obuf = dc % 2 if qt == 3 else 0
                ob = ot_sb[:, 0, :] if obuf == 0 else rhb_sb[:, :]
                onn = f"ot{obuf}" if obuf == 0 else "rhb"
                a1 = P.op("vector", lambda e, dc=dc, ob=ob: e.tensor_tensor(
                    ob, x1_sb[:, dc, :], psW[0][:, :], ALU.mult),
                    deps=[bca, r2deps[dc]] + R[onn].wd(), inc="dve")
                psW_res[0].read(a1)
                R["x1"].read(a1)
                a2 = P.op("vector", lambda e, dc=dc, ob=ob: e.tensor_tensor(
                    ob, ob, psW[1][:, :], ALU.add),
                    deps=[bcb, a1], inc="dve")
                psW_res[1].read(a2)
                R[onn].wrote(a2)
                od = P.op("sync", lambda e, dc=dc, qt=qt, ob=ob: e.dma_start(
                    out=out[dc * 128:dc * 128 + 128, qt * 512:qt * 512 + 512],
                    in_=ob), deps=[a2], inc=f"ot{obuf}")
                R[onn].read(od)
                out_toks.append(od)
            import functools as _ft
            if qt == 3:
                down_chunk(qt, [5])
                work_q.extend([_ft.partial(th_stat2, 0),
                               _ft.partial(th_stat2, 1)])
                down_chunk(qt, [6])
                work_q.extend([_ft.partial(th_stat2, 2),
                               _ft.partial(th_stat2, 3)])
                down_chunk(qt, [7])
                seq = [_ft.partial(th_stat2, dc) for dc in range(4, 8)]
            else:
                down_chunk(qt, [5, 6, 7])
                seq = [_ft.partial(th_stat2, dc) for dc in range(8)]
            seq.append(th_ln2)
            seq += [_ft.partial(th_out, dc) for dc in range(8)]
            work_q.extend(seq)

    work_q.extend(tern_thunks)
    tern_thunks = []
    for qt in range(NQT + 1):
        for pr in range(8):
            if qt >= 1 and pr == 0:
                while epi_q:
                    epi_q.pop(0)()
                for p_ in range(8):
                    stage2(qt - 1, p_)
            if qt < NQT:
                attention(qt, pr)
        if qt == NQT:
            while work_q:
                work_q.popleft()()

    P.op("sync", lambda e: None, deps=[out_toks[-1]])

    sems = {}
    semctx = ExitStack()
    for s in ("pe", "act", "dve", "pl") + DMASEMS:
        sems[s] = semctx.enter_context(nc.semaphore(s))
    with nc.Block() as block:
        @block.sync
        def _(eng):
            P.replay("sync", eng, sems)

        @block.tensor
        def _(eng):
            P.replay("tensor", eng, sems)

        @block.scalar
        def _(eng):
            P.replay("scalar", eng, sems)

        @block.vector
        def _(eng):
            P.replay("vector", eng, sems)

        @block.gpsimd
        def _(eng):
            P.replay("gpsimd", eng, sems)
    semctx.close()
    ctx.close()
    return nc


_NC_CACHE = {}


def kernel(L0_attn_x, L0_attn_Wq, L0_attn_Wk, L0_attn_Wv, L0_attn_Wo,
           L0_attn_bo, L0_ffn_w_up, L0_ffn_w_down):
    x = np.ascontiguousarray(np.asarray(L0_attn_x, dtype=np.float32))
    Wq = np.asarray(L0_attn_Wq, dtype=np.float32)
    Wk = np.asarray(L0_attn_Wk, dtype=np.float32)
    Wv = np.asarray(L0_attn_Wv, dtype=np.float32)
    Wo = np.asarray(L0_attn_Wo, dtype=np.float32)
    bov = np.asarray(L0_attn_bo, dtype=np.float32)
    wu = np.asarray(L0_ffn_w_up, dtype=np.float32)
    wd = np.asarray(L0_ffn_w_down, dtype=np.float32)

    perm = PERM_ROWS
    wqT_t = np.ascontiguousarray(
        Wq[perm].T.reshape(8, 128, 8, 128).transpose(2, 0, 1, 3))
    woT_t = np.ascontiguousarray(
        Wo.T[perm].reshape(8, 128, 8, 128).transpose(2, 0, 1, 3))
    wkT = np.ascontiguousarray(Wk.T)
    wvT = np.ascontiguousarray(Wv.T)
    wupT_t = np.ascontiguousarray(
        wu.T.reshape(8, 128, 32, 128).transpose(2, 0, 1, 3))
    wdnT_t = np.ascontiguousarray(
        wd.T.reshape(32, 128, 8, 128).transpose(2, 0, 1, 3))
    bo2 = bov.reshape(1, DM)
    slopes = (-(2.0 ** (-(np.arange(16, dtype=np.float64) + 1))) / 8.0)\
        .astype(np.float32).reshape(16, 1)

    j = np.arange(L, dtype=np.float64)
    dist_h = [np.ascontiguousarray(
        np.broadcast_to(np.abs(j - 4095.0 + half * 2048.0), (16, L))
        .astype(np.float32)) for half in (0, 1)]

    if "nc" not in _NC_CACHE:
        _NC_CACHE["nc"] = build_nc()
    nc = _NC_CACHE["nc"]

    in_maps = []
    for c in range(8):
        b, half = c // 2, c % 2
        xT_c = np.ascontiguousarray(x[b].T)
        xqT_c = np.ascontiguousarray(x[b].T[:, half * TOK:(half + 1) * TOK])
        in_maps.append({
            "xT": xT_c, "xqT": xqT_c, "wqT": wqT_t, "wkT": wkT, "wvT": wvT,
            "woT": woT_t, "bo": bo2, "wup": wupT_t, "wdn": wdnT_t,
            "dist": dist_h[half], "slop": slopes,
        })
    res = run_bass_kernel_spmd(nc, in_maps, core_ids=list(range(8)))
    y = np.empty((4, S, DM), dtype=np.float32)
    for c in range(8):
        b, half = c // 2, c % 2
        y[b, half * TOK:(half + 1) * TOK, :] = res.results[c]["out"].T
    return y


# version 42
# speedup vs baseline: 1.0323x; 1.0110x over previous
import numpy as np
import concourse.bass as bass
import concourse.mybir as mybir
from concourse.bass_utils import run_bass_kernel_spmd
from concourse.ap import AP
from contextlib import ExitStack

F32 = mybir.dt.float32
BF16 = mybir.dt.bfloat16
AF = mybir.ActivationFunctionType
ALU = mybir.AluOpType
AX = mybir.AxisListType

DM = 1024
S = 4096
HD = 64
FF = 4096
TOK = 2048
NQT = 4
QT = 512
NKC = 32
L = 6144
WPC = 2432
EPS_LN = 1e-5
EPS_RMS = 1e-8
SHIFT = 20.0

PAIRS = [(4 * (j // 2) + (j % 2), 4 * (j // 2) + (j % 2) + 2) for j in range(8)]
PERM_ROWS = np.concatenate(
    [np.arange(h * HD, (h + 1) * HD) for j in range(8) for h in PAIRS[j]])

ENGS = ("sync", "tensor", "scalar", "vector", "gpsimd")


class Prog:
    def __init__(self):
        self.ops = {e: [] for e in ENGS}
        self.cnt = {}
        self.step = {}
        self.marks = {e: {} for e in ENGS}

    def sem(self, name, step=1):
        self.cnt[name] = 0
        self.step[name] = step

    def op(self, eng, fn, deps=(), inc=None):
        deps = list(deps)
        if inc is not None and self.step[inc] == 16:
            if self.cnt[inc] > 0:
                deps.append((inc, self.cnt[inc]))
        waits = []
        for d in deps:
            if d is None:
                continue
            s, v = d
            if v <= 0 or self.marks[eng].get(s, 0) >= v:
                continue
            self.marks[eng][s] = v
            waits.append((s, v))
        tok = None
        if inc is not None:
            self.cnt[inc] += self.step[inc]
            tok = (inc, self.cnt[inc])
        self.ops[eng].append((waits, fn, inc))
        return tok

    def barrier(self):
        toks = [(s, self.cnt[s]) for s in self.cnt]
        for e in ENGS:
            self.op(e, lambda eng: None, deps=toks)

    def replay(self, name, eng, sems):
        for waits, fn, inc in self.ops[name]:
            for (s, v) in waits:
                eng.wait_ge(sems[s], v)
            ins = fn(eng)
            if inc is not None:
                if ins is None:
                    eng.sem_inc(sems[inc], self.step[inc])
                else:
                    ins.then_inc(sems[inc], self.step[inc])


class Res:

    def __init__(self):
        self.w = None
        self.r = []

    def wd(self):
        return ([self.w] if self.w else []) + list(self.r)

    def rd(self):
        return [self.w] if self.w else []

    def wrote(self, tok):
        self.w = tok
        self.r = []

    def read(self, tok):
        if tok is None:
            return
        best = {}
        for s, v in self.r + [tok]:
            best[s] = max(best.get(s, 0), v)
        self.r = list(best.items())


def build_nc():
    nc = bass.Bass("TRN2", target_bir_lowering=False)

    xT = nc.dram_tensor("xT", [DM, S], F32, kind="ExternalInput").ap()
    xqT = nc.dram_tensor("xqT", [DM, TOK], F32, kind="ExternalInput").ap()
    wqT = nc.dram_tensor("wqT", [8, 8, 128, 128], F32, kind="ExternalInput").ap()
    wkT = nc.dram_tensor("wkT", [DM, 512], F32, kind="ExternalInput").ap()
    wvT = nc.dram_tensor("wvT", [DM, 512], F32, kind="ExternalInput").ap()
    woT = nc.dram_tensor("woT", [8, 8, 128, 128], F32, kind="ExternalInput").ap()
    bo = nc.dram_tensor("bo", [1, DM], F32, kind="ExternalInput").ap()
    wup = nc.dram_tensor("wup", [32, 8, 128, 128], F32, kind="ExternalInput").ap()
    wdn = nc.dram_tensor("wdn", [8, 32, 128, 128], F32, kind="ExternalInput").ap()
    dist = nc.dram_tensor("dist", [16, L], F32, kind="ExternalInput").ap()
    slop = nc.dram_tensor("slop", [16, 1], F32, kind="ExternalInput").ap()
    out = nc.dram_tensor("out", [DM, TOK], F32, kind="ExternalOutput").ap()

    erow_h = nc.dram_tensor("erow_h", [16, L], BF16, kind="Internal").ap()
    erep = nc.dram_tensor("erep", [16, 128, L], BF16, kind="Internal").ap()
    q_hbm = nc.dram_tensor("q_hbm", [NQT, 8, 128, QT], BF16, kind="Internal").ap()
    wo2 = nc.dram_tensor("wo2", [8, 8, 128, 128], BF16, kind="Internal").ap()
    wtu2 = nc.dram_tensor("wtu2", [32, 8, 128, 128], BF16, kind="Internal").ap()
    wtd2 = nc.dram_tensor("wtd2", [8, 32, 128, 128], BF16, kind="Internal").ap()
    scal_h = nc.dram_tensor("scal_h", [1, 8], F32, kind="Internal").ap()

    P = Prog()
    DMASEMS = ("xq0", "xq1", "xf0", "xf1", "xf2", "xf3", "rep", "scal", "wqs0", "wqs1",
               "wk", "wv", "qs0", "qs1", "wts0", "wts1", "w2s", "tl0", "tl1",
               "ew0", "ew1", "ql0", "ql1", "xo", "xo1", "wo0", "wo1", "wu0", "wu1", "wu2", "wu3",
               "wd0", "wd1", "ot0", "ot1", "t2s", "rep2")
    for s in ("pe", "act", "dve", "pl"):
        P.sem(s)
    for s in DMASEMS:
        P.sem(s, 16)

    ctx = ExitStack()
    sbt = lambda n, sh, dt: ctx.enter_context(nc.sbuf_tensor(n, sh, dt))
    pst = lambda n, sh: ctx.enter_context(nc.psum_tensor(n, sh, F32))

    k2_sb = sbt("k2", [128, 4 * S], BF16)
    v_sb = sbt("v", [128, NKC * 512], BF16)
    vec = sbt("vec", [128, 4 * QT], F32)
    ones64 = sbt("ones64", [128, 64], BF16)
    ones1f = sbt("ones1f", [1, 128], F32)
    onesb = sbt("onesb", [128, 1], BF16)
    ones1c = sbt("ones1c", [128, 1], F32)
    bo_sb = sbt("bosb", [128, 8], F32)
    bias_sb = sbt("biasc", [128, 1], F32)
    tauu_sb = sbt("tauu", [128, 2], F32)
    taud_sb = sbt("taud", [128, 2], F32)
    slop_sb = sbt("slopsb", [16, 1], F32)

    psS = [pst("psS0", [128, 1024]), pst("psS1", [128, 1024])]
    psO = pst("psO", [128, QT])
    psR = pst("psR", [128, QT])
    psW = [pst("psW0", [128, QT]), pst("psW1", [128, QT])]

    R = {}
    for n in ("psS0", "psS1", "psO", "psR", "psW0", "psW1", "eo0", "eo1",
              "pt0", "pt1", "pt2", "ew0", "ew1", "q0", "q1", "osb", "xo0", "xo1",
              "x1", "xn", "h2", "rhb", "r2", "ot0", "ot1", "wo0", "wo1",
              "wtu0", "wtu1", "wtd0", "wtd1", "rr", "bias", "vs0", "vs1",
              "vs2", "vs3", "wm1", "wm2", "wt2t", "wts0", "wts1", "acc",
              "tauu", "taud", "nrm0", "nrm1", "nrm2", "h2p", "accr",
              "tu2", "tu3", "td2", "td3", "tw2", "tw3", "ts2", "ts3",
              "ts4", "ts5"):
        R[n] = Res()
    for fc in range(32):
        R[f"h{fc}"] = Res()
    hbm = {}

    vrow = lambda k: vec[0:1, k * QT:(k + 1) * QT]

    slop_t = P.op("sync", lambda e: e.dma_start(out=slop_sb[:, :], in_=slop[:, :]),
                  inc="scal")
    bo_t = P.op("sync", lambda e: e.dma_start(
        out=bo_sb[:, :], in_=AP(bo.tensor, 0, [[1, 128], [128, 8]]),
        allow_slow_non_contiguous=True), inc="scal")
    R["bias"].wrote(bo_t)
    c1 = P.op("gpsimd", lambda e: e.memset(ones64[:, :], 1.0), inc="pl")
    c2 = P.op("gpsimd", lambda e: e.memset(ones1f[:, :], 1.0), inc="pl")
    c3 = P.op("gpsimd", lambda e: e.memset(onesb[:, :], 1.0), inc="pl")
    c4 = P.op("gpsimd", lambda e: e.memset(ones1c[:, :], 1.0), inc="pl")
    ones_t = [c1, c2, c3, c4]

    ctxQ = ExitStack()
    dst = ctxQ.enter_context(nc.sbuf_tensor("dst", [16, L], F32))
    ero = ctxQ.enter_context(nc.sbuf_tensor("ero", [16, L], BF16))
    xsq = ctxQ.enter_context(nc.sbuf_tensor("xsq", [128, 2, 512], F32))
    xsf = ctxQ.enter_context(nc.sbuf_tensor("xsf", [128, 4, 512], F32))
    xbq = ctxQ.enter_context(nc.sbuf_tensor("xbq", [128, 2, 8, 512], BF16))
    xbf = ctxQ.enter_context(nc.sbuf_tensor("xbf", [128, 2, 8, 512], BF16))
    wqs = ctxQ.enter_context(nc.sbuf_tensor("wqs", [128, 2, 1024], F32))
    wqb = ctxQ.enter_context(nc.sbuf_tensor("wqb", [128, 8, 8, 128], BF16))
    qev = ctxQ.enter_context(nc.sbuf_tensor("qev", [128, 2, QT], BF16))
    qsq = ctxQ.enter_context(nc.sbuf_tensor("qsq", [128, QT], BF16))
    wks = ctxQ.enter_context(nc.sbuf_tensor("wks", [128, 4, 512], F32))
    wkb = ctxQ.enter_context(nc.sbuf_tensor("wkb", [128, 8, 512], BF16))
    wvb = ctxQ.enter_context(nc.sbuf_tensor("wvb", [128, 8, 512], BF16))
    ksq = ctxQ.enter_context(nc.sbuf_tensor("ksq", [128, QT], BF16))

    dst_t = P.op("sync", lambda e: e.dma_start(out=dst[:, :], in_=dist[:, :]),
                 inc="rep")

    wqb_res = Res()
    wq_ld = []
    for j in range(2):
        wq_ld.append(P.op("scalar", lambda e, j=j: e.dma_start(
            out=wqs[:, j, :].rearrange("p (a m) -> p a m", a=8),
            in_=wqT[j, :, :, :].transpose([1, 0, 2])), inc=f"wqs{j}"))

    ero_t = P.op("scalar", lambda e: e.activation(
        out=ero[:, :], in_=dst[:, :], func=AF.Exp, bias=0.0,
        scale=slop_sb[:, 0:1]), deps=[dst_t, slop_t], inc="act")
    erow_t = P.op("scalar", lambda e: e.dma_start(out=erow_h[:, :], in_=ero[:, :]),
                  deps=[ero_t], inc="rep")
    for h in [hh for j in range(8) for hh in PAIRS[j]]:
        hbm[f"erep{h}"] = P.op("gpsimd", lambda e, h=h: e.dma_start(
            out=erep[h, :, :], in_=AP(erow_h.tensor, h * L, [[0, 128], [1, L]])),
            deps=[erow_t], inc="rep2")

    xsq_res = [Res(), Res()]
    xsf_res = [Res(), Res(), Res(), Res()]
    xbq_res = [Res(), Res()]
    xbf_res = [Res(), Res()]
    xq_cv = {}
    xf_cv = {}

    def load_xq_slab(qt):
        xq_cv[qt] = []
        for c in range(8):
            sb = c % 2
            ld = P.op("sync", lambda e, c=c, qt=qt, sb=sb: e.dma_start(
                out=xsq[:, sb, :],
                in_=xqT[c * 128:(c + 1) * 128, qt * 512:(qt + 1) * 512]),
                deps=xsq_res[sb].wd(), inc=f"xq{sb}")
            xsq_res[sb].wrote(ld)
            cv = P.op("vector", lambda e, c=c, qt=qt, sb=sb: e.tensor_copy(
                xbq[:, qt % 2, c, :], xsq[:, sb, :]),
                deps=[ld] + xbq_res[qt % 2].wd(), inc="dve")
            xsq_res[sb].read(cv)
            xbq_res[qt % 2].wrote(cv)
            xq_cv[qt].append(cv)

    def load_xf_slab(tb):
        xf_cv[tb] = []
        for c in range(8):
            sb = c % 4
            ld = P.op("sync", lambda e, c=c, tb=tb, sb=sb: e.dma_start(
                out=xsf[:, sb, :],
                in_=xT[c * 128:(c + 1) * 128, tb * 512:(tb + 1) * 512]),
                deps=xsf_res[sb].wd(), inc=f"xf{sb}")
            xsf_res[sb].wrote(ld)
            cv = P.op("vector", lambda e, c=c, tb=tb, sb=sb: e.tensor_copy(
                xbf[:, tb % 2, c, :], xsf[:, sb, :]),
                deps=[ld] + xbf_res[tb % 2].wd(), inc="dve")
            xsf_res[sb].read(cv)
            xbf_res[tb % 2].wrote(cv)
            xf_cv[tb].append(cv)

    load_xq_slab(0)
    load_xf_slab(0)

    wk_cv, wv_cv = [], []
    for c in range(8):
        l1 = P.op("scalar", lambda e, c=c: e.dma_start(
            out=wks[:, c % 2, :], in_=wkT[c * 128:(c + 1) * 128, :]),
            deps=[wk_cv[c - 2]] if c >= 2 else [], inc="wk")
        k1 = P.op("vector", lambda e, c=c: e.tensor_copy(wkb[:, c, :], wks[:, c % 2, :]),
                  deps=[l1], inc="dve")
        wk_cv.append(k1)
        l2 = P.op("scalar", lambda e, c=c: e.dma_start(
            out=wks[:, 2 + c % 2, :], in_=wvT[c * 128:(c + 1) * 128, :]),
            deps=[wv_cv[c - 2]] if c >= 2 else [], inc="wv")
        k2t_ = P.op("vector", lambda e, c=c: e.tensor_copy(wvb[:, c, :], wks[:, 2 + c % 2, :]),
                    deps=[l2], inc="dve")
        wv_cv.append(k2t_)

    wq_cv = []
    for j in range(8):
        if j >= 2:
            wq_ld.append(P.op("scalar", lambda e, j=j: e.dma_start(
                out=wqs[:, j % 2, :].rearrange("p (a m) -> p a m", a=8),
                in_=wqT[j, :, :, :].transpose([1, 0, 2])),
                deps=[wq_cv[j - 2]], inc=f"wqs{j % 2}"))
        cv = P.op("vector", lambda e, j=j: e.tensor_copy(
            wqb[:, j, :, :].rearrange("p a m -> p (a m)"), wqs[:, j % 2, :]),
            deps=[wq_ld[j]], inc="dve")
        wq_cv.append(cv)

    load_xq_slab(1)
    load_xf_slab(1)

    qev_res = [Res(), Res()]
    qsq_res = Res()
    ksq_res = Res()
    psW_res = [R["psW0"], R["psW1"]]
    k2_ev = {}
    v_ev = []
    gq = [0]
    nrm_i = [0]
    qn_i = [0]
    kn_i = [0]

    def emit_q(qt):
        for j in range(8):
            pw = gq[0] % 2
            gq[0] += 1
            deps = [wq_cv[j]] + xq_cv[qt] + psW_res[pw].wd()

            def mk_q(e, j=j, qt=qt, pw=pw):
                last = None
                for k in range(8):
                    last = e.matmul(psW[pw][:, :], wqb[:, j, k, :],
                                    xbq[:, qt % 2, k, :],
                                    start=(k == 0), stop=(k == 7),
                                    skip_group_check=True)
                return last
            mm = P.op("tensor", mk_q, deps=deps, inc="pe")
            psW_res[pw].wrote(mm)
            xbq_res[qt % 2].read(mm)
            ev = P.op("vector", lambda e, pw=pw: e.tensor_scalar(
                qev[:, pw, :], psW[pw][:, :], 0.125, None, ALU.mult),
                deps=[mm] + qev_res[pw].wd(), inc="dve")
            psW_res[pw].read(ev)
            qev_res[pw].wrote(ev)
            st = P.op("scalar", lambda e, j=j, qt=qt, pw=pw: e.dma_start(
                out=q_hbm[qt, j, :, :], in_=qev[:, pw, :]), deps=[ev],
                inc=f"qs{pw}")
            hbm[f"q{qt}_{j}"] = st
            qev_res[pw].read(st)
            sq = P.op("scalar", lambda e, pw=pw: e.activation(
                out=qsq[:, :], in_=qev[:, pw, :], func=AF.Square),
                deps=[ev] + qsq_res.wd(), inc="act")
            qev_res[pw].read(sq)
            qsq_res.wrote(sq)
            row = 32 * (nrm_i[0] % 3)
            rown = f"nrm{nrm_i[0] % 3}"
            nrm_i[0] += 1
            qn = qn_i[0]
            qn_i[0] += 1
            nm = P.op("tensor", lambda e, row=row: e.matmul(
                psR[row:row + 1, :], onesb[:, :], qsq[:, :], start=True,
                stop=True, skip_group_check=True),
                deps=[sq, c3] + R[rown].wd(), inc="pe")
            qsq_res.read(nm)
            R[rown].wrote(nm)
            rd = P.op("vector", lambda e, row=row, qn=qn: e.reduce_max(
                vec[0:1, 16 + qn:17 + qn], psR[row:row + 1, :], axis=AX.X),
                deps=[nm] + (R["vs0"].wd() if qn == 0 else []), inc="dve")
            R[rown].read(rd)
            R["vs0"].w = rd

    def emit_knorm():
        kn = kn_i[0]
        if kn >= 32:
            return
        tt, tb = kn % 4, kn // 4
        if (tt, tb) not in k2_ev:
            return
        kn_i[0] += 1
        sq = P.op("scalar", lambda e, tt=tt, tb=tb: e.activation(
            out=ksq[:, :], in_=k2_sb[:, tt * S + tb * 512:tt * S + tb * 512 + 512],
            func=AF.Square), deps=[k2_ev[(tt, tb)]] + ksq_res.wd(), inc="act")
        ksq_res.wrote(sq)
        row = 32 * (nrm_i[0] % 3)
        rown = f"nrm{nrm_i[0] % 3}"
        nrm_i[0] += 1
        nm = P.op("tensor", lambda e, row=row: e.matmul(
            psR[row:row + 1, :], onesb[:, :], ksq[:, :], start=True,
            stop=True, skip_group_check=True),
            deps=[sq, c3] + R[rown].wd(), inc="pe")
        ksq_res.read(nm)
        R[rown].wrote(nm)
        rd = P.op("vector", lambda e, row=row, kn=kn: e.reduce_max(
            vec[0:1, 48 + kn:49 + kn], psR[row:row + 1, :], axis=AX.X),
            deps=[nm], inc="dve")
        R[rown].read(rd)
        R["vs0"].w = rd

    def emit_kv(tb):
        for tt in range(4):
            pw = gq[0] % 2
            gq[0] += 1
            deps = wk_cv + xf_cv[tb] + psW_res[pw].wd()

            def mk_k(e, tt=tt, tb=tb, pw=pw):
                last = None
                for k in range(8):
                    last = e.matmul(psW[pw][:, :],
                                    wkb[:, k, tt * 128:tt * 128 + 128],
                                    xbf[:, tb % 2, k, :],
                                    start=(k == 0), stop=(k == 7),
                                    skip_group_check=True)
                return last
            mm = P.op("tensor", mk_k, deps=deps, inc="pe")
            psW_res[pw].wrote(mm)
            xbf_res[tb % 2].read(mm)
            ev = P.op("vector", lambda e, tt=tt, tb=tb, pw=pw: e.tensor_copy(
                k2_sb[:, tt * S + tb * 512:tt * S + (tb + 1) * 512],
                psW[pw][:, :]), deps=[mm], inc="dve")
            psW_res[pw].read(ev)
            k2_ev[(tt, tb)] = ev
        for kcl in range(4):
            kc = tb * 4 + kcl
            pw = gq[0] % 2
            gq[0] += 1
            deps = wv_cv + xf_cv[tb] + psW_res[pw].wd()

            def mk_v(e, kcl=kcl, tb=tb, pw=pw):
                last = None
                for k in range(8):
                    last = e.matmul(psW[pw][:, :],
                                    xbf[:, tb % 2, k, kcl * 128:(kcl + 1) * 128],
                                    wvb[:, k, :], start=(k == 0), stop=(k == 7),
                                    skip_group_check=True)
                return last
            mm = P.op("tensor", mk_v, deps=deps, inc="pe")
            psW_res[pw].wrote(mm)
            xbf_res[tb % 2].read(mm)
            ev = P.op("vector", lambda e, kc=kc, pw=pw: e.tensor_copy(
                v_sb[:, kc * 512:(kc + 1) * 512], psW[pw][:, :]), deps=[mm],
                inc="dve")
            psW_res[pw].read(ev)
            v_ev.append(ev)
            emit_knorm()

    for stp in range(8):
        emit_kv(stp)
        if stp < 4:
            emit_q(stp)
            if stp + 2 < 4:
                load_xq_slab(stp + 2)
        if stp + 2 < 8:
            load_xf_slab(stp + 2)
    while kn_i[0] < 32:
        emit_knorm()
    k2_t = k2_ev[(3, 7)]
    v_t = v_ev[-1]

    ctxQ.close()
    P.barrier()

    nrm = R["vs0"].w
    r1_ = P.op("vector", lambda e: e.reduce_max(
        vec[0:1, 0:1], vec[0:1, 16:48], axis=AX.X), deps=[nrm], inc="dve")
    r2_ = P.op("vector", lambda e: e.reduce_max(
        vec[0:1, 5:6], vec[0:1, 48:80], axis=AX.X), deps=[nrm], inc="dve")
    R["vs0"].w = r2_
    m1_ = P.op("vector", lambda e: e.tensor_tensor(
        vec[0:1, 1:2], vec[0:1, 0:1], vec[0:1, 5:6], ALU.mult),
        deps=[r1_, r2_], inc="dve")
    l1_ = P.op("scalar", lambda e: e.activation(
        out=vec[0:1, 2:3], in_=vec[0:1, 1:2], func=AF.Ln), deps=[m1_], inc="act")
    e1_ = P.op("scalar", lambda e: e.activation(
        out=vec[0:1, 3:4], in_=vec[0:1, 2:3], func=AF.Exp, scale=0.5),
        deps=[l1_], inc="act")
    n1_ = P.op("vector", lambda e: e.tensor_scalar(
        vec[0:1, 4:5], vec[0:1, 3:4], -1.0, SHIFT, ALU.mult, ALU.add),
        deps=[e1_], inc="dve")
    s1_ = P.op("sync", lambda e: e.dma_start(
        out=AP(scal_h.tensor, 4, [[1, 1], [1, 1]]), in_=vec[0:1, 4:5]),
        deps=[n1_], inc="scal")
    bias_t = P.op("sync", lambda e: e.dma_start(
        out=bias_sb[:, :], in_=AP(scal_h.tensor, 4, [[0, 128], [1, 1]])),
        deps=[s1_], inc="scal")
    R["bias"].wrote(bias_t)
    for _tk in (m1_, l1_, e1_, n1_, s1_):
        R["vs0"].read(_tk)

    ewin = sbt("ewin", [128, 2, 2, WPC], BF16)
    eo_sb = sbt("eo", [128, 2, 1024], BF16)
    pt_sb = sbt("pt", [128, 3, 1024], BF16)
    rr_sb = sbt("rr", [128, QT], F32)
    o_sb = sbt("osb", [128, 8, QT], BF16)
    q_sb = sbt("qsb", [128, 2, QT], BF16)
    xo_sb = sbt("xo", [128, QT], F32)
    x1_sb = sbt("x1", [128, 8, QT], F32)
    xn_sb = sbt("xn", [128, 8, QT], BF16)
    h_sb = sbt("h", [128, 32, QT], BF16)
    h2_sb = sbt("h2", [128, QT], BF16)
    rhb_sb = sbt("rhb", [128, QT], F32)
    r2t_sb = sbt("r2t", [128, QT], F32)
    ot_sb = sbt("ot", [128, 1, QT], F32)
    wo_sb = sbt("wosb", [128, 2, 8, 128], BF16)
    wtu_sb = sbt("wtusb", [128, 2, 8, 128], BF16)
    wtd_sb = sbt("wtdsb", [128, 2, 4, 128], BF16)

    wts = sbt("wts", [128, 2, 1024], F32)
    wmix = sbt("wmix", [128, 2048], BF16)
    acca = sbt("acca", [128, 64], F32)
    accr = sbt("accr", [128, 32], F32)
    wt2t = sbt("wt2t", [128, 1024], BF16)

    wts_res = [R["wts0"], R["wts1"]]
    for dc in range(8):
        b = dc % 2
        l1 = P.op("scalar", lambda e, dc=dc, b=b: e.dma_start(
            out=wts[:, b, :].rearrange("p (a m) -> p a m", a=8), in_=woT[dc, :, :, :].transpose([1, 0, 2])),
            deps=wts_res[b].wd(), inc=f"wts{b}")
        wts_res[b].wrote(l1)
        cv = P.op("vector", lambda e, b=b: e.tensor_copy(wt2t[:, :], wts[:, b, :]),
                  deps=[l1] + R["wt2t"].wd(), inc="dve")
        wts_res[b].read(cv)
        R["wt2t"].wrote(cv)
        st = P.op("scalar", lambda e, dc=dc: e.dma_start(
            out=wo2[dc, :, :, :].transpose([1, 0, 2]),
            in_=wt2t[:, :].rearrange("p (a m) -> p a m", a=8)),
            deps=[cv], inc="w2s")
        R["wt2t"].read(st)
        hbm[f"wo2_{dc}"] = st

    def tern_units(which):
        if which == "up":
            for fc in range(32):
                yield (lambda fc=fc: wup[fc, :, :, :].transpose([1, 0, 2]),
                       lambda fc=fc: wtu2[fc, :, :, :].transpose([1, 0, 2]),
                       f"wtu2_{fc}")
        else:
            for dc in range(8):
                for u in range(4):
                    yield (lambda dc=dc, u=u: wdn[dc, u * 8:(u + 1) * 8, :, :].transpose([1, 0, 2]),
                           lambda dc=dc, u=u: wtd2[dc, u * 8:(u + 1) * 8, :, :].transpose([1, 0, 2]),
                           f"wtd2_{dc}" if u == 3 else None)

    ai_ctr = [0]
    acc2_res = Res()
    wm1f_res = R["wm1"]
    R["wm2"] = R["wm1"]
    m0_sb = sbt("m0sb", [128, 2], F32)
    m0_res = Res()
    sum_toks = {}
    tau_toks = {}
    tern_thunks = []

    tern_led = {}

    def _mk_ldA(inf, uid):
        def th():
            b = uid % 2
            ld = P.op("gpsimd", lambda e: e.dma_start(
                out=wts[:, b, :].rearrange("p (a m) -> p a m", a=8), in_=inf()),
                deps=wts_res[b].wd(), inc=f"tl{b}")
            wts_res[b].wrote(ld)
            tern_led[uid] = ld
        return th

    def _mk_unitA(inf, uid):
        def th():
            b = uid % 2
            ld = tern_led[uid]
            ab = P.op("scalar", lambda e, uid=uid: e.activation(
                out=wmix[:, :].bitcast(F32), in_=wts[:, b, :], func=AF.Abs,
                accum_out=acca[:, uid:uid + 1]),
                deps=[ld] + R["wm1"].wd() + acc2_res.wd(), inc="act")
            wts_res[b].read(ab)
            R["wm1"].wrote(ab)
            acc2_res.w = ab
        return th

    def _mk_fin(which):
        row = 0 if which == "up" else 32
        col = 0 if which == "up" else 1
        c0 = 0 if which == "up" else 32
        def th():
            deps = [acc2_res.w, c4] + psW_res[0].wd()
            mm = P.op("tensor", lambda e, row=row, c0=c0: e.matmul(
                psW[0][row:row + 1, 0:32], ones1c[:, :], acca[:, c0:c0 + 32],
                start=True, stop=True, skip_group_check=True),
                deps=deps, inc="pe")
            acc2_res.read(mm)
            psW_res[0].w = mm
            s_rough = P.op("vector", lambda e, row=row, col=col: e.reduce_sum(
                vec[0:1, 8 + col:9 + col], psW[0][row:row + 1, 0:32], axis=AX.X),
                deps=[mm] + R["vs0"].wd(), inc="dve")
            R["vs0"].w = s_rough
            psW_res[0].read(s_rough)
            m0 = P.op("vector", lambda e, col=col: e.tensor_scalar(
                vec[0:1, 10 + col:11 + col], vec[0:1, 8 + col:9 + col],
                1.0 / 4096.0, None, ALU.mult), deps=[s_rough], inc="dve")
            sm0 = P.op("sync", lambda e, col=col: e.dma_start(
                out=AP(scal_h.tensor, 6 + col, [[1, 1], [1, 1]]),
                in_=vec[0:1, 10 + col:11 + col]), deps=[m0], inc="scal")
            m0b = P.op("sync", lambda e, col=col: e.dma_start(
                out=m0_sb[:, col:col + 1],
                in_=AP(scal_h.tensor, 6 + col, [[0, 128], [1, 1]])),
                deps=[sm0] + m0_res.wd(), inc="scal")
            m0_res.wrote(m0b)
            rs = P.op("vector", lambda e, col=col, c0=c0: e.tensor_scalar(
                accr[:, :], acca[:, c0:c0 + 32], m0_sb[:, col:col + 1], None,
                ALU.subtract), deps=[acc2_res.w, m0b] + R["accr"].wd(),
                inc="dve")
            R["accr"].wrote(rs)
            acc2_res.read(rs)
            m0_res.read(rs)
            rrow = 64 if which == "up" else 0
            mmr = P.op("tensor", lambda e, rrow=rrow: e.matmul(
                psW[0][rrow:rrow + 1, 0:32], ones1c[:, :], accr[:, :],
                start=True, stop=True, skip_group_check=True),
                deps=[rs, c4, psW_res[0].w], inc="pe")
            R["accr"].read(mmr)
            psW_res[0].w = mmr
            rsum = P.op("vector", lambda e, col=col, rrow=rrow: e.reduce_sum(
                vec[0:1, 12 + col:13 + col], psW[0][rrow:rrow + 1, 0:32],
                axis=AX.X), deps=[mmr], inc="dve")
            psW_res[0].read(rsum)
            fin = P.op("vector", lambda e, col=col: e.scalar_tensor_tensor(
                vec[0:1, 8 + col:9 + col], vec[0:1, 10 + col:11 + col], 4096.0,
                vec[0:1, 12 + col:13 + col], ALU.mult, ALU.add),
                deps=[rsum, m0, R["vs0"].w], inc="dve")
            R["vs0"].w = fin
            sum_toks[which] = fin
        return th

    def _tau_thunk():
        t1 = P.op("vector", lambda e: e.tensor_scalar(
            vec[0:1, 10:11], vec[0:1, 8:9], -0.5 / (4 * 1024 * 1024), None,
            ALU.mult), deps=[sum_toks["up"], sum_toks["dn"]], inc="dve")
        t2 = P.op("vector", lambda e: e.tensor_scalar(
            vec[0:1, 11:12], vec[0:1, 8:9], 0.5 / (4 * 1024 * 1024), None,
            ALU.mult), deps=[sum_toks["up"]], inc="dve")
        t3 = P.op("vector", lambda e: e.tensor_scalar(
            vec[0:1, 12:13], vec[0:1, 9:10], -0.5 / (4 * 1024 * 1024), None,
            ALU.mult), deps=[sum_toks["dn"]], inc="dve")
        t4 = P.op("vector", lambda e: e.tensor_scalar(
            vec[0:1, 13:14], vec[0:1, 9:10], 0.5 / (4 * 1024 * 1024), None,
            ALU.mult), deps=[sum_toks["dn"]], inc="dve")
        t5 = P.op("sync", lambda e: e.dma_start(
            out=AP(scal_h.tensor, 0, [[1, 1], [1, 4]]), in_=vec[0:1, 10:14]),
            deps=[t1, t2, t3, t4], inc="scal")
        tu_t = P.op("sync", lambda e: e.dma_start(
            out=tauu_sb[:, :], in_=AP(scal_h.tensor, 0, [[0, 128], [1, 2]])),
            deps=[t5], inc="scal")
        td_t = P.op("sync", lambda e: e.dma_start(
            out=taud_sb[:, :], in_=AP(scal_h.tensor, 2, [[0, 128], [1, 2]])),
            deps=[t5] + R["taud"].wd(), inc="scal")
        R["tauu"].wrote(tu_t)
        R["taud"].wrote(td_t)
        tau_toks["up"] = tu_t
        tau_toks["dn"] = td_t

    def _mk_unitB(which, inf, outf, nname, uid):
        tau = tauu_sb if which == "up" else taud_sb
        def th():
            b = uid % 2
            tau_t = tau_toks[which]
            ld = tern_led[uid]
            p1 = P.op("vector", lambda e: e.tensor_scalar(
                wmix[:, 0:1024], wts[:, b, :], tau[:, 1:2], 2.0, ALU.is_ge,
                ALU.mult), deps=[ld, tau_t] + R["wm1"].wd(), inc="dve")
            R["wm1"].wrote(p1)
            p2 = P.op("vector", lambda e: e.tensor_scalar(
                wmix[:, 1024:2048], wts[:, b, :], tau[:, 0:1], -2.0, ALU.is_le,
                ALU.mult), deps=[ld] + R["wm2"].wd(), inc="dve")
            R["wm2"].wrote(p2)
            wts_res[b].read(p2)
            p3 = P.op("gpsimd", lambda e: e.tensor_tensor(
                wt2t[:, :], wmix[:, 0:1024], wmix[:, 1024:2048], ALU.add),
                deps=[p1, p2] + R["wt2t"].wd(), inc="pl")
            R["wm1"].read(p3)
            R["wm2"].read(p3)
            R["wt2t"].wrote(p3)
            st = P.op("gpsimd", lambda e: e.dma_start(
                out=outf(), in_=wt2t[:, :].rearrange("p (a m) -> p a m", a=8)),
                deps=[p3], inc="t2s")
            R["wt2t"].read(st)
            if nname:
                hbm[nname] = st
        return th

    lds = []
    comps = []
    uid = 0
    for which in ("up", "dn"):
        first = True
        for (inf, _outf, _n) in tern_units(which):
            lds.append(_mk_ldA(inf, uid))
            th = _mk_unitA(inf, uid)
            th._first = first
            first = False
            comps.append(th)
            uid += 1
    for which in ("up", "dn"):
        for (inf, outf, nname) in tern_units(which):
            lds.append(_mk_ldA(inf, uid))
            comps.append(_mk_unitB(which, inf, outf, nname, uid))
            uid += 1
    tern_thunks.append(lds[0])
    tern_thunks.append(lds[1])
    for u in range(128):
        tern_thunks.append(comps[u])
        if u + 2 < 128:
            tern_thunks.append(lds[u + 2])
        if u == 31:
            tern_thunks.append(_mk_fin("up"))
        if u == 63:
            tern_thunks.append(_mk_fin("dn"))
            tern_thunks.append(_tau_thunk)

    def load_piece(qt, pr, ph):
        slot = 1 - ph
        base = 127 + qt * 512 + (2048 if ph == 1 else 0)
        hA, hB = PAIRS[pr]
        rn = f"ew{slot}"
        dep = R[rn].wd() + [hbm[f"erep{hA}"], hbm[f"erep{hB}"]]
        tA = P.op("sync", lambda e, hA=hA, slot=slot, base=base: e.dma_start(
            out=ewin[:, slot, 0, :],
            in_=AP(erep.tensor, hA * 128 * L + base, [[L - 1, 128], [1, WPC]])),
            deps=dep, inc=f"ew{slot}")
        tB = P.op("sync", lambda e, hB=hB, slot=slot, base=base: e.dma_start(
            out=ewin[:, slot, 1, :],
            in_=AP(erep.tensor, hB * 128 * L + base, [[L - 1, 128], [1, WPC]])),
            deps=dep, inc=f"ew{slot}")
        R[rn].wrote(tB)
        R[rn].read(tA)
        return (tA, tB)

    def load_qtile(qt, pr):
        g = qt * 8 + pr
        b = g % 2
        rn = f"q{b}"
        tok = P.op("scalar", lambda e, qt=qt, pr=pr, b=b: e.dma_start(
            out=q_sb[:, b, :], in_=q_hbm[qt, pr, :, :]),
            deps=R[rn].wd() + [hbm[f"q{qt}_{pr}"]], inc=f"ql{b}")
        R[rn].wrote(tok)
        return tok

    win_tok = {}
    q_tok = {}
    win_tok[(0, 0, 1)] = load_piece(0, 0, 1)
    q_tok[0] = load_qtile(0, 0)
    q_tok[1] = load_qtile(0, 1)

    onorm = {q: [] for q in range(NQT)}
    out_toks = []

    from collections import deque
    work_q = deque()
    epi_q = []
    wu_led = {}

    def pump(n=1):
        for _ in range(n):
            if work_q:
                work_q.popleft()()

    def attention(qt, pr):
        g = qt * 8 + pr
        qb = g % 2
        t4 = pr // 2
        hA, hB = PAIRS[pr]
        kvA, kvB = hA // 2, hB // 2
        def emit_scores(kc):
            par = kc % 2
            sn = f"psS{par}"
            if kc == 0:
                win_tok[(qt, pr, 0)] = load_piece(qt, pr, 0)
            if kc == 16 and g + 1 < NQT * 8:
                win_tok[((g + 1) // 8, (g + 1) % 8, 1)] = \
                    load_piece((g + 1) // 8, (g + 1) % 8, 1)
            deps = R[sn].wd() + [k2_t, q_tok[g]]
            def mk_sc(e, t4=t4, kc=kc, qb=qb, par=par):
                e.matmul(psS[par][:, 0:512],
                         k2_sb[0:64, t4 * S + kc * 128:t4 * S + kc * 128 + 128],
                         q_sb[0:64, qb, :], start=True, stop=True,
                         tile_position=(0, 0), skip_group_check=True)
                return e.matmul(psS[par][:, 512:1024],
                                k2_sb[64:128, t4 * S + kc * 128:t4 * S + kc * 128 + 128],
                                q_sb[64:128, qb, :], start=True, stop=True,
                                tile_position=(64, 0), skip_group_check=True)
            sc = P.op("tensor", mk_sc, deps=deps, inc="pe")
            R[sn].wrote(sc)
            R[f"q{qb}"].read(sc)

        def emit_expmult(kc):
            par = kc % 2
            sn = f"psS{par}"
            en = f"eo{par}"
            ex = P.op("scalar", lambda e, par=par: e.activation(
                out=eo_sb[:, par, :], in_=psS[par][:, :], func=AF.Exp,
                bias=bias_sb[:, 0:1], scale=1.0),
                deps=R[sn].rd() + R[en].wd() + R["bias"].rd(), inc="act")
            R[sn].read(ex)
            R[en].wrote(ex)
            pn = f"pt{kc % 3}"
            ph = 1 if kc < 16 else 0
            slot = 1 - ph
            w0 = (1920 - 128 * kc) if ph == 1 else (3968 - 128 * kc)
            wt_ = win_tok[(qt, pr, ph)]
            deps_em = R[en].rd() + R[pn].wd() + [wt_[0], wt_[1]]
            p3 = kc % 3
            em0 = P.op("vector", lambda e, par=par, p3=p3, slot=slot, w0=w0:
                e.tensor_tensor(
                    pt_sb[:, p3, 0:512], eo_sb[:, par, 0:512],
                    ewin[:, slot, 0, w0:w0 + 512], ALU.mult),
                deps=deps_em, inc="dve")
            em = P.op("vector", lambda e, par=par, p3=p3, slot=slot, w0=w0:
                e.tensor_tensor(
                    pt_sb[:, p3, 512:1024], eo_sb[:, par, 512:1024],
                    ewin[:, slot, 1, w0:w0 + 512], ALU.mult),
                deps=deps_em, inc="dve")
            R[en].read(em0)
            R[en].read(em)
            R[pn].wrote(em)
            R[pn].read(em0)
            R[f"ew{slot}"].read(em)
            R[f"ew{slot}"].read(em0)

        def emit_av(kc):
            P.op("tensor", lambda e: e.ldweights(ones64[:, :]), deps=[c1])
            par = kc % 3
            pn = f"pt{par}"
            deps = R[pn].rd() + [v_t, c1]
            if kc == 0:
                deps += R["psO"].wd() + R["psR"].wd()
            def mk_av(e, kc=kc, par=par, kvA=kvA, kvB=kvB):
                st_, sp_ = (kc == 0), (kc == NKC - 1)
                e.matmul(psO[0:64, :],
                         v_sb[:, kc * 512 + 64 * kvA:kc * 512 + 64 * kvA + 64],
                         pt_sb[:, par, 0:512], start=st_, stop=sp_,
                         tile_position=(0, 0), skip_group_check=True)
                e.matmul(psO[64:128, :],
                         v_sb[:, kc * 512 + 64 * kvB:kc * 512 + 64 * kvB + 64],
                         pt_sb[:, par, 512:1024], start=st_, stop=sp_,
                         tile_position=(0, 64), skip_group_check=True)
                e.matmul(psR[0:64, :], ones64[:, :], pt_sb[:, par, 0:512],
                         start=st_, stop=sp_, tile_position=(0, 0),
                         skip_group_check=True)
                return e.matmul(psR[64:128, :], ones64[:, :], pt_sb[:, par, 512:1024],
                                start=st_, stop=sp_, tile_position=(0, 64),
                                skip_group_check=True)
            av = P.op("tensor", mk_av, deps=deps, inc="pe")
            R[pn].read(av)
            if kc == 0:
                R["psO"].wrote(av)
                R["psR"].wrote(av)
            else:
                R["psO"].w = av
                R["psR"].w = av

        for kc in range(NKC + 3):
            if kc < NKC:
                emit_scores(kc)
            if kc == 2 and epi_q:
                epi_q.pop(0)()
            if kc >= 3:
                emit_av(kc - 3)
            if kc < NKC:
                emit_expmult(kc)
            pump(1 + (len(work_q) > 160) + (len(work_q) > 320))
        if g + 2 < NQT * 8:
            q_tok[g + 2] = load_qtile((g + 2) // 8, (g + 2) % 8)

        def epilogue(qt=qt, pr=pr):
            rc0 = P.op("scalar", lambda e: e.activation(
                out=rr_sb[:, :], in_=psR[:, :], func=AF.Ln),
                deps=R["psR"].rd() + R["rr"].wd(), inc="act")
            R["psR"].read(rc0)
            R["rr"].wrote(rc0)
            rc = P.op("scalar", lambda e: e.activation(
                out=rr_sb[:, :], in_=rr_sb[:, :], func=AF.Exp, scale=-1.0),
                deps=[rc0], inc="act")
            R["rr"].wrote(rc)
            on = P.op("vector", lambda e, pr=pr: e.tensor_tensor(
                o_sb[:, pr, :], psO[:, :], rr_sb[:, :], ALU.mult),
                deps=R["psO"].rd() + [rc] + R["osb"].wd(), inc="dve")
            R["psO"].read(on)
            R["rr"].read(on)
            onorm[qt].append(on)
            R["osb"].w = on
            if pr == 7:
                R["osb"].wrote(on)
        epi_q.append(epilogue)

    def ln_scalars(sum_ap, sum2_ap, dep, n):
        m = P.op("vector", lambda e: e.tensor_scalar(
            vrow(0), sum_ap, 1.0 / n, None, ALU.mult),
            deps=[dep] + R["vs0"].wd(), inc="dve")
        R["vs0"].wrote(m)
        m2 = P.op("gpsimd", lambda e: e.tensor_tensor(
            vrow(3), vrow(0), vrow(0), ALU.mult),
            deps=[m] + R["vs3"].wd(), inc="pl")
        R["vs0"].read(m2)
        R["vs3"].wrote(m2)
        v = P.op("vector", lambda e: e.scalar_tensor_tensor(
            vrow(1), sum2_ap, 1.0 / n, vrow(3), ALU.mult, ALU.subtract),
            deps=[dep, m2] + R["vs1"].wd(), inc="dve")
        R["vs3"].read(v)
        R["vs1"].wrote(v)
        u = P.op("vector", lambda e: e.tensor_scalar(
            vrow(3), vrow(1), EPS_LN, None, ALU.add),
            deps=[v] + R["vs3"].wd(), inc="dve")
        R["vs1"].read(u)
        R["vs3"].wrote(u)
        lnu = P.op("scalar", lambda e: e.activation(
            out=vrow(3), in_=vrow(3), func=AF.Ln), deps=[u], inc="act")
        R["vs3"].wrote(lnu)
        a = P.op("scalar", lambda e: e.activation(
            out=vrow(2), in_=vrow(3), func=AF.Exp, scale=-0.5),
            deps=[lnu] + R["vs2"].wd(), inc="act")
        R["vs3"].read(a)
        R["vs2"].wrote(a)
        b = P.op("vector", lambda e: e.scalar_tensor_tensor(
            vrow(0), vrow(0), -1.0, vrow(2), ALU.mult, ALU.mult),
            deps=[a, m] + R["vs0"].wd(), inc="dve")
        R["vs2"].read(b)
        R["vs0"].wrote(b)
        return {"a": a, "b": b, "var": v}

    def rms_scale(var_dep, a_tok, out_slot_guarded):
        g1 = P.op("vector", lambda e: e.tensor_scalar_max(
            vrow(3), vrow(1), 1e-30), deps=[var_dep] + R["vs3"].wd(), inc="dve")
        R["vs1"].read(g1)
        R["vs3"].wrote(g1)
        l1 = P.op("scalar", lambda e: e.activation(
            out=vrow(3), in_=vrow(3), func=AF.Ln), deps=[g1], inc="act")
        R["vs3"].wrote(l1)
        e1 = P.op("scalar", lambda e: e.activation(
            out=vrow(3), in_=vrow(3), func=AF.Exp, scale=0.5), deps=[l1], inc="act")
        R["vs3"].wrote(e1)
        m1 = P.op("gpsimd", lambda e: e.tensor_tensor(
            vrow(3), vrow(3), vrow(2), ALU.mult), deps=[e1, a_tok], inc="pl")
        R["vs2"].read(m1)
        R["vs3"].wrote(m1)
        a1 = P.op("vector", lambda e: e.tensor_scalar(
            vrow(3), vrow(3), EPS_RMS, 2.0, ALU.add, ALU.mult),
            deps=[m1], inc="dve")
        R["vs3"].wrote(a1)
        rc0 = P.op("scalar", lambda e: e.activation(
            out=vrow(3), in_=vrow(3), func=AF.Ln), deps=[a1], inc="act")
        R["vs3"].wrote(rc0)
        sc = P.op("scalar", lambda e: e.activation(
            out=vrow(3), in_=vrow(3), func=AF.Exp, scale=-1.0),
            deps=[rc0], inc="act")
        R["vs3"].wrote(sc)
        return sc

    r2deps = [None] * 8

    def down_chunk(qt, dcs):
        NBD = 4 if qt == 3 else 2
        dbank = [psW[0][:, :], psW[1][:, :],
                 psS[1][:, 0:512], psS[1][:, 512:1024]]
        dbres = [psW_res[0], psW_res[1], R["td2"], R["td3"]]
        NWD = 6 if qt == 3 else 2
        wtsb = wts[:, 0, :].bitcast(BF16)
        dslot = [wtd_sb[:, 0, :, :].rearrange("p a m -> p (a m)"),
                 wtd_sb[:, 1, :, :].rearrange("p a m -> p (a m)"),
                 wtsb[:, 0:512], wtsb[:, 512:1024],
                 wtsb[:, 1024:1536], wtsb[:, 1536:2048]]
        dsres = [R["wtd0"], R["wtd1"], R["ts2"], R["ts3"], R["ts4"], R["ts5"]]
        for dc in dcs:
            pw = dc % NBD
            pwn = f"psW{pw}"
            state = {"first": True, "mm": None}
            def th_load(dc=dc, blk=0, state=state):
                g = dc * 8 + blk
                wb = g % NWD
                seed = (R["wts0"].wd() if (wb >= 2 and g == wb) else [])
                wt_ = P.op("scalar", lambda e, dc=dc, blk=blk, wb=wb: e.dma_start(
                    out=dslot[wb].rearrange("p (a m) -> p a m", a=4),
                    in_=wtd2[dc, blk * 4:blk * 4 + 4, :, :].transpose([1, 0, 2])),
                    deps=[hbm[f"wtd2_{dc}"]] + dsres[wb].wd() + seed,
                    inc=f"wd{wb % 2}")
                dsres[wb].wrote(wt_)
                state[f"ld{blk}"] = wt_
            def th_mm(dc=dc, blk=0, state=state, pw=pw, pwn=pwn):
                g = dc * 8 + blk
                wb = g % NWD
                first = state["first"]
                deps = [state[f"ld{blk}"]] + \
                    [R[f"h{blk * 4 + i}"].rd()[0] for i in range(4)
                     if R[f"h{blk * 4 + i}"].rd()]
                if first:
                    deps += dbres[pw].wd()
                    if pw >= 2 and dc == pw:
                        deps += R["psS1"].wd()
                def mk_dn(e, wb=wb, blk=blk, pw=pw, first=first):
                    last = None
                    for i in range(4):
                        last = e.matmul(dbank[pw],
                                        dslot[wb][:, i * 128:(i + 1) * 128],
                                        h_sb[:, blk * 4 + i, :],
                                        start=(first and i == 0),
                                        stop=(blk == 7 and i == 3),
                                        skip_group_check=True)
                    return last
                mm_tok = P.op("tensor", mk_dn, deps=deps, inc="pe")
                dsres[wb].read(mm_tok)
                if first:
                    dbres[pw].wrote(mm_tok)
                else:
                    dbres[pw].w = mm_tok
                state["first"] = False
                state["mm"] = mm_tok
            def th_epi(dc=dc, state=state, pw=pw, pwn=pwn):
                mm_tok = state["mm"]
                for i in range(32):
                    R[f"h{i}"].read(mm_tok)
                deps = [mm_tok] + R["rhb"].rd() + R["r2"].wd()
                tt1 = P.op("vector", lambda e, pw=pw: e.tensor_tensor(
                    r2t_sb[:, :], dbank[pw], rhb_sb[:, :], ALU.mult),
                    deps=deps, inc="dve")
                dbres[pw].read(tt1)
                R["rhb"].read(tt1)
                R["r2"].wrote(tt1)
                tt2 = P.op("vector", lambda e, dc=dc: e.tensor_tensor(
                    x1_sb[:, dc, :], r2t_sb[:, :], x1_sb[:, dc, :], ALU.add),
                    deps=[tt1] + R["x1"].wd(), inc="dve")
                R["r2"].read(tt2)
                R["x1"].w = tt2
                r2deps[dc] = tt2
            import functools
            lead = 4 if NWD > 2 else 2
            thunks = [functools.partial(th_load, dc, blk, state)
                      for blk in range(8)]
            seq = list(thunks[:lead])
            for blk in range(8):
                seq.append(functools.partial(th_mm, dc, blk, state, pw, pwn))
                if blk + lead < 8:
                    seq.append(thunks[blk + lead])
            seq.append(functools.partial(th_epi, dc, state, pw, pwn))
            work_q.extend(seq)

    def stage2(qt, p):
        ob = qt % 2
        if p == 0:
            wo_led = {}
            def th_wo_load(dc):
                wb = dc % 2
                wn = f"wo{wb}"
                wt_ = P.op("scalar", lambda e, dc=dc, wb=wb: e.dma_start(
                    out=wo_sb[:, wb, :, :].rearrange("p a m -> p (a m)"),
                    in_=wo2[dc, :, :, :].transpose([1, 0, 2])),
                    deps=[hbm[f"wo2_{dc}"]] + R[wn].wd(), inc=f"wo{wb}")
                R[wn].wrote(wt_)
                wo_led[dc] = wt_
            def th_wo_mm(dc, half):
                wb = dc % 2
                wn = f"wo{wb}"
                pw = dc % 2
                xbuf = xo_sb[:, :] if dc % 2 == 0 else ot_sb[:, 0, :]
                xrn = "xo0" if dc % 2 == 0 else "ot0"
                if half == 0:
                    xt_ = P.op("sync", lambda e, dc=dc, qt=qt, xbuf=xbuf: e.dma_start(
                        out=xbuf,
                        in_=xqT[dc * 128:dc * 128 + 128,
                                qt * 512:qt * 512 + 512]),
                        deps=R[xrn].wd(), inc=("xo" if dc % 2 == 0 else "xo1"))
                    R[xrn].wrote(xt_)
                    wo_led[(dc, "xo")] = xt_
                xt_ = wo_led[(dc, "xo")]
                deps = [wo_led[dc], R["osb"].w]
                if half == 0:
                    deps += psW_res[pw].wd()
                def mk_wo(e, wb=wb, pw=pw, half=half):
                    last = None
                    for j in range(half * 4, half * 4 + 4):
                        last = e.matmul(psW[pw][:, :], wo_sb[:, wb, j, :],
                                        o_sb[:, j, :],
                                        start=(half == 0 and j == 0),
                                        stop=(half == 1 and j == 7),
                                        skip_group_check=True)
                    return last
                mm = P.op("tensor", mk_wo, deps=deps, inc="pe")
                if half == 0:
                    psW_res[pw].wrote(mm)
                    R[wn].read(mm)
                    return
                psW_res[pw].w = mm
                R[wn].read(mm)
                if dc == 7:
                    R["osb"].read(mm)
                    R["osb"].wrote(mm)
                r1 = P.op("vector", lambda e, dc=dc, pw=pw, xbuf=xbuf:
                    e.scalar_tensor_tensor(
                        x1_sb[:, dc, :], psW[pw][:, :], bo_sb[:, dc:dc + 1], xbuf,
                        ALU.add, ALU.add),
                    deps=[mm, xt_, bo_t] + (R["x1"].wd() if dc == 0 else []),
                    inc="dve")
                psW_res[pw].read(r1)
                R[xrn].read(r1)
                R["x1"].w = r1
            def th_stat(dc):
                x1w = R["x1"].w
                sq = P.op("vector", lambda e, dc=dc: e.tensor_tensor(
                    h2_sb[:, :], x1_sb[:, dc, :], x1_sb[:, dc, :], ALU.mult),
                    deps=[x1w] + R["h2"].wd(), inc="dve")
                R["h2"].wrote(sq)
                deps = [x1w, sq, c3, c4]
                if dc == 0:
                    deps += psW_res[0].wd() + psW_res[1].wd()
                def mk_st(e, dc=dc):
                    e.matmul(psW[0][0:1, :], ones1c[:, :], x1_sb[:, dc, :],
                             start=(dc == 0), stop=(dc == 7), skip_group_check=True)
                    return e.matmul(psW[1][0:1, :], onesb[:, :], h2_sb[:, :],
                                    start=(dc == 0), stop=(dc == 7),
                                    skip_group_check=True)
                mm = P.op("tensor", mk_st, deps=deps, inc="pe")
                R["h2"].read(mm)
                if dc == 0:
                    psW_res[0].wrote(mm)
                    psW_res[1].wrote(mm)
                else:
                    psW_res[0].w = mm
                    psW_res[1].w = mm
                R["x1"].read(mm)
            ln_state = {}
            def th_ln_a():
                toks = ln_scalars(psW[0][0:1, :], psW[1][0:1, :], psW_res[0].w, DM)
                psW_res[0].read(toks["var"])
                psW_res[1].read(toks["var"])
                ln_state.update(toks)
            def th_ln_b():
                s_tok = rms_scale(ln_state["var"], ln_state["a"], None)
                ln_state["s"] = s_tok
            def th_bcast():
                bca = P.op("tensor", lambda e: e.matmul(
                    psW[0][:, :], ones1f[:, :], vrow(2), start=True, stop=True,
                    skip_group_check=True),
                    deps=[ln_state["a"], c2] + psW_res[0].wd(), inc="pe")
                psW_res[0].wrote(bca)
                R["vs2"].read(bca)
                bcb = P.op("tensor", lambda e: e.matmul(
                    psW[1][:, :], ones1f[:, :], vrow(0), start=True, stop=True,
                    skip_group_check=True),
                    deps=[ln_state["b"]] + psW_res[1].wd(), inc="pe")
                psW_res[1].wrote(bcb)
                R["vs0"].read(bcb)
                ln_state["bca"] = bca
                ln_state["bcb"] = bcb
            def th_apply(dc):
                bca, bcb = ln_state["bca"], ln_state["bcb"]
                a1 = P.op("vector", lambda e, dc=dc: e.tensor_tensor(
                    r2t_sb[:, :], x1_sb[:, dc, :], psW[0][:, :], ALU.mult),
                    deps=[bca] + R["x1"].wd() + R["r2"].wd(), inc="dve")
                psW_res[0].read(a1)
                R["r2"].wrote(a1)
                a2 = P.op("vector", lambda e, dc=dc: e.tensor_tensor(
                    x1_sb[:, dc, :], r2t_sb[:, :], psW[1][:, :], ALU.add),
                    deps=[bcb, a1], inc="dve")
                psW_res[1].read(a2)
                R["r2"].read(a2)
                R["x1"].w = a2
            def th_bcast_s():
                bcs = P.op("tensor", lambda e: e.matmul(
                    psW[0][:, :], ones1f[:, :], vrow(3), start=True, stop=True,
                    skip_group_check=True),
                    deps=[ln_state["s"], R["x1"].w] + psW_res[0].wd(), inc="pe")
                psW_res[0].wrote(bcs)
                R["vs3"].read(bcs)
                ln_state["bcs"] = bcs
            def th_xn(dc):
                xn = P.op("vector", lambda e, dc=dc: e.tensor_tensor(
                    xn_sb[:, dc, :], x1_sb[:, dc, :], psW[0][:, :], ALU.mult),
                    deps=[ln_state["bcs"], R["x1"].w] +
                    (R["xn"].wd() if dc == 0 else []), inc="dve")
                psW_res[0].read(xn)
                R["x1"].read(xn)
                R["xn"].w = xn
            import functools as _ft
            seq = [_ft.partial(th_wo_load, 0), _ft.partial(th_wo_load, 1)]
            for dc in range(8):
                seq.append(_ft.partial(th_wo_mm, dc, 0))
                seq.append(_ft.partial(th_wo_mm, dc, 1))
                if dc + 2 < 8:
                    seq.append(_ft.partial(th_wo_load, dc + 2))
            for dc in range(8):
                seq.append(_ft.partial(th_stat, dc))
            seq += [th_ln_a, th_ln_b, th_bcast]
            for dc in range(8):
                seq.append(_ft.partial(th_apply, dc))
            seq.append(th_bcast_s)
            for dc in range(8):
                seq.append(_ft.partial(th_xn, dc))
            work_q.extend(seq)
        elif p in (1, 2, 3, 4):
            NW = 4 if qt == 3 else 2
            wslot = [wtu_sb[:, 0, :, :].rearrange("p a m -> p (a m)"),
                     wtu_sb[:, 1, :, :].rearrange("p a m -> p (a m)"),
                     wmix[:, 0:1024], wmix[:, 1024:2048]]
            wsres = [R["wtu0"], R["wtu1"], R["tw2"], R["tw3"]]
            def th_up_load(fc):
                wb = fc % NW
                seed = R["wm1"].wd() if (wb >= 2 and fc == wb) else []
                wt_ = P.op("scalar", lambda e, fc=fc, wb=wb: e.dma_start(
                    out=wslot[wb].rearrange("p (a m) -> p a m", a=8),
                    in_=wtu2[fc, :, :, :].transpose([1, 0, 2])),
                    deps=[hbm[f"wtu2_{fc}"]] + wsres[wb].wd() + seed,
                    inc=f"wu{wb}")
                wsres[wb].wrote(wt_)
                wu_led[fc] = wt_
            NB = 4 if qt == 3 else 2
            ubank = [psW[0][:, :], psW[1][:, :],
                     psS[0][:, 0:512], psS[0][:, 512:1024]]
            ubres = [psW_res[0], psW_res[1], R["tu2"], R["tu3"]]
            def th_up_mm(fc, half):
                wb = fc % NW
                pw = fc % NB
                deps = [wu_led[fc], R["xn"].w]
                if half == 0:
                    deps += ubres[pw].wd()
                    if pw >= 2 and fc == pw:
                        deps += R["psS0"].wd()
                def mk_up(e, wb=wb, pw=pw, half=half):
                    last = None
                    for j in range(half * 4, half * 4 + 4):
                        last = e.matmul(ubank[pw],
                                        wslot[wb][:, j * 128:(j + 1) * 128],
                                        xn_sb[:, j, :],
                                        start=(half == 0 and j == 0),
                                        stop=(half == 1 and j == 7),
                                        skip_group_check=True)
                    return last
                mm = P.op("tensor", mk_up, deps=deps, inc="pe")
                if half == 0:
                    ubres[pw].wrote(mm)
                else:
                    ubres[pw].w = mm
                wsres[wb].read(mm)
                R["xn"].read(mm)
            def th_gelu_pair(fc0):
                for fc in (fc0, fc0 + 1):
                    pw = fc % NB
                    g_ = P.op("scalar", lambda e, fc=fc, pw=pw: e.activation(
                        out=h_sb[:, fc, :], in_=ubank[pw], func=AF.Gelu),
                        deps=[ubres[pw].w] + R[f"h{fc}"].wd(), inc="act")
                    ubres[pw].read(g_)
                    R[f"h{fc}"].wrote(g_)
            import functools as _ft
            f0 = (p - 1) * 8
            seq = []
            if p == 1:
                seq += [_ft.partial(th_up_load, i) for i in range(NW)]
            for fc in range(f0, f0 + 8):
                seq.append(_ft.partial(th_up_mm, fc, 0))
                seq.append(_ft.partial(th_up_mm, fc, 1))
                if fc % 2 == 1:
                    seq.append(_ft.partial(th_gelu_pair, fc - 1))
                if fc + NW < 32:
                    seq.append(_ft.partial(th_up_load, fc + NW))
            work_q.extend(seq)
        elif p == 5:
            hsum_state = {"mm": None}
            def th_h2(fc):
                par = fc % 2
                hb = r2t_sb[:, :].bitcast(BF16)[:, 0:512] if par else h2_sb[:, :]
                hn = "r2" if par else "h2"
                sq = P.op("gpsimd", lambda e, fc=fc, hb=hb: e.tensor_tensor(
                    hb, h_sb[:, fc, :], h_sb[:, fc, :], ALU.mult),
                    deps=R[f"h{fc}"].rd() + R[hn].wd(), inc="pl")
                R[hn].wrote(sq)
                deps = [sq, c3]
                if fc == 0:
                    deps += psW_res[0].wd()
                mm = P.op("tensor", lambda e, fc=fc, hb=hb: e.matmul(
                    psW[0][0:1, :], onesb[:, :], hb, start=(fc == 0),
                    stop=(fc == 31), skip_group_check=True), deps=deps, inc="pe")
                R[hn].read(mm)
                if fc == 0:
                    psW_res[0].wrote(mm)
                else:
                    psW_res[0].w = mm
                hsum_state["mm"] = mm
            def th_rmsh():
                hmm = hsum_state["mm"]
                u0 = P.op("vector", lambda e: e.tensor_scalar(
                    vrow(3), psW[0][0:1, :], 1.0 / FF, None, ALU.mult),
                    deps=[hmm] + R["vs3"].wd(), inc="dve")
                psW_res[0].read(u0)
                R["vs3"].wrote(u0)
                u = P.op("vector", lambda e: e.tensor_scalar_max(
                    vrow(3), vrow(3), 1e-30), deps=[u0], inc="dve")
                R["vs3"].wrote(u)
                l1 = P.op("scalar", lambda e: e.activation(
                    out=vrow(3), in_=vrow(3), func=AF.Ln), deps=[u], inc="act")
                R["vs3"].wrote(l1)
                e1 = P.op("scalar", lambda e: e.activation(
                    out=vrow(3), in_=vrow(3), func=AF.Exp, scale=0.5),
                    deps=[l1], inc="act")
                R["vs3"].wrote(e1)
                a1 = P.op("vector", lambda e: e.tensor_scalar(
                    vrow(3), vrow(3), EPS_RMS, 2.0, ALU.add, ALU.mult),
                    deps=[e1], inc="dve")
                R["vs3"].wrote(a1)
                rc0 = P.op("scalar", lambda e: e.activation(
                    out=vrow(3), in_=vrow(3), func=AF.Ln), deps=[a1], inc="act")
                R["vs3"].wrote(rc0)
                sc = P.op("scalar", lambda e: e.activation(
                    out=vrow(3), in_=vrow(3), func=AF.Exp, scale=-1.0),
                    deps=[rc0], inc="act")
                R["vs3"].wrote(sc)
                bc = P.op("tensor", lambda e: e.matmul(
                    psW[1][:, :], ones1f[:, :], vrow(3), start=True, stop=True,
                    skip_group_check=True), deps=[sc, c2] + psW_res[1].wd(),
                    inc="pe")
                psW_res[1].wrote(bc)
                R["vs3"].read(bc)
                cp = P.op("vector", lambda e: e.tensor_copy(rhb_sb[:, :],
                                                            psW[1][:, :]),
                          deps=[bc] + R["rhb"].wd(), inc="dve")
                psW_res[1].read(cp)
                R["rhb"].wrote(cp)
            import functools as _ft
            seq = [_ft.partial(th_h2, fc) for fc in range(32)]
            seq.append(th_rmsh)
            work_q.extend(seq)
            down_chunk(qt, [0, 1])
        elif p == 6:
            down_chunk(qt, [2, 3, 4])
        elif p == 7:
            ln2 = {}
            if qt == 3:
                sA, sB = psO[0:1, :], psR[0:1, :]
                resA, resB = R["psO"], R["psR"]
            else:
                sA, sB = psW[0][0:1, :], psW[1][0:1, :]
                resA, resB = psW_res[0], psW_res[1]
            def th_stat2(dc):
                sq = P.op("vector", lambda e, dc=dc: e.tensor_tensor(
                    h2_sb[:, :], x1_sb[:, dc, :], x1_sb[:, dc, :], ALU.mult),
                    deps=[r2deps[dc]] + R["h2"].wd(), inc="dve")
                R["h2"].wrote(sq)
                deps = [r2deps[dc], sq, c3, c4]
                if dc == 0:
                    deps += resA.wd() + resB.wd()
                def mk_st(e, dc=dc):
                    e.matmul(sA, ones1c[:, :], x1_sb[:, dc, :],
                             start=(dc == 0), stop=(dc == 7), skip_group_check=True)
                    return e.matmul(sB, onesb[:, :], h2_sb[:, :],
                                    start=(dc == 0), stop=(dc == 7),
                                    skip_group_check=True)
                mm = P.op("tensor", mk_st, deps=deps, inc="pe")
                R["h2"].read(mm)
                if dc == 0:
                    resA.wrote(mm)
                    resB.wrote(mm)
                else:
                    resA.w = mm
                    resB.w = mm
                R["x1"].read(mm)
            def th_ln2():
                toks = ln_scalars(sA, sB, resA.w, DM)
                resA.read(toks["var"])
                resB.read(toks["var"])
                bca = P.op("tensor", lambda e: e.matmul(
                    psW[0][:, :], ones1f[:, :], vrow(2), start=True, stop=True,
                    skip_group_check=True),
                    deps=[toks["a"], c2] + psW_res[0].wd(), inc="pe")
                psW_res[0].wrote(bca)
                R["vs2"].read(bca)
                bcb = P.op("tensor", lambda e: e.matmul(
                    psW[1][:, :], ones1f[:, :], vrow(0), start=True, stop=True,
                    skip_group_check=True),
                    deps=[toks["b"]] + psW_res[1].wd(), inc="pe")
                psW_res[1].wrote(bcb)
                R["vs0"].read(bcb)
                ln2["bca"] = bca
                ln2["bcb"] = bcb
            def th_out(dc):
                bca, bcb = ln2["bca"], ln2["bcb"]
                obuf = dc % 2 if qt == 3 else 0
                ob = ot_sb[:, 0, :] if obuf == 0 else rhb_sb[:, :]
                onn = f"ot{obuf}" if obuf == 0 else "rhb"
                a1 = P.op("vector", lambda e, dc=dc, ob=ob: e.tensor_tensor(
                    ob, x1_sb[:, dc, :], psW[0][:, :], ALU.mult),
                    deps=[bca, r2deps[dc]] + R[onn].wd(), inc="dve")
                psW_res[0].read(a1)
                R["x1"].read(a1)
                a2 = P.op("vector", lambda e, dc=dc, ob=ob: e.tensor_tensor(
                    ob, ob, psW[1][:, :], ALU.add),
                    deps=[bcb, a1], inc="dve")
                psW_res[1].read(a2)
                R[onn].wrote(a2)
                od = P.op("sync", lambda e, dc=dc, qt=qt, ob=ob: e.dma_start(
                    out=out[dc * 128:dc * 128 + 128, qt * 512:qt * 512 + 512],
                    in_=ob), deps=[a2], inc=f"ot{obuf}")
                R[onn].read(od)
                out_toks.append(od)
            import functools as _ft
            if qt == 3:
                down_chunk(qt, [5])
                work_q.extend([_ft.partial(th_stat2, 0),
                               _ft.partial(th_stat2, 1)])
                down_chunk(qt, [6])
                work_q.extend([_ft.partial(th_stat2, 2),
                               _ft.partial(th_stat2, 3)])
                down_chunk(qt, [7])
                seq = [_ft.partial(th_stat2, dc) for dc in range(4, 8)]
            else:
                down_chunk(qt, [5, 6, 7])
                seq = [_ft.partial(th_stat2, dc) for dc in range(8)]
            seq.append(th_ln2)
            seq += [_ft.partial(th_out, dc) for dc in range(8)]
            work_q.extend(seq)

    work_q.extend(tern_thunks)
    tern_thunks = []
    for qt in range(NQT + 1):
        for pr in range(8):
            if qt >= 1 and pr == 0:
                while epi_q:
                    epi_q.pop(0)()
                for p_ in range(8):
                    stage2(qt - 1, p_)
            if qt < NQT:
                attention(qt, pr)
        if qt == NQT:
            while work_q:
                work_q.popleft()()

    P.op("sync", lambda e: None, deps=[out_toks[-1]])

    sems = {}
    semctx = ExitStack()
    for s in ("pe", "act", "dve", "pl") + DMASEMS:
        sems[s] = semctx.enter_context(nc.semaphore(s))
    with nc.Block() as block:
        @block.sync
        def _(eng):
            P.replay("sync", eng, sems)

        @block.tensor
        def _(eng):
            P.replay("tensor", eng, sems)

        @block.scalar
        def _(eng):
            P.replay("scalar", eng, sems)

        @block.vector
        def _(eng):
            P.replay("vector", eng, sems)

        @block.gpsimd
        def _(eng):
            P.replay("gpsimd", eng, sems)
    semctx.close()
    ctx.close()
    return nc


_NC_CACHE = {}


def kernel(L0_attn_x, L0_attn_Wq, L0_attn_Wk, L0_attn_Wv, L0_attn_Wo,
           L0_attn_bo, L0_ffn_w_up, L0_ffn_w_down):
    x = np.ascontiguousarray(np.asarray(L0_attn_x, dtype=np.float32))
    Wq = np.asarray(L0_attn_Wq, dtype=np.float32)
    Wk = np.asarray(L0_attn_Wk, dtype=np.float32)
    Wv = np.asarray(L0_attn_Wv, dtype=np.float32)
    Wo = np.asarray(L0_attn_Wo, dtype=np.float32)
    bov = np.asarray(L0_attn_bo, dtype=np.float32)
    wu = np.asarray(L0_ffn_w_up, dtype=np.float32)
    wd = np.asarray(L0_ffn_w_down, dtype=np.float32)

    perm = PERM_ROWS
    wqT_t = np.ascontiguousarray(
        Wq[perm].T.reshape(8, 128, 8, 128).transpose(2, 0, 1, 3))
    woT_t = np.ascontiguousarray(
        Wo.T[perm].reshape(8, 128, 8, 128).transpose(2, 0, 1, 3))
    wkT = np.ascontiguousarray(Wk.T)
    wvT = np.ascontiguousarray(Wv.T)
    wupT_t = np.ascontiguousarray(
        wu.T.reshape(8, 128, 32, 128).transpose(2, 0, 1, 3))
    wdnT_t = np.ascontiguousarray(
        wd.T.reshape(32, 128, 8, 128).transpose(2, 0, 1, 3))
    bo2 = bov.reshape(1, DM)
    slopes = (-(2.0 ** (-(np.arange(16, dtype=np.float64) + 1))) / 8.0)\
        .astype(np.float32).reshape(16, 1)

    j = np.arange(L, dtype=np.float64)
    dist_h = [np.ascontiguousarray(
        np.broadcast_to(np.abs(j - 4095.0 + half * 2048.0), (16, L))
        .astype(np.float32)) for half in (0, 1)]

    if "nc" not in _NC_CACHE:
        _NC_CACHE["nc"] = build_nc()
    nc = _NC_CACHE["nc"]

    in_maps = []
    for c in range(8):
        b, half = c // 2, c % 2
        xT_c = np.ascontiguousarray(x[b].T)
        xqT_c = np.ascontiguousarray(x[b].T[:, half * TOK:(half + 1) * TOK])
        in_maps.append({
            "xT": xT_c, "xqT": xqT_c, "wqT": wqT_t, "wkT": wkT, "wvT": wvT,
            "woT": woT_t, "bo": bo2, "wup": wupT_t, "wdn": wdnT_t,
            "dist": dist_h[half], "slop": slopes,
        })
    res = run_bass_kernel_spmd(nc, in_maps, core_ids=list(range(8)))
    y = np.empty((4, S, DM), dtype=np.float32)
    for c in range(8):
        b, half = c // 2, c % 2
        y[b, half * TOK:(half + 1) * TOK, :] = res.results[c]["out"].T
    return y
